# version 02
# baseline (speedup 1.0000x reference)
import sys
for _p in ("/opt/trn_rl_repo",):
    if _p not in sys.path:
        sys.path.insert(0, _p)
import math
import os as _os
from contextlib import ExitStack

import numpy as np

import concourse.bass as bass
import concourse.mybir as mybir
from concourse.bass_utils import run_bass_kernel_spmd

F32 = mybir.dt.float32
BF16 = mybir.dt.bfloat16
I8 = mybir.dt.int8
I32 = mybir.dt.int32
AF = mybir.ActivationFunctionType
ALU = mybir.AluOpType
AX = mybir.AxisListType

TOK = 2048
D = 4096
T = 4096
NI = 32
SCALE = 1.0 / math.sqrt(32.0)
LN_EPS = 1e-5
DEN_EPS = 1e-6
HA = 8
KVH = 8

P2_KK = int(_os.environ.get("P2_KK", "8"))
P2_NC = int(_os.environ.get("P2_NC", "8"))
P2_C0 = int(_os.environ.get("P2_C0", "0"))
P2_CC = int(_os.environ.get("P2_CC", "1"))
P0_CC = int(_os.environ.get("P0_CC", "1"))
SIM_NOCC = int(_os.environ.get("SIM_NOCC", "0"))


class Counters:
    def __init__(self):
        self.c = {}

    def bump(self, name, n):
        self.c[name] = self.c.get(name, 0) + n
        return self.c[name]

    def val(self, name):
        return self.c.get(name, 0)

    def get(self, name):
        return self.c.get(name, 0)

    def set(self, name, v):
        self.c[name] = v

    def snap(self):
        return dict(self.c)

    def restore(self, s):
        self.c = dict(s)


class Alloc:
    def __init__(self, tensor, n):
        self.t = tensor
        self.n = n
        self.off = 0

    def tile(self, n):
        assert self.off + n <= self.n, f"arena overflow {self.off}+{n}>{self.n}"
        o = self.off
        self.off += n
        return self.t[:, o:o + n]


def build_nc(debug=False, no_branch=False, upto=99, only=None):
    nc = bass.Bass("TRN2", target_bir_lowering=False)

    def emit(i):
        return (upto >= i) if only is None else (i in only)

    x0 = nc.dram_tensor("x0", [TOK, D], F32, kind="ExternalInput")
    wqa = nc.dram_tensor("wqa", [HA * 128, D], F32, kind="ExternalInput")
    bqa = nc.dram_tensor("bqa", [128, HA], F32, kind="ExternalInput")
    m_a = nc.dram_tensor("m_a", [HA, 128, 128], F32, kind="ExternalInput")
    z_a = nc.dram_tensor("z_a", [128, HA], F32, kind="ExternalInput")
    wo = nc.dram_tensor("wo", [D, D], F32, kind="ExternalInput")
    bases = nc.dram_tensor("bases", [4, 512, D], I8, kind="ExternalInput")
    adiag = nc.dram_tensor("adiag", [4, 4, 128, 128], F32, kind="ExternalInput")
    wqb = nc.dram_tensor("wqb", [16 * 128, D], F32, kind="ExternalInput")
    bqb_r = nc.dram_tensor("bqb_r", [1, 16 * 128], F32, kind="ExternalInput")
    wk = nc.dram_tensor("wk", [KVH * 128, D], F32, kind="ExternalInput")
    bk_r = nc.dram_tensor("bk_r", [1, KVH * 128], F32, kind="ExternalInput")
    wv = nc.dram_tensor("wv", [KVH * 128, D], F32, kind="ExternalInput")
    bvr = nc.dram_tensor("bvr", [1, KVH * 128], F32, kind="ExternalInput")
    identi = nc.dram_tensor("identi", [128, 128], F32, kind="ExternalInput")
    nsrow = nc.dram_tensor("nsrow", [1, 128], F32, kind="ExternalInput")
    masks = nc.dram_tensor("masks", [4, 128, 512], F32, kind="ExternalInput")
    cid = nc.dram_tensor("cid", [1, 1], I32, kind="ExternalInput")

    out_t = nc.dram_tensor("out", [TOK, D], F32, kind="ExternalOutput")

    wqa_bf = nc.dram_tensor("wqa_bf", [HA * 128, D], BF16)
    wof = nc.dram_tensor("wof", [D, HA * 128], F32)
    wo_bf = nc.dram_tensor("wo_bf", [D, HA * 128], BF16)
    wqb_bf = nc.dram_tensor("wqb_bf", [16 * 128, D], BF16)
    wk_bf = nc.dram_tensor("wk_bf", [KVH * 128, D], BF16)
    wv_bf = nc.dram_tensor("wv_bf", [KVH * 128, D], BF16)
    wq_own = nc.dram_tensor("wq_own", [512, D], BF16)
    wq_all = nc.dram_tensor("wq_all", [D, D], BF16, addr_space="Shared")
    x1 = nc.dram_tensor("x1", [TOK, D], F32)
    x2 = nc.dram_tensor("x2", [TOK, D], F32)
    x3 = nc.dram_tensor("x3", [TOK, D], F32)
    ln1_tm = nc.dram_tensor("ln1_tm", [TOK, D], BF16)
    ln2_tm = nc.dram_tensor("ln2_tm", [TOK, D], BF16)
    ln4_tm = nc.dram_tensor("ln4_tm", [TOK, D], BF16)
    ln3_own = nc.dram_tensor("ln3_own", [TOK, D], BF16)
    ln3_full = nc.dram_tensor("ln3_full", [T, D], BF16)
    qT_d = nc.dram_tensor("qT_d", [16 * 128, T], BF16)
    kT_d = nc.dram_tensor("kT_d", [KVH * 128, T], BF16)
    v_d = nc.dram_tensor("v_d", [T, KVH * 128], BF16)
    attn_own = nc.dram_tensor("attn_own", [T, 2048], BF16)
    attn_full = nc.dram_tensor("attn_full", [2 * T, 2048], BF16)
    ffn_fm = nc.dram_tensor("ffn_fm", [D, TOK], BF16)

    dbg = {}
    if debug:
        for nm, shp, dt in [
            ("d_x1", [TOK, D], F32), ("d_x2", [TOK, D], F32),
            ("d_ln3", [T, D], BF16), ("d_qT", [16 * 128, T], BF16),
            ("d_kT", [KVH * 128, T], BF16), ("d_v", [T, KVH * 128], BF16),
            ("d_attn", [2 * T, 2048], BF16), ("d_x3", [TOK, D], F32),
            ("d_wq", [D, D], BF16),
        ]:
            dbg[nm] = nc.dram_tensor(nm, shp, dt, kind="ExternalOutput")

    C = Counters()

    with ExitStack() as ctx:
        en = ctx.enter_context
        AB_N = 67 * 1024
        AF_N = 14 * 1024
        ab_t = en(nc.sbuf_tensor("ab", [128, AB_N], BF16))
        af_t = en(nc.sbuf_tensor("af", [128, AF_N], F32))

        bq_sb = en(nc.sbuf_tensor("bq_sb", [1, 16 * 128], BF16))
        bk_sb = en(nc.sbuf_tensor("bk_sb", [1, KVH * 128], BF16))
        ones512 = en(nc.sbuf_tensor("ones512", [1, 512], BF16))
        bqa_sb = en(nc.sbuf_tensor("bqa_sb", [128, HA], F32))
        m_sb = en(nc.sbuf_tensor("m_sb", [128, HA * 128], BF16))
        z_sb = en(nc.sbuf_tensor("z_sb", [128, HA], BF16))
        ident_sb = en(nc.sbuf_tensor("ident_sb", [128, 128], BF16))
        ones_sb = en(nc.sbuf_tensor("ones_sb", [1, 128], BF16))
        nsrow_sb = en(nc.sbuf_tensor("nsrow_sb", [1, 128], BF16))
        bvr_sb = en(nc.sbuf_tensor("bvr_sb", [1, KVH * 128], BF16))
        mask_sb = en(nc.sbuf_tensor("mask_sb", [128, 4 * 512], BF16))
        cid_sb = en(nc.sbuf_tensor("cid_sb", [1, 1], I32))

        ps = [en(nc.psum_tensor(f"ps{i}", [128, 512], F32)) for i in range(8)]

        base_sems = ["pre", "preg", "wt", "wt2", "xlt", "cc",
                     "pe", "dv", "ac", "po", "ev", "q2", "k2",
                     "mx", "wq1", "wv2", "fin", "dbgs"]
        slot_sems = ["wta", "wtb", "fwa", "fwb", "w2a", "w2b",
                     "k2a", "k2b", "q2a", "q2b",
                     "wqs0", "wqs1",
                     "xl0", "xl1",
                     "lo0", "lo1",
                     "xr0", "xr1",
                     "x1t",
                     "ev0", "ev1", "ev2", "ev3",
                     "ra0", "ra1",
                     "rf0", "rf1",
                     "ro0", "ro1",
                     "q0", "q1", "q2s", "q3", "q4", "q5", "q6", "q7",
                     "at0", "at1", "at2", "at3",
                     "pa0", "pa1", "pa2", "pa3", "pa4", "pa5", "pa6", "pa7",
                     ]
        S = {n: en(nc.semaphore(f"s_{n}")) for n in base_sems + slot_sems}

        def W(E, e, eng, *pairs):
            if E == eng:
                for sm, vv in pairs:
                    if vv > 0:
                        e.wait_ge(S[sm], vv)

        def ln_pipeline(E, e, xf, sq, st_, ln_out, waits_v=(), waits_s=(), war_v=()):
            mean, var, rstd, nmr = st_["mean"], st_["var"], st_["rstd"], st_["nmr"]
            nmean = st_["nmean"]
            tac = C.bump("ac", 1)
            if E == "s":
                for sm, vv in waits_s:
                    if vv > 0:
                        e.wait_ge(S[sm], vv)
                e.activation(sq, xf, AF.Square, accum_out=var).then_inc(S["ac"], 1)
            t1 = C.bump("dv", 1)
            if E == "v":
                for sm, vv in list(waits_v) + list(war_v):
                    if vv > 0:
                        e.wait_ge(S[sm], vv)
                e.reduce_sum(mean, xf, axis=AX.X).then_inc(S["dv"], 1)
            t2 = C.bump("dv", 1)
            if E == "v":
                e.wait_ge(S["dv"], t1)
                e.tensor_scalar(mean, mean, 1.0 / D, 0.0, ALU.mult, ALU.add).then_inc(S["dv"], 1)
            t3 = C.bump("dv", 1)
            if E == "v":
                e.wait_ge(S["dv"], t2)
                e.tensor_scalar(nmean, mean, -1.0, 0.0, ALU.mult, ALU.add)
                e.tensor_mul(nmr, mean, mean).then_inc(S["dv"], 1)
            t4 = C.bump("dv", 1)
            if E == "v":
                e.wait_ge(S["ac"], tac)
                e.wait_ge(S["dv"], t3)
                e.tensor_scalar(var, var, 1.0 / D, LN_EPS, ALU.mult, ALU.add).then_inc(S["dv"], 1)
            t5 = C.bump("dv", 1)
            if E == "v":
                e.wait_ge(S["dv"], t4)
                e.tensor_sub(var, var, nmr).then_inc(S["dv"], 1)
            t6 = C.bump("dv", 1)
            if E == "v":
                e.wait_ge(S["dv"], t5)
                e.reciprocal(rstd, var).then_inc(S["dv"], 1)
            tac2 = C.bump("ac", 1)
            if E == "s":
                e.wait_ge(S["dv"], t6)
                e.activation(rstd, rstd, AF.Sqrt).then_inc(S["ac"], 1)
            tac3 = C.bump("ac", 1)
            if E == "s":
                e.wait_ge(S["ac"], tac2)
                e.activation(nmr, rstd, AF.Copy, scale=nmean[:, 0:1]).then_inc(S["ac"], 1)
            tdv = C.bump("dv", 1)
            if E == "v":
                e.wait_ge(S["ac"], tac3)
                e.tensor_scalar(ln_out, xf, rstd[:, 0:1], nmr[:, 0:1],
                                ALU.mult, ALU.add).then_inc(S["dv"], 1)
            return tdv

        def mk_stats(af, n):
            out = []
            for _ in range(n):
                sc = af.tile(8)
                out.append({"mean": sc[:, 0:1], "var": sc[:, 1:2],
                            "rstd": sc[:, 2:3], "nmr": sc[:, 3:4],
                            "nmean": sc[:, 4:5]})
            return out

        def run_phase(fn, name):
            snap = C.snap()
            ends = {}
            with nc.Block(name) as blk:
                def mk(E):
                    def f(e):
                        C.restore(snap)
                        fn(E, e)
                        ends[E] = C.snap()
                    return f
                blk.tensor(mk("t"))
                blk.vector(mk("v"))
                blk.scalar(mk("s"))
                blk.gpsimd(mk("g"))
                blk.sync(mk("y"))
            vals = list(ends.values())
            for v_ in vals[1:]:
                assert v_ == vals[0], f"counter divergence in phase {name}"
            C.restore(vals[0])

        def p0(E, e):
            ab = Alloc(ab_t, AB_N)
            n_small = 11
            if E == "g":
                e.dma_start(out=ident_sb[:, :], in_=identi[:, :]).then_inc(S["pre"], 16)
                e.dma_start(out=nsrow_sb[0:1, :], in_=nsrow[0:1, :]).then_inc(S["pre"], 16)
                e.dma_start(out=bvr_sb[0:1, :], in_=bvr[0:1, :]).then_inc(S["pre"], 16)
                e.dma_start(out=m_sb[:, :].rearrange("d (h e) -> d h e", e=128),
                            in_=m_a.ap().rearrange("h d e -> d h e")).then_inc(S["pre"], 16)
                e.dma_start(out=z_sb[:, :], in_=z_a[:, :]).then_inc(S["pre"], 16)
                e.dma_start(out=bq_sb[0:1, :], in_=bqb_r[0:1, :]).then_inc(S["pre"], 16)
                e.dma_start(out=bk_sb[0:1, :], in_=bk_r[0:1, :]).then_inc(S["pre"], 16)
                e.dma_start(out=bqa_sb[:, :], in_=bqa[:, :]).then_inc(S["pre"], 16)
                e.dma_start(out=mask_sb[:, :].rearrange("p (s q) -> p s q", q=512),
                            in_=masks.ap().rearrange("s p q -> p s q")).then_inc(S["pre"], 16)
                e.dma_start(out=cid_sb[:, :], in_=cid[:, :]).then_inc(S["pre"], 16)
                e.dma_start(out=wqa_bf[:, :], in_=wqa[:, :]).then_inc(S["pre"], 16)
            C.bump("pre", 16 * n_small)
            C.bump("po", 1)
            if E == "g":
                e.memset(ones_sb[0:1, :], 1.0)
                e.memset(ones512[0:1, :], 1.0).then_inc(S["po"], 1)

            bas, adg = [], []
            for k in range(4):
                rb, ra = [], []
                for ob in range(4):
                    tb = ab.tile(D)
                    rb.append(tb)
                    C.bump("preg", 16)
                    if E == "g":
                        e.dma_start(out=tb, in_=bases[k, ob * 128:(ob + 1) * 128, :]).then_inc(S["preg"], 16)
                    ta = ab.tile(128)
                    ra.append(ta)
                    C.bump("preg", 16)
                    if E == "g":
                        e.dma_start(out=ta, in_=adiag[k, ob, :, :]).then_inc(S["preg"], 16)
                bas.append(rb)
                adg.append(ra)
            t_bases = C.val("preg")

            for jj in range(4):
                for h in range(HA):
                    C.bump("pre", 16)
                    if E == "g":
                        src = wo[:, h * 512 + jj * 128: h * 512 + (jj + 1) * 128]
                        if jj == 0:
                            e.dma_start(out=wof[:, h * 128:(h + 1) * 128], in_=src).then_inc(S["pre"], 16)
                        else:
                            e.dma_start(out=wof[:, h * 128:(h + 1) * 128], in_=src,
                                        accum_op=ALU.add).then_inc(S["pre"], 16)
                if E == "g":
                    e.wait_ge(S["pre"], C.val("pre"))
            C.bump("pre", 16)
            if E == "g":
                e.dma_start(out=wo_bf[:, :], in_=wof[:, :]).then_inc(S["pre"], 16)

            wq_stage = [ab.tile(512) for _ in range(2)]
            for ob in range(4):
                for ic in range(8):
                    gi = ob * 8 + ic
                    bank = gi % 4
                    slot = gi % 2
                    ssem = "wqs%d" % slot
                    tpe = C.bump("pe", 1)
                    if E == "t":
                        if gi == 0:
                            e.wait_ge(S["preg"], t_bases)
                        e.wait_ge(S["ev"], C.get("wqev_%d" % bank))
                        for k in range(4):
                            ins = e.matmul(ps[bank][:, 0:512], adg[k][ob],
                                           bas[k][ob][:, ic * 512:(ic + 1) * 512],
                                           start=(k == 0), stop=(k == 3))
                        ins.then_inc(S["pe"], 1)
                    tev = C.bump("ev", 1)
                    if E == "v":
                        e.wait_ge(S["pe"], tpe)
                        e.wait_ge(S[ssem], C.val(ssem))
                        e.tensor_copy(wq_stage[slot], ps[bank][:, 0:512]).then_inc(S["ev"], 1)
                    C.set("wqev_%d" % bank, tev)
                    C.bump(ssem, 16)
                    if E == "y":
                        e.wait_ge(S["ev"], tev)
                        e.dma_start(out=wq_own[ob * 128:(ob + 1) * 128, ic * 512:(ic + 1) * 512],
                                    in_=wq_stage[slot]).then_inc(S[ssem], 16)
            t_wq0, t_wq1 = C.val("wqs0"), C.val("wqs1")

            C.bump("cc", 1)
            if E == "g":
                e.wait_ge(S["wqs0"], t_wq0)
                e.wait_ge(S["wqs1"], t_wq1)
                if P0_CC:
                    e.collective_compute(
                        "AllGather", ALU.bypass,
                        replica_groups=[[0, 1, 2, 3, 4, 5, 6, 7]],
                        ins=[wq_own.ap().opt()],
                        outs=[wq_all.ap().opt()],
                    ).then_inc(S["cc"], 1)
                else:
                    e.engine_nop().then_inc(S["cc"], 1)
                e.dma_start(out=wqb_bf[:, :], in_=wqb[:, :]).then_inc(S["wq1"], 16)
                e.dma_start(out=wk_bf[:, :], in_=wk[:, :]).then_inc(S["wq1"], 16)
                e.dma_start(out=wv_bf[:, :], in_=wv[:, :]).then_inc(S["wq1"], 16)
            C.bump("wq1", 48)

        if emit(0):
            run_phase(p0, "p0")

        def p1a(E, e):
            ab = Alloc(ab_t, AB_N)
            af = Alloc(af_t, AF_N)
            CH = 256
            NCH = TOK // CH
            wqaT = [[ab.tile(512) for _ in range(NI)] for _ in range(2)]
            for hg in range(2):
                for i in range(NI):
                    C.bump("wt", 16)
                    if E == "y":
                        if hg == 0 and i == 0:
                            e.wait_ge(S["pre"], C.val("pre"))
                        e.dma_start(out=wqaT[hg][i],
                                    in_=wqa_bf[hg * 512:(hg + 1) * 512, i * 128:(i + 1) * 128],
                                    transpose=True).then_inc(S["wt"], 16)
            t_wqaT = C.val("wt")

            x_ln = [af.tile(D) for _ in range(2)]
            stat = mk_stats(af, 2)
            stat2 = mk_stats(af, 2)
            sq_s = ab.tile(D)
            ln_s = [ab.tile(D) for _ in range(2)]
            xlt = [ab.tile(CH) for _ in range(NI)]
            woT = [[ab.tile(512) for _ in range(HA)] for _ in range(2)]
            sig = [ab.tile(CH) for _ in range(2)]
            retr = [ab.tile(CH) for _ in range(HA)]
            rec_bf = ab.tile(CH)
            bc_sb = ab.tile(CH)
            elu_t = af.tile(CH)
            elu_t2 = af.tile(CH)
            x0r = [af.tile(512) for _ in range(2)]
            x1t = af.tile(D)

            def ln_store(gt, sl, dst, tdv):
                ssem = "lo%d" % sl
                C.bump(ssem, 16)
                if E == "s":
                    e.wait_ge(S["dv"], tdv)
                    e.dma_start(out=dst[gt * 128:(gt + 1) * 128, :], in_=ln_s[sl]).then_inc(S[ssem], 16)

            for ch in range(NCH):
                for tt in range(2):
                    gt = ch * 2 + tt
                    sl = gt % 2
                    xsem = "xl%d" % sl
                    t_x = C.bump(xsem, 16)
                    if E == "s":
                        e.wait_ge(S["dv"], C.get("xlnuse_%d" % sl))
                        e.dma_start(out=x_ln[sl], in_=x0[gt * 128:(gt + 1) * 128, :]).then_inc(S[xsem], 16)
                    tdv = ln_pipeline(E, e, x_ln[sl], sq_s, stat[sl], ln_s[sl],
                                      waits_v=[(xsem, t_x)], waits_s=[(xsem, t_x)],
                                      war_v=[("lo%d" % sl, C.get("lnslot_%d" % sl))])
                    C.set("xlnuse_%d" % sl, tdv)
                    ln_store(gt, sl, ln1_tm, tdv)
                    C.set("lnslot_%d" % sl, C.val("lo%d" % sl))

                t_lo0, t_lo1 = C.val("lo0"), C.val("lo1")
                for i in range(NI):
                    C.bump("xlt", 16)
                    if E == "y":
                        if i == 0:
                            e.wait_ge(S["lo0"], t_lo0)
                            e.wait_ge(S["lo1"], t_lo1)
                            e.wait_ge(S["pe"], C.get("xlt_use"))
                            if ch == 0:
                                e.wait_ge(S["wt"], t_wqaT)
                        e.dma_start(out=xlt[i],
                                    in_=ln1_tm[ch * CH:(ch + 1) * CH, i * 128:(i + 1) * 128],
                                    transpose=True).then_inc(S["xlt"], 16)
                t_xlt = C.val("xlt")

                for h in range(HA):
                    hg, hh = h // 4, h % 4
                    bank = h % 2
                    tpe = C.bump("pe", 1)
                    if E == "t":
                        if h == 0:
                            e.wait_ge(S["xlt"], t_xlt)
                        e.wait_ge(S["dv"], C.get("sigdone_%d" % bank))
                        for i in range(NI):
                            ins = e.matmul(ps[bank][:, 0:CH], wqaT[hg][i][:, hh * 128:(hh + 1) * 128],
                                           xlt[i], start=(i == 0), stop=(i == NI - 1))
                        ins.then_inc(S["pe"], 1)
                    C.bump("dv", 1)
                    if E == "v":
                        e.wait_ge(S["pe"], tpe)
                        e.tensor_scalar(elu_t[:, 0:CH], ps[bank][:, 0:CH], bqa_sb[:, h:h + 1], 0.0,
                                        ALU.add, ALU.min).then_inc(S["dv"], 1)
                    tac = C.bump("ac", 1)
                    if E == "s":
                        e.wait_ge(S["dv"], C.val("dv"))
                        e.activation(elu_t2[:, 0:CH], elu_t[:, 0:CH], AF.Exp).then_inc(S["ac"], 1)
                    tsg0 = C.bump("dv", 1)
                    if E == "v":
                        e.wait_ge(S["ac"], tac)
                        e.tensor_scalar(elu_t[:, 0:CH], ps[bank][:, 0:CH], bqa_sb[:, h:h + 1], 0.0,
                                        ALU.add, ALU.max).then_inc(S["dv"], 1)
                    tsig = C.bump("dv", 1)
                    if E == "v":
                        e.wait_ge(S["dv"], tsg0)
                        e.tensor_add(sig[bank], elu_t[:, 0:CH], elu_t2[:, 0:CH]).then_inc(S["dv"], 1)
                    C.set("sigdone_%d" % bank, tsig)

                    tpe2 = C.bump("pe", 1)
                    if E == "t":
                        e.wait_ge(S["dv"], tsig)
                        ins = e.matmul(ps[6][0:1, 0:CH], z_sb[:, h:h + 1], sig[bank],
                                       start=True, stop=True)
                        ins.then_inc(S["pe"], 1)
                    td0 = C.bump("dv", 1)
                    if E == "v":
                        e.wait_ge(S["pe"], tpe2)
                        e.tensor_scalar(elu_t[0:1, 0:CH], ps[6][0:1, 0:CH], 1.0, DEN_EPS,
                                        ALU.mult, ALU.add).then_inc(S["dv"], 1)
                    td1 = C.bump("dv", 1)
                    if E == "v":
                        e.wait_ge(S["dv"], td0)
                        e.reciprocal(elu_t2[0:1, 0:CH], elu_t[0:1, 0:CH]).then_inc(S["dv"], 1)
                    td2 = C.bump("dv", 1)
                    if E == "v":
                        e.wait_ge(S["dv"], td1)
                        e.tensor_copy(rec_bf[0:1, 0:CH], elu_t2[0:1, 0:CH]).then_inc(S["dv"], 1)
                    tpe3 = C.bump("pe", 1)
                    if E == "t":
                        e.wait_ge(S["dv"], td2)
                        e.wait_ge(S["ac"], C.get("bc_use"))
                        ins = e.matmul(ps[7][:, 0:CH], ones_sb[0:1, :], rec_bf[0:1, 0:CH],
                                       start=True, stop=True)
                        ins.then_inc(S["pe"], 1)
                    tac2 = C.bump("ac", 1)
                    if E == "s":
                        e.wait_ge(S["pe"], tpe3)
                        e.wait_ge(S["dv"], C.get("bcs_use"))
                        e.activation(bc_sb, ps[7][:, 0:CH], AF.Copy).then_inc(S["ac"], 1)
                    C.set("bc_use", tac2)
                    tpe4 = C.bump("pe", 1)
                    if E == "t":
                        e.wait_ge(S["dv"], C.get("retr_rd_%d" % bank))
                        ins = e.matmul(ps[bank + 2][:, 0:CH], m_sb[:, h * 128:(h + 1) * 128],
                                       sig[bank], start=True, stop=True)
                        ins.then_inc(S["pe"], 1)
                    tretr = C.bump("dv", 1)
                    if E == "v":
                        e.wait_ge(S["pe"], tpe4)
                        e.wait_ge(S["ac"], tac2)
                        e.wait_ge(S["pe"], C.get("woproj_done"))
                        e.tensor_mul(retr[h], ps[bank + 2][:, 0:CH], bc_sb).then_inc(S["dv"], 1)
                    C.set("retr_rd_%d" % bank, tretr)
                    C.set("bcs_use", tretr)
                    C.set("retr_done", tretr)
                C.set("xlt_use", C.val("pe"))

                for oc in range(HA):
                    buf = oc % 2
                    bsem = "w2a" if buf == 0 else "w2b"
                    for h in range(HA):
                        C.bump(bsem, 16)
                        if E == "y":
                            if h == 0:
                                e.wait_ge(S["pe"], C.get("woTuse_%d" % buf))
                            e.dma_start(out=woT[buf][h],
                                        in_=wo_bf[oc * 512:(oc + 1) * 512, h * 128:(h + 1) * 128],
                                        transpose=True).then_inc(S[bsem], 16)
                    t_wo = C.val(bsem)
                    for ts2 in range(2):
                        gt = ch * 2 + ts2
                        bank = 4 + (ts2 % 2)
                        tpe = C.bump("pe", 1)
                        if E == "t":
                            if ts2 == 0:
                                e.wait_ge(S[bsem], t_wo)
                                if oc == 0:
                                    e.wait_ge(S["dv"], C.get("retr_done"))
                            e.wait_ge(S["dv"], C.get("woev_%d" % bank))
                            for h in range(HA):
                                ins = e.matmul(ps[bank][:, 0:512], retr[h][:, ts2 * 128:(ts2 + 1) * 128],
                                               woT[buf][h], start=(h == 0), stop=(h == HA - 1))
                            ins.then_inc(S["pe"], 1)
                        if ts2 == 1:
                            C.set("woTuse_%d" % buf, tpe)
                        sl = ts2 % 2
                        rsem = "xr%d" % sl
                        t_prev = C.val(rsem)
                        t_x0 = C.bump(rsem, 16)
                        if E == "s":
                            e.wait_ge(S["dv"], C.get("x0r_use_%d" % sl))
                            e.wait_ge(S[rsem], t_prev)
                            e.dma_start(out=x0r[sl], in_=x0[gt * 128:(gt + 1) * 128,
                                                           oc * 512:(oc + 1) * 512]).then_inc(S[rsem], 16)
                        tadd = C.bump("dv", 1)
                        if E == "v":
                            e.wait_ge(S["pe"], tpe)
                            e.wait_ge(S[rsem], t_x0)
                            e.tensor_add(x0r[sl], ps[bank][:, 0:512], x0r[sl]).then_inc(S["dv"], 1)
                        C.set("x0r_use_%d" % sl, tadd)
                        C.set("woev_%d" % bank, tadd)
                        C.bump(rsem, 16)
                        if E == "s":
                            e.wait_ge(S["dv"], tadd)
                            e.dma_start(out=x1[gt * 128:(gt + 1) * 128, oc * 512:(oc + 1) * 512],
                                        in_=x0r[sl]).then_inc(S[rsem], 16)
                C.set("woproj_done", C.val("pe"))

                for tt in range(2):
                    gt = ch * 2 + tt
                    t_xr0, t_xr1 = C.val("xr0"), C.val("xr1")
                    t_x1 = C.bump("x1t", 16)
                    if E == "s":
                        e.wait_ge(S["xr0"], t_xr0)
                        e.wait_ge(S["xr1"], t_xr1)
                        e.wait_ge(S["dv"], C.get("x1t_use"))
                        e.dma_start(out=x1t, in_=x1[gt * 128:(gt + 1) * 128, :]).then_inc(S["x1t"], 16)
                    sl = gt % 2
                    tdv = ln_pipeline(E, e, x1t, sq_s, stat2[sl], ln_s[sl],
                                      waits_v=[("x1t", t_x1)], waits_s=[("x1t", t_x1)],
                                      war_v=[("lo%d" % sl, C.get("lnslot_%d" % sl))])
                    C.set("x1t_use", tdv)
                    ln_store(gt, sl, ln2_tm, tdv)
                    C.set("lnslot_%d" % sl, C.val("lo%d" % sl))

            if debug:
                C.bump("dbgs", 16)
                if E == "g":
                    e.wait_ge(S["xr0"], C.val("xr0"))
                    e.wait_ge(S["xr1"], C.val("xr1"))
                    e.dma_start(out=dbg["d_x1"][:, :], in_=x1[:, :]).then_inc(S["dbgs"], 16)

        if emit(1):
            run_phase(p1a, "p1a")

        def ffn_pass(E, e, lnsrc):
            ab = Alloc(ab_t, AB_N)
            xlt2 = [ab.tile(1024) for _ in range(NI)]
            wtiles = [[ab.tile(512) for _ in range(NI)] for _ in range(2)]
            evst = [ab.tile(512) for _ in range(4)]
            t_srcs = [(sm, C.val(sm)) for sm in ("lo0", "lo1", "ro0", "ro1")]
            for th in range(2):
                for i in range(NI):
                    C.bump("xlt", 16)
                    if E == "y":
                        if i == 0:
                            if th == 0:
                                for sm, vv in t_srcs:
                                    if vv > 0:
                                        e.wait_ge(S[sm], vv)
                            e.wait_ge(S["pe"], C.get("fx_use"))
                        e.dma_start(out=xlt2[i],
                                    in_=lnsrc[th * 1024:(th + 1) * 1024, i * 128:(i + 1) * 128],
                                    transpose=True).then_inc(S["xlt"], 16)
                t_xl = C.val("xlt")
                for obg in range(8):
                    buf = obg % 2
                    bsem = "fwa" if buf == 0 else "fwb"
                    for i in range(NI):
                        C.bump(bsem, 16)
                        if E == "y":
                            if i == 0:
                                e.wait_ge(S["pe"], C.get("fw_use_%d" % buf))
                                if th == 0 and obg == 0:
                                    e.wait_ge(S["cc"], 1)
                            e.dma_start(out=wtiles[buf][i],
                                        in_=wq_all[obg * 512:(obg + 1) * 512, i * 128:(i + 1) * 128],
                                        transpose=True).then_inc(S[bsem], 16)
                    t_w = C.val(bsem)
                    for ob4 in range(4):
                        ob = obg * 4 + ob4
                        bank = (ob4 % 2) * 2
                        tpe = C.bump("pe", 1)
                        if E == "t":
                            if ob4 == 0:
                                e.wait_ge(S[bsem], t_w)
                                e.wait_ge(S["xlt"], t_xl)
                            e.wait_ge(S["ev"], C.get("fev_%d" % bank))
                            for i in range(NI):
                                for tc in range(2):
                                    ins = e.matmul(ps[bank + tc][:, 0:512],
                                                   wtiles[buf][i][:, ob4 * 128:(ob4 + 1) * 128],
                                                   xlt2[i][:, tc * 512:(tc + 1) * 512],
                                                   start=(i == 0), stop=(i == NI - 1))
                            ins.then_inc(S["pe"], 1)
                        if ob4 == 3:
                            C.set("fw_use_%d" % buf, tpe)
                            if obg == 7:
                                C.set("fx_use", tpe)
                        for tc in range(2):
                            sl = (ob4 % 2) * 2 + tc
                            ssem = "ev%d" % sl
                            tev = C.bump("ev", 1)
                            if E == "v":
                                e.wait_ge(S["pe"], tpe)
                                e.wait_ge(S[ssem], C.val(ssem))
                                e.tensor_copy(evst[sl], ps[bank + tc][:, 0:512]).then_inc(S["ev"], 1)
                            C.bump(ssem, 16)
                            if E == "s":
                                e.wait_ge(S["ev"], tev)
                                e.dma_start(out=ffn_fm[ob * 128:(ob + 1) * 128,
                                                       th * 1024 + tc * 512: th * 1024 + (tc + 1) * 512],
                                            in_=evst[sl]).then_inc(S[ssem], 16)
                        C.set("fev_%d" % bank, C.val("ev"))

        if emit(2):
            run_phase(lambda E, e: ffn_pass(E, e, ln2_tm), "p1b")

        def resid_ln(E, e, xin, xout, lnout, is_final, xin_sems):
            ab = Alloc(ab_t, AB_N)
            af = Alloc(af_t, AF_N)
            xa = [af.tile(D) for _ in range(2)]
            ft = [ab.tile(D) for _ in range(2)]
            ln_s = [ab.tile(D) for _ in range(2)]
            sq_s = ab.tile(D)
            stat = mk_stats(af, 2)
            t_ev = [(sm, C.val(sm)) for sm in ("ev0", "ev1", "ev2", "ev3")]
            t_xin = [(sm, C.val(sm)) for sm in xin_sems]
            for tt in range(16):
                sl = tt % 2
                asem = "ra%d" % sl
                fsem = "rf%d" % sl
                osem = "ro%d" % sl
                t_x = C.bump(asem, 16)
                if E == "s":
                    if tt < 2:
                        for sm, vv in t_xin:
                            if vv > 0:
                                e.wait_ge(S[sm], vv)
                    e.wait_ge(S["dv"], C.get("rl_xa_%d" % sl))
                    if C.get("rl_xo_%d" % sl) > 0:
                        e.wait_ge(S[osem], C.get("rl_xo_%d" % sl))
                    e.dma_start(out=xa[sl], in_=xin[tt * 128:(tt + 1) * 128, :]).then_inc(S[asem], 16)
                t_f = C.bump(fsem, 16)
                if E == "y":
                    if tt < 2:
                        for sm, vv in t_ev:
                            if vv > 0:
                                e.wait_ge(S[sm], vv)
                    e.wait_ge(S["dv"], C.get("rl_xa_%d" % sl))
                    e.dma_start(out=ft[sl], in_=ffn_fm[0:D, tt * 128:(tt + 1) * 128],
                                transpose=True).then_inc(S[fsem], 16)
                tadd = C.bump("dv", 1)
                if E == "v":
                    e.wait_ge(S[asem], t_x)
                    e.wait_ge(S[fsem], t_f)
                    e.wait_ge(S[osem], C.get("rl_xo_%d" % sl))
                    e.tensor_add(xa[sl], xa[sl], ft[sl]).then_inc(S["dv"], 1)
                C.bump(osem, 16)
                if E == "s":
                    e.wait_ge(S["dv"], tadd)
                    e.dma_start(out=xout[tt * 128:(tt + 1) * 128, :], in_=xa[sl]).then_inc(S[osem], 16)
                tlast = tadd
                if lnout is not None:
                    tdv = ln_pipeline(E, e, xa[sl], sq_s, stat[sl], ln_s[sl],
                                      waits_v=[("dv", tadd)], waits_s=[("dv", tadd)],
                                      war_v=[(osem, C.get("rl_xo_%d" % sl))])
                    tlast = tdv
                    C.bump(osem, 16)
                    if E == "s":
                        e.wait_ge(S["dv"], tdv)
                        e.dma_start(out=lnout[tt * 128:(tt + 1) * 128, :], in_=ln_s[sl]).then_inc(S[osem], 16)
                C.set("rl_xa_%d" % sl, tlast)
                C.set("rl_xo_%d" % sl, C.val(osem))
            if is_final:
                t_ro0, t_ro1 = C.val("ro0"), C.val("ro1")
                C.bump("fin", 1)
                if E == "g":
                    e.wait_ge(S["ro0"], t_ro0)
                    e.wait_ge(S["ro1"], t_ro1)
                    if C.val("dbgs"):
                        e.wait_ge(S["dbgs"], C.val("dbgs"))
                    e.engine_nop().then_inc(S["fin"], 1)

        def p1c(E, e):
            resid_ln(E, e, x1, x2, ln3_own, False, ("xr0", "xr1"))
            t_ro0, t_ro1 = C.val("ro0"), C.val("ro1")
            C.bump("cc", 1)
            if E == "g":
                e.wait_ge(S["ro0"], t_ro0)
                e.wait_ge(S["ro1"], t_ro1)
                if SIM_NOCC:
                    e.engine_nop().then_inc(S["cc"], 1)
                else:
                    e.collective_compute(
                        "AllGather", ALU.bypass,
                        replica_groups=[[0, 1], [2, 3], [4, 5], [6, 7]],
                        ins=[ln3_own.ap().opt()],
                        outs=[ln3_full.ap().opt()],
                    ).then_inc(S["cc"], 1)
                if debug:
                    e.dma_start(out=dbg["d_x2"][:, :], in_=x2[:, :]).then_inc(S["dbgs"], 16)
                    e.wait_ge(S["cc"], C.val("cc"))
                    e.dma_start(out=dbg["d_ln3"][:, :], in_=ln3_full[:, :]).then_inc(S["dbgs"], 16)
                    e.dma_start(out=dbg["d_wq"][:, :], in_=wq_all[:, :]).then_inc(S["dbgs"], 16)
            if debug:
                C.bump("dbgs", 48)

        if emit(3):
            run_phase(p1c, "p1c")

        def p1d(E, e):
            ab = Alloc(ab_t, AB_N)
            xlt3 = [ab.tile(1024) for _ in range(NI)]
            wtl_base = ab.off
            wtl = [[ab.tile(256) for _ in range(NI)] for _ in range(2)]
            qst = [ab.tile(512) for _ in range(8)]
            wvt = [ab_t[:, wtl_base + i * 512: wtl_base + (i + 1) * 512] for i in range(NI)]
            qsems = ["q0", "q1", "q2s", "q3", "q4", "q5", "q6", "q7"]
            t_cc = C.val("cc")
            C.bump("po", 1)
            if E == "g":
                e.wait_ge(S["cc"], t_cc)
                e.engine_nop().then_inc(S["po"], 1)
            for qt4 in range(4):
                for i in range(NI):
                    C.bump("xlt", 16)
                    if E == "y":
                        if i == 0:
                            e.wait_ge(S["po"], C.val("po"))
                            e.wait_ge(S["pe"], C.get("p1d_x_use"))
                        e.dma_start(out=xlt3[i],
                                    in_=ln3_full[qt4 * 1024:(qt4 + 1) * 1024, i * 128:(i + 1) * 128],
                                    transpose=True).then_inc(S["xlt"], 16)
                t_xl = C.val("xlt")
                for wname, wsrc, n_sb, bias_sb, dstT in (
                    ("q", wqb_bf, 4, bq_sb, qT_d),
                    ("k", wk_bf, 2, bk_sb, kT_d),
                ):
                    for sb in range(n_sb):
                        for dbp in range(2):
                            buf = (sb * 2 + dbp) % 2
                            bsem = "wta" if buf == 0 else "wtb"
                            for i in range(NI):
                                C.bump(bsem, 16)
                                if E == "y":
                                    if i == 0:
                                        e.wait_ge(S["wq1"], 48)
                                        e.wait_ge(S["pe"], C.get("p1d_w_use_%d" % buf))
                                        e.wait_ge(S["pe"], C.get("p1d_wv_use"))
                                    e.dma_start(out=wtl[buf][i],
                                                in_=wsrc[sb * 512 + dbp * 256: sb * 512 + (dbp + 1) * 256,
                                                         i * 128:(i + 1) * 128],
                                                transpose=True).then_inc(S[bsem], 16)
                            t_wl = C.val(bsem)
                            grp = (sb * 2 + dbp) % 2
                            pb = grp * 4
                            tpe = C.bump("pe", 1)
                            if E == "t":
                                e.wait_ge(S[bsem], t_wl)
                                if sb == 0 and dbp == 0 and wname == "q":
                                    e.wait_ge(S["xlt"], t_xl)
                                    e.wait_ge(S["dv"], C.get("p1d_vev_all"))
                                e.wait_ge(S["ac"], C.get("p1d_ev_%d" % grp))
                                for i in range(NI):
                                    for db2 in range(2):
                                        for tc in range(2):
                                            e.matmul(ps[pb + db2 * 2 + tc][:, 0:512],
                                                     wtl[buf][i][:, db2 * 128:(db2 + 1) * 128],
                                                     xlt3[i][:, tc * 512:(tc + 1) * 512],
                                                     start=(i == 0), stop=False)
                                for db2 in range(2):
                                    db_ = sb * 4 + dbp * 2 + db2
                                    for tc in range(2):
                                        ins = e.matmul(ps[pb + db2 * 2 + tc][:, 0:512],
                                                       bias_sb[0:1, db_ * 128:(db_ + 1) * 128],
                                                       ones512[0:1, :],
                                                       start=False, stop=True)
                                ins.then_inc(S["pe"], 1)
                            C.set("p1d_w_use_%d" % buf, tpe)
                            for db2 in range(2):
                                db = sb * 4 + dbp * 2 + db2
                                for tc in range(2):
                                    slq = grp * 4 + db2 * 2 + tc
                                    ssem = qsems[slq]
                                    tev = C.bump("ac", 1)
                                    if E == "s":
                                        e.wait_ge(S["pe"], tpe)
                                        e.wait_ge(S[ssem], C.val(ssem))
                                        e.activation(qst[slq], ps[pb + db2 * 2 + tc][:, 0:512],
                                                     AF.Copy).then_inc(S["ac"], 1)
                                    C.set("p1d_ev_%d" % grp, tev)
                                    C.bump(ssem, 16)
                                    if E == "s":
                                        e.wait_ge(S["ac"], tev)
                                        e.dma_start(out=dstT[db * 128:(db + 1) * 128,
                                                             qt4 * 1024 + tc * 512: qt4 * 1024 + (tc + 1) * 512],
                                                    in_=qst[slq]).then_inc(S[ssem], 16)
                for dc in range(2):
                    for i in range(NI):
                        C.bump("wv2", 16)
                        if E == "y":
                            if i == 0:
                                e.wait_ge(S["pe"], C.get("p1d_w_use_0"))
                                e.wait_ge(S["pe"], C.get("p1d_w_use_1"))
                                e.wait_ge(S["pe"], C.get("p1d_wv_use"))
                            e.dma_start(out=wvt[i],
                                        in_=wv_bf[dc * 512:(dc + 1) * 512, i * 128:(i + 1) * 128],
                                        transpose=True).then_inc(S["wv2"], 16)
                    t_wv = C.val("wv2")
                    for tt in range(8):
                        bank = tt % 4
                        tpe = C.bump("pe", 1)
                        if E == "t":
                            if tt == 0:
                                e.wait_ge(S["wv2"], t_wv)
                                if dc == 0:
                                    e.wait_ge(S["ac"], C.get("p1d_ev_0"))
                                    e.wait_ge(S["ac"], C.get("p1d_ev_1"))
                            e.wait_ge(S["dv"], C.get("p1d_vev_%d" % bank))
                            for i in range(NI):
                                e.matmul(ps[bank][:, 0:512],
                                         xlt3[i][:, tt * 128:(tt + 1) * 128],
                                         wvt[i], start=(i == 0), stop=False)
                            ins = e.matmul(ps[bank][:, 0:512], ones_sb[0:1, 0:128],
                                           bvr_sb[0:1, dc * 512:(dc + 1) * 512],
                                           start=False, stop=True)
                            ins.then_inc(S["pe"], 1)
                        if tt == 7:
                            C.set("p1d_wv_use", tpe)
                        slv = 4 + bank
                        ssem = qsems[slv]
                        tev = C.bump("dv", 1)
                        if E == "v":
                            e.wait_ge(S["pe"], tpe)
                            e.wait_ge(S[ssem], C.val(ssem))
                            e.tensor_copy(qst[slv], ps[bank][:, 0:512]).then_inc(S["dv"], 1)
                        C.set("p1d_vev_%d" % bank, tev)
                        C.bump(ssem, 16)
                        if E == "s":
                            e.wait_ge(S["dv"], tev)
                            e.dma_start(out=v_d[qt4 * 1024 + tt * 128: qt4 * 1024 + (tt + 1) * 128,
                                                dc * 512:(dc + 1) * 512],
                                        in_=qst[slv]).then_inc(S[ssem], 16)
                C.set("p1d_x_use", C.val("pe"))
                C.set("p1d_vev_all", C.val("dv"))
            if debug:
                C.bump("dbgs", 48)
                if E == "g":
                    for sm in qsems:
                        if C.val(sm) > 0:
                            e.wait_ge(S[sm], C.val(sm))
                    e.dma_start(out=dbg["d_qT"][:, :], in_=qT_d[:, :]).then_inc(S["dbgs"], 16)
                    e.dma_start(out=dbg["d_kT"][:, :], in_=kT_d[:, :]).then_inc(S["dbgs"], 16)
                    e.dma_start(out=dbg["d_v"][:, :], in_=v_d[:, :]).then_inc(S["dbgs"], 16)

        if emit(4):
            run_phase(p1d, "p1d")

        def p2(E, e):
            ab = Alloc(ab_t, AB_N)
            af = Alloc(af_t, AF_N)
            kts = [ab.tile(T) for _ in range(2)]
            qts = [ab.tile(T) for _ in range(2)]
            v129 = [ab.tile(NI * 129) for _ in range(2)]
            pt_s = [ab.tile(512) for _ in range(2)]
            att_s = [ab.tile(128) for _ in range(4)]
            mxb = ab.tile(4)
            mxcol = af.tile(8)
            mxt = af.tile(256)
            bia = af.tile(2)
            recs = af.tile(4)
            qsems = ["q0", "q1", "q2s", "q3", "q4", "q5", "q6", "q7"]
            t_qkv = [(sm, C.val(sm)) for sm in qsems]
            t_mxinit = C.bump("dv", 1)
            if E == "v":
                e.memset(mxt, -1e30).then_inc(S["dv"], 1)
            C.bump("po", 1)
            if E == "g":
                for b in range(2):
                    vv = v129[b].rearrange("p (a b) -> p a b", b=129)
                    e.memset(vv[:, :, 128:129], 1.0)
                e.engine_nop().then_inc(S["po"], 1)
            t_ones = C.val("po")

            for kk in range(P2_KK):
                kb = kk % 2
                ksem = "k2a" if kb == 0 else "k2b"
                C.bump(ksem, 32)
                if E == "y":
                    e.wait_ge(S["pe"], C.get("p2_kv_use_%d" % kb))
                    if kk < 2:
                        e.wait_ge(S["po"], t_ones)
                        for sm, vv in t_qkv:
                            if vv > 0:
                                e.wait_ge(S[sm], vv)
                    e.dma_start(out=kts[kb], in_=kT_d[kk * 128:(kk + 1) * 128, :]).then_inc(S[ksem], 16)
                    vv = v129[kb].rearrange("p (a b) -> p a b", b=129)
                    e.dma_start(out=vv[:, :, 0:128],
                                in_=v_d.ap().rearrange("(a p) c -> p a c", p=128)[:, :, kk * 128:(kk + 1) * 128],
                                ).then_inc(S[ksem], 16)
                t_kv = C.val(ksem)
                for j in range(2):
                    qb = j % 2
                    qsem = "q2a" if qb == 0 else "q2b"
                    lh = j * 8 + kk
                    C.bump(qsem, 16)
                    if E == "y":
                        e.wait_ge(S["pe"], C.get("p2_q_use_%d" % qb))
                        e.dma_start(out=qts[qb], in_=qT_d[lh * 128:(lh + 1) * 128, :]).then_inc(S[qsem], 16)
                    t_q = C.val(qsem)
                    for qt in range(32):
                        nlc = qt // 4 + 1
                        for lc in range(nlc):
                            bank = 6 + lc % 2
                            tpe = C.bump("pe", 1)
                            if E == "t":
                                if qt == 0 and lc == 0:
                                    e.wait_ge(S[ksem], t_kv)
                                    e.wait_ge(S[qsem], t_q)
                                    e.wait_ge(S["dv"], C.val("dv"))
                                e.wait_ge(S["mx"], C.get("p2_psA_%d" % bank))
                                ins = e.matmul(ps[bank][:, 0:512], qts[qb][:, qt * 128:(qt + 1) * 128],
                                               kts[kb][:, lc * 512:(lc + 1) * 512],
                                               start=True, stop=True)
                                ins.then_inc(S["pe"], 1)
                            tmx = C.bump("mx", 1)
                            if E == "v":
                                e.wait_ge(S["pe"], tpe)
                                if qt == 0 and lc == 0:
                                    e.wait_ge(S["dv"], t_mxinit)
                                e.reduce_max(mxt[:, qt * 8 + lc: qt * 8 + lc + 1],
                                             ps[bank][:, 0:512:4], axis=AX.X).then_inc(S["mx"], 1)
                            C.set("p2_psA_%d" % bank, tmx)
                    def chain_step(c, step):
                        cb2 = c % 2
                        if step == 0:
                            tc0 = C.bump("dv", 1)
                            if E == "v":
                                e.wait_ge(S["mx"], C.val("mx"))
                                e.wait_ge(S["pe"], C.get("p2_mxb_use"))
                                e.wait_ge(S["ac"], C.get("p2_bia_use_%d" % cb2))
                                e.reduce_max(mxcol[:, cb2:cb2 + 1], mxt[:, 32 * c:32 * c + 32],
                                             axis=AX.X).then_inc(S["dv"], 1)
                            C.set("p2_ch0_%d" % cb2, tc0)
                        elif step == 1:
                            tc1 = C.bump("dv", 1)
                            if E == "v":
                                e.wait_ge(S["dv"], C.get("p2_ch0_%d" % cb2))
                                e.tensor_copy(mxb[:, cb2:cb2 + 1], mxcol[:, cb2:cb2 + 1]).then_inc(S["dv"], 1)
                            C.set("p2_ch1_%d" % cb2, tc1)
                        elif step == 2:
                            tpe_ = C.bump("pe", 1)
                            if E == "t":
                                e.wait_ge(S["dv"], C.get("p2_ch1_%d" % cb2))
                                e.wait_ge(S["dv"], C.get("p2_ch4rd"))
                                ins_ = e.matmul(ps[6][0:1, 0:128], mxb[:, cb2:cb2 + 1], ident_sb[:, :],
                                                start=True, stop=True)
                                ins_.then_inc(S["pe"], 1)
                            C.set("p2_ch2_%d" % cb2, tpe_)
                        elif step == 3:
                            tc2 = C.bump("dv", 1)
                            if E == "v":
                                e.wait_ge(S["pe"], C.get("p2_ch2_%d" % cb2))
                                e.reduce_max(mxcol[0:1, 2 + cb2:3 + cb2], ps[6][0:1, 0:128],
                                             axis=AX.X).then_inc(S["dv"], 1)
                            C.set("p2_ch3_%d" % cb2, C.val("dv"))
                            tc3 = C.bump("dv", 1)
                            if E == "v":
                                e.wait_ge(S["dv"], C.get("p2_ch3_%d" % cb2))
                                e.tensor_copy(mxb[0:1, 2 + cb2:3 + cb2],
                                              mxcol[0:1, 2 + cb2:3 + cb2]).then_inc(S["dv"], 1)
                            C.set("p2_ch4rd", tc3)
                            C.set("p2_ch3b_%d" % cb2, tc3)
                        elif step == 4:
                            tpe2_ = C.bump("pe", 1)
                            if E == "t":
                                e.wait_ge(S["dv"], C.get("p2_ch3b_%d" % cb2))
                                ins_ = e.matmul(ps[6][:, 256 + cb2:257 + cb2], nsrow_sb[0:1, :],
                                                mxb[0:1, 2 + cb2:3 + cb2], start=True, stop=True)
                                ins_.then_inc(S["pe"], 1)
                            C.set("p2_mxb_use", tpe2_)
                            C.set("p2_ch5_%d" % cb2, tpe2_)
                        elif step == 5:
                            tbia_ = C.bump("dv", 1)
                            if E == "v":
                                e.wait_ge(S["pe"], C.get("p2_ch5_%d" % cb2))
                                e.tensor_copy(bia[:, cb2:cb2 + 1],
                                              ps[6][:, 256 + cb2:257 + cb2]).then_inc(S["dv"], 1)
                            C.set("p2_tbia_%d" % cb2, tbia_)

                    for st_ in range(6):
                        chain_step(P2_C0, st_)
                    for c in range(P2_C0, P2_NC):
                        cb = c % 2
                        tbia = C.get("p2_tbia_%d" % cb)
                        nlt = 4 * c + 4
                        for lt in range(nlt):
                            bank = lt % 2
                            slp = lt % 2
                            tpe3 = C.bump("pe", 1)
                            if E == "t":
                                e.wait_ge(S["ac"], C.get("p2_psT_%d" % bank))
                                ins = e.matmul(ps[2 + bank][:, 0:512], kts[kb][:, lt * 128:(lt + 1) * 128],
                                               qts[qb][:, c * 512:(c + 1) * 512], start=True, stop=True)
                                ins.then_inc(S["pe"], 1)
                            tex = C.bump("ac", 1)
                            if E == "s":
                                e.wait_ge(S["pe"], tpe3)
                                if lt == 0:
                                    e.wait_ge(S["dv"], tbia)
                                e.wait_ge(S["pe"], C.get("p2_pt_use_%d" % slp))
                                e.wait_ge(S["po"], C.get("p2_msk_%d" % slp))
                                e.activation(pt_s[slp], ps[2 + bank][:, 0:512], AF.Exp,
                                             bias=bia[:, cb:cb + 1], scale=SCALE).then_inc(S["ac"], 1)
                            C.set("p2_psT_%d" % bank, tex)
                            if c + 1 < P2_NC and 1 <= lt <= 6:
                                chain_step(c + 1, lt - 1)
                            tmask = None
                            if lt >= 4 * c:
                                s_idx = lt - 4 * c
                                tmask = C.bump("po", 1)
                                if E == "g":
                                    e.wait_ge(S["ac"], tex)
                                    e.tensor_mul(pt_s[slp], pt_s[slp],
                                                 mask_sb[:, s_idx * 512:(s_idx + 1) * 512]).then_inc(S["po"], 1)
                                C.set("p2_msk_%d" % slp, tmask)
                            tpe4 = C.bump("pe", 1)
                            if E == "t":
                                if tmask is not None:
                                    e.wait_ge(S["po"], tmask)
                                else:
                                    e.wait_ge(S["ac"], tex)
                                if lt == 0:
                                    for s2 in range(4):
                                        e.wait_ge(S["ac"], C.get("p2_avb_%d" % s2))
                                        e.wait_ge(S["dv"], C.get("p2_avd_%d" % s2))
                                for s2 in range(4):
                                    avb = (4, 5, 0, 1)[s2]
                                    ins = e.matmul(ps[avb][:, 0:129],
                                                   pt_s[slp][:, s2 * 128:(s2 + 1) * 128],
                                                   v129[kb][:, lt * 129:(lt + 1) * 129],
                                                   start=(lt == 0), stop=(lt == nlt - 1))
                                ins.then_inc(S["pe"], 1)
                            C.set("p2_pt_use_%d" % slp, tpe4)
                        if c + 1 < P2_NC and nlt < 7:
                            for st_ in range(max(0, nlt - 1), 6):
                                chain_step(c + 1, st_)
                        tpe_av = C.val("pe")
                        for s2 in range(4):
                            avb = (4, 5, 0, 1)[s2]
                            pcol = ps[avb][:, 128:129]
                            pmain = ps[avb][:, 0:128]
                            asem = "at%d" % s2
                            trc = C.bump("dv", 1)
                            if E == "v":
                                e.wait_ge(S["pe"], tpe_av)
                                e.wait_ge(S["ac"], C.get("p2_rc_use_%d" % s2))
                                e.reciprocal(recs[:, s2:s2 + 1], pcol).then_inc(S["dv"], 1)
                            C.set("p2_avd_%d" % s2, trc)
                            tav = C.bump("ac", 1)
                            if E == "s":
                                e.wait_ge(S["dv"], trc)
                                e.wait_ge(S[asem], C.get("p2_att_%d" % s2))
                                e.activation(att_s[s2], pmain, AF.Copy,
                                             scale=recs[:, s2:s2 + 1]).then_inc(S["ac"], 1)
                            C.set("p2_avb_%d" % s2, tav)
                            C.set("p2_rc_use_%d" % s2, tav)
                            C.bump(asem, 16)
                            if E == "y":
                                e.wait_ge(S["ac"], tav)
                                e.dma_start(out=attn_own[c * 512 + s2 * 128: c * 512 + (s2 + 1) * 128,
                                                         lh * 128:(lh + 1) * 128],
                                            in_=att_s[s2]).then_inc(S[asem], 16)
                            C.set("p2_att_%d" % s2, C.val(asem))
                    C.set("p2_q_use_%d" % qb, C.val("pe"))
                C.set("p2_kv_use_%d" % kb, C.val("pe"))

            if P2_CC:
                t_at = [(("at%d" % s2), C.val("at%d" % s2)) for s2 in range(4)]
                C.bump("cc", 1)
                if E == "g":
                    for sm, vv in t_at:
                        if vv > 0:
                            e.wait_ge(S[sm], vv)
                    if SIM_NOCC:
                        e.engine_nop().then_inc(S["cc"], 1)
                    else:
                        e.collective_compute(
                            "AllGather", ALU.bypass,
                            replica_groups=[[0, 1], [2, 3], [4, 5], [6, 7]],
                            ins=[attn_own.ap().opt()],
                            outs=[attn_full.ap().opt()],
                        ).then_inc(S["cc"], 1)
                    if debug:
                        e.dma_start(out=dbg["d_attn"][:, :], in_=attn_full[:, :]).then_inc(S["dbgs"], 16)
                if debug:
                    C.bump("dbgs", 16)

        if emit(5):
            run_phase(p2, "p2")

        def p3a(E, e):
            ab = Alloc(ab_t, AB_N)
            af = Alloc(af_t, AF_N)
            xa = [af.tile(D) for _ in range(2)]
            at = [ab.tile(2048) for _ in range(8)]
            ln_s = [ab.tile(D) for _ in range(2)]
            sq_s = ab.tile(D)
            stat = mk_stats(af, 2)
            t_x2w = [(sm, C.val(sm)) for sm in ("ro0", "ro1")]
            t_cc3 = C.val("cc")
            C.bump("po", 1)
            if E == "g":
                e.wait_ge(S["cc"], t_cc3)
                e.engine_nop().then_inc(S["po"], 1)
            t_rdy = C.val("po")

            dv0 = C.val("dv")

            def add_dv_of(tt):
                return dv0 + 8 * tt + 1

            def emit_at_loads(g, base):
                for tt in range(16):
                    sl = tt % 4
                    if tt >= 4:
                        g.wait_ge(S["dv"], add_dv_of(tt - 4))
                    for r in range(2):
                        g.dma_start(out=at[r * 4 + sl],
                                    in_=attn_full[r * T + base + tt * 128: r * T + base + (tt + 1) * 128, :],
                                    ).then_inc(S["pa%d" % (r * 4 + sl)], 16)

            if E == "g":
                e.wait_ge(S["po"], t_rdy)
                if no_branch:
                    emit_at_loads(e, 0)
                else:
                    with e.register("rid") as rid:
                        e.reg_load(rid, cid_sb[0:1, 0:1])
                        with e.If_eq(rid, 0):
                            emit_at_loads(e, 0)
                        with e.If_ne(rid, 0):
                            emit_at_loads(e, TOK)
            pa_base = [C.val("pa%d" % i) for i in range(8)]
            for tt in range(16):
                for r in range(2):
                    C.bump("pa%d" % (r * 4 + tt % 4), 16)

            for tt in range(16):
                sl = tt % 2
                slat = tt % 4
                asem = "ra%d" % sl
                osem = "ro%d" % sl
                t_x = C.bump(asem, 16)
                if E == "s":
                    if tt < 2:
                        for sm, vv in t_x2w:
                            if vv > 0:
                                e.wait_ge(S[sm], vv)
                    e.wait_ge(S["dv"], C.get("p3_xa_%d" % sl))
                    if C.get("p3_xo_%d" % sl) > 0:
                        e.wait_ge(S[osem], C.get("p3_xo_%d" % sl))
                    e.dma_start(out=xa[sl], in_=x2[tt * 128:(tt + 1) * 128, :]).then_inc(S[asem], 16)
                tadd = C.bump("dv", 1)
                assert tadd == add_dv_of(tt)
                if E == "v":
                    e.wait_ge(S[asem], t_x)
                    for r in range(2):
                        e.wait_ge(S["pa%d" % (r * 4 + slat)],
                                  pa_base[r * 4 + slat] + 16 * (tt // 4 + 1))
                    e.wait_ge(S[osem], C.get("p3_xo_%d" % sl))
                    for r in range(2):
                        for jj in range(2):
                            o0 = (jj * 16 + r * 8) * 128
                            ins = e.tensor_add(xa[sl][:, o0:o0 + 1024],
                                               xa[sl][:, o0:o0 + 1024],
                                               at[r * 4 + slat][:, jj * 1024:(jj + 1) * 1024])
                    ins.then_inc(S["dv"], 1)
                C.bump(osem, 16)
                if E == "s":
                    e.wait_ge(S["dv"], tadd)
                    e.dma_start(out=x3[tt * 128:(tt + 1) * 128, :], in_=xa[sl]).then_inc(S[osem], 16)
                tdv = ln_pipeline(E, e, xa[sl], sq_s, stat[sl], ln_s[sl],
                                  waits_v=[("dv", tadd)], waits_s=[("dv", tadd)],
                                  war_v=[(osem, C.get("p3_xo_%d" % sl))])
                C.set("p3_xa_%d" % sl, tdv)
                C.bump(osem, 16)
                if E == "s":
                    e.wait_ge(S["dv"], tdv)
                    e.dma_start(out=ln4_tm[tt * 128:(tt + 1) * 128, :], in_=ln_s[sl]).then_inc(S[osem], 16)
                C.set("p3_xo_%d" % sl, C.val(osem))
            if debug:
                C.bump("dbgs", 16)
                if E == "g":
                    e.wait_ge(S["ro0"], C.val("ro0"))
                    e.wait_ge(S["ro1"], C.val("ro1"))
                    e.dma_start(out=dbg["d_x3"][:, :], in_=x3[:, :]).then_inc(S["dbgs"], 16)

        if emit(6):
            run_phase(p3a, "p3a")

        if emit(7):
            run_phase(lambda E, e: ffn_pass(E, e, ln4_tm), "p3b")

        if emit(8):
            run_phase(lambda E, e: resid_ln(E, e, x3, out_t, None, True, ("ro0", "ro1")), "p3c")

    return nc



def _shard_inputs(inp):
    X = np.asarray(inp["L0_attn_a_hidden_states"], np.float32)
    Wqa = np.asarray(inp["L0_attn_a_Wq"], np.float32).reshape(32, 128, D)
    bqa_full = np.asarray(inp["L0_attn_a_bq"], np.float32).reshape(32, 128)
    M = np.asarray(inp["L0_attn_a_M"], np.float32)
    Z = np.asarray(inp["L0_attn_a_z"], np.float32)
    Wo = np.asarray(inp["L0_attn_a_Wo"], np.float32)
    Bs = np.asarray(inp["L0_ffn_bases"], np.int8)
    Al = np.asarray(inp["L0_ffn_alphas"], np.float32)[:, :, 0]
    Wqb = np.asarray(inp["L0_attn_b_Wq"], np.float32).reshape(32, 128, D)
    bqb_full = np.asarray(inp["L0_attn_b_bq"], np.float32).reshape(32, 128)
    Wk = np.asarray(inp["L0_attn_b_Wk"], np.float32).reshape(16, 128, D)
    bk_full = np.asarray(inp["L0_attn_b_bk"], np.float32).reshape(16, 128)
    Wv = np.asarray(inp["L0_attn_b_Wv"], np.float32).reshape(16, 128, D)
    bv_full = np.asarray(inp["L0_attn_b_bv"], np.float32).reshape(16, 128)

    sel = list(range(0, 32, 4))
    wqa_s = np.ascontiguousarray(Wqa[sel].reshape(HA * 128, D))
    bqa_s = np.ascontiguousarray(bqa_full[sel].T)
    z_s = np.ascontiguousarray(Z.T)

    ident = np.eye(128, dtype=np.float32)
    nsr = np.full((1, 128), -SCALE, np.float32)
    msk = np.zeros((4, 128, 512), np.float32)
    for s in range(4):
        for l in range(128):
            msk[s, l, s * 128 + l:] = 1.0

    in_maps = []
    for c in range(8):
        p, half = c // 2, c % 2
        x_own = np.ascontiguousarray(X[p, half * TOK:(half + 1) * TOK, :])
        kvs = [half * 8 + k for k in range(8)]
        qhs = [j * 16 + half * 8 + k for j in range(2) for k in range(8)]
        ad = np.zeros((4, 4, 128, 128), np.float32)
        for k in range(4):
            for ob in range(4):
                np.fill_diagonal(ad[k, ob],
                                 Al[k, 512 * c + 128 * ob: 512 * c + 128 * (ob + 1)])
        in_maps.append({
            "x0": x_own,
            "wqa": wqa_s,
            "bqa": bqa_s,
            "m_a": M,
            "z_a": z_s,
            "wo": Wo,
            "bases": np.ascontiguousarray(Bs[:, 512 * c:512 * (c + 1), :]),
            "adiag": ad,
            "wqb": np.ascontiguousarray(Wqb[qhs].reshape(16 * 128, D)),
            "bqb_r": np.ascontiguousarray(bqb_full[qhs].reshape(1, 16 * 128)),
            "wk": np.ascontiguousarray(Wk[kvs].reshape(KVH * 128, D)),
            "bk_r": np.ascontiguousarray(bk_full[kvs].reshape(1, KVH * 128)),
            "wv": np.ascontiguousarray(Wv[kvs].reshape(KVH * 128, D)),
            "bvr": np.ascontiguousarray(bv_full[kvs].reshape(1, KVH * 128)),
            "identi": ident,
            "nsrow": nsr,
            "masks": msk,
            "cid": np.array([[half]], np.int32),
        })
    return in_maps


def kernel(**inputs):
    nc = build_nc(debug=False)
    in_maps = _shard_inputs(inputs)
    res = run_bass_kernel_spmd(nc, in_maps, core_ids=list(range(8)))
    X = np.asarray(inputs["L0_attn_a_hidden_states"], np.float32)
    outp = np.empty_like(X)
    for c in range(8):
        p, half = c // 2, c % 2
        outp[p, half * TOK:(half + 1) * TOK, :] = np.asarray(res.results[c]["out"], np.float32)
    return outp


# version 03
# speedup vs baseline: 1.2988x; 1.2988x over previous
import sys
for _p in ("/opt/trn_rl_repo",):
    if _p not in sys.path:
        sys.path.insert(0, _p)
import math
import os as _os
from contextlib import ExitStack

import numpy as np

import concourse.bass as bass
import concourse.mybir as mybir
from concourse.bass_utils import run_bass_kernel_spmd

F32 = mybir.dt.float32
BF16 = mybir.dt.bfloat16
I8 = mybir.dt.int8
I32 = mybir.dt.int32
AF = mybir.ActivationFunctionType
ALU = mybir.AluOpType
AX = mybir.AxisListType

TOK = 2048
D = 4096
T = 4096
NI = 32
SCALE = 1.0 / math.sqrt(32.0)
LN_EPS = 1e-5
DEN_EPS = 1e-6
HA = 8
KVH = 8

P2_KK = int(_os.environ.get("P2_KK", "8"))
P2_NC = int(_os.environ.get("P2_NC", "8"))
P2_C0 = int(_os.environ.get("P2_C0", "0"))
P2_CC = int(_os.environ.get("P2_CC", "1"))
P0_CC = int(_os.environ.get("P0_CC", "1"))
SIM_NOCC = int(_os.environ.get("SIM_NOCC", "0"))


class Counters:
    def __init__(self):
        self.c = {}

    def bump(self, name, n):
        self.c[name] = self.c.get(name, 0) + n
        return self.c[name]

    def val(self, name):
        return self.c.get(name, 0)

    def get(self, name):
        return self.c.get(name, 0)

    def set(self, name, v):
        self.c[name] = v

    def snap(self):
        return dict(self.c)

    def restore(self, s):
        self.c = dict(s)


class Alloc:
    def __init__(self, tensor, n):
        self.t = tensor
        self.n = n
        self.off = 0

    def tile(self, n):
        assert self.off + n <= self.n, f"arena overflow {self.off}+{n}>{self.n}"
        o = self.off
        self.off += n
        return self.t[:, o:o + n]


def build_nc(debug=False, no_branch=False, upto=99, only=None):
    nc = bass.Bass("TRN2", target_bir_lowering=False)

    def emit(i):
        return (upto >= i) if only is None else (i in only)

    x0 = nc.dram_tensor("x0", [TOK, D], F32, kind="ExternalInput")
    wqa = nc.dram_tensor("wqa", [HA * 128, D], F32, kind="ExternalInput")
    bqa = nc.dram_tensor("bqa", [128, HA], F32, kind="ExternalInput")
    m_a = nc.dram_tensor("m_a", [HA, 128, 128], F32, kind="ExternalInput")
    z_a = nc.dram_tensor("z_a", [128, HA], F32, kind="ExternalInput")
    wo = nc.dram_tensor("wo", [D, D], F32, kind="ExternalInput")
    bases = nc.dram_tensor("bases", [4, 512, D], I8, kind="ExternalInput")
    adiag = nc.dram_tensor("adiag", [4, 4, 128, 128], F32, kind="ExternalInput")
    wqb = nc.dram_tensor("wqb", [16 * 128, D], F32, kind="ExternalInput")
    bqb_r = nc.dram_tensor("bqb_r", [1, 16 * 128], F32, kind="ExternalInput")
    wk = nc.dram_tensor("wk", [KVH * 128, D], F32, kind="ExternalInput")
    bk_r = nc.dram_tensor("bk_r", [1, KVH * 128], F32, kind="ExternalInput")
    wv = nc.dram_tensor("wv", [KVH * 128, D], F32, kind="ExternalInput")
    bvr = nc.dram_tensor("bvr", [1, KVH * 128], F32, kind="ExternalInput")
    identi = nc.dram_tensor("identi", [128, 128], F32, kind="ExternalInput")
    nsrow = nc.dram_tensor("nsrow", [1, 128], F32, kind="ExternalInput")
    masks = nc.dram_tensor("masks", [4, 128, 512], F32, kind="ExternalInput")
    cid = nc.dram_tensor("cid", [1, 1], I32, kind="ExternalInput")

    out_t = nc.dram_tensor("out", [TOK, D], F32, kind="ExternalOutput")

    wqa_bf = nc.dram_tensor("wqa_bf", [HA * 128, D], BF16)
    wof = nc.dram_tensor("wof", [D, HA * 128], F32)
    wo_bf = nc.dram_tensor("wo_bf", [D, HA * 128], BF16)
    wqb_bf = nc.dram_tensor("wqb_bf", [16 * 128, D], BF16)
    wk_bf = nc.dram_tensor("wk_bf", [KVH * 128, D], BF16)
    wv_bf = nc.dram_tensor("wv_bf", [KVH * 128, D], BF16)
    wq_own = nc.dram_tensor("wq_own", [512, D], BF16)
    wq_all = nc.dram_tensor("wq_all", [D, D], BF16, addr_space="Shared")
    x1 = nc.dram_tensor("x1", [TOK, D], F32)
    x2 = nc.dram_tensor("x2", [TOK, D], F32)
    x3 = nc.dram_tensor("x3", [TOK, D], F32)
    ln1_tm = nc.dram_tensor("ln1_tm", [TOK, D], BF16)
    ln2_tm = nc.dram_tensor("ln2_tm", [TOK, D], BF16)
    ln4_tm = nc.dram_tensor("ln4_tm", [TOK, D], BF16)
    ln3_own = nc.dram_tensor("ln3_own", [TOK, D], BF16)
    ln3_full = nc.dram_tensor("ln3_full", [T, D], BF16)
    qT_d = nc.dram_tensor("qT_d", [16 * 128, T], BF16)
    kT_d = nc.dram_tensor("kT_d", [KVH * 128, T], BF16)
    v_d = nc.dram_tensor("v_d", [T, KVH * 128], BF16)
    attn_own = nc.dram_tensor("attn_own", [T, 2048], BF16)
    attn_full = nc.dram_tensor("attn_full", [2 * T, 2048], BF16)
    ffn_fm = nc.dram_tensor("ffn_fm", [D, TOK], BF16)

    dbg = {}
    if debug:
        for nm, shp, dt in [
            ("d_x1", [TOK, D], F32), ("d_x2", [TOK, D], F32),
            ("d_ln3", [T, D], BF16), ("d_qT", [16 * 128, T], BF16),
            ("d_kT", [KVH * 128, T], BF16), ("d_v", [T, KVH * 128], BF16),
            ("d_attn", [2 * T, 2048], BF16), ("d_x3", [TOK, D], F32),
            ("d_wq", [D, D], BF16),
        ]:
            dbg[nm] = nc.dram_tensor(nm, shp, dt, kind="ExternalOutput")

    C = Counters()

    with ExitStack() as ctx:
        en = ctx.enter_context
        AB_N = 67 * 1024
        AF_N = 14 * 1024
        ab_t = en(nc.sbuf_tensor("ab", [128, AB_N], BF16))
        af_t = en(nc.sbuf_tensor("af", [128, AF_N], F32))

        bq_sb = en(nc.sbuf_tensor("bq_sb", [1, 16 * 128], BF16))
        bk_sb = en(nc.sbuf_tensor("bk_sb", [1, KVH * 128], BF16))
        ones512 = en(nc.sbuf_tensor("ones512", [1, 512], BF16))
        bqa_sb = en(nc.sbuf_tensor("bqa_sb", [128, HA], F32))
        m_sb = en(nc.sbuf_tensor("m_sb", [128, HA * 128], BF16))
        z_sb = en(nc.sbuf_tensor("z_sb", [128, HA], BF16))
        ident_sb = en(nc.sbuf_tensor("ident_sb", [128, 128], BF16))
        ones_sb = en(nc.sbuf_tensor("ones_sb", [1, 128], BF16))
        nsrow_sb = en(nc.sbuf_tensor("nsrow_sb", [1, 128], BF16))
        bvr_sb = en(nc.sbuf_tensor("bvr_sb", [1, KVH * 128], BF16))
        mask_sb = en(nc.sbuf_tensor("mask_sb", [128, 4 * 512], BF16))
        cid_sb = en(nc.sbuf_tensor("cid_sb", [1, 1], I32))

        ps = [en(nc.psum_tensor(f"ps{i}", [128, 512], F32)) for i in range(8)]

        base_sems = ["pre", "preg", "wt", "wt2", "xlt", "cc",
                     "pe", "dv", "ac", "po", "ev", "q2", "k2",
                     "mx", "wq1", "wv2", "fin", "dbgs"]
        slot_sems = ["wta", "wtb", "fwa", "fwb", "w2a", "w2b",
                     "k2a", "k2b", "q2a", "q2b",
                     "wqs0", "wqs1",
                     "xl0", "xl1",
                     "lo0", "lo1",
                     "xr0", "xr1",
                     "x1t",
                     "ev0", "ev1", "ev2", "ev3",
                     "ra0", "ra1",
                     "rf0", "rf1",
                     "ro0", "ro1",
                     "q0", "q1", "q2s", "q3", "q4", "q5", "q6", "q7",
                     "at0", "at1", "at2", "at3",
                     "pa0", "pa1", "pa2", "pa3", "pa4", "pa5", "pa6", "pa7",
                     ]
        S = {n: en(nc.semaphore(f"s_{n}")) for n in base_sems + slot_sems}

        def W(E, e, eng, *pairs):
            if E == eng:
                for sm, vv in pairs:
                    if vv > 0:
                        e.wait_ge(S[sm], vv)

        def ln_pipeline(E, e, xf, sq, st_, ln_out, waits_v=(), waits_s=(), war_v=()):
            mean, var, rstd, nmr = st_["mean"], st_["var"], st_["rstd"], st_["nmr"]
            nmean = st_["nmean"]
            tac = C.bump("ac", 1)
            if E == "s":
                for sm, vv in waits_s:
                    if vv > 0:
                        e.wait_ge(S[sm], vv)
                e.activation(sq, xf, AF.Square, accum_out=var).then_inc(S["ac"], 1)
            t1 = C.bump("dv", 1)
            if E == "v":
                for sm, vv in list(waits_v) + list(war_v):
                    if vv > 0:
                        e.wait_ge(S[sm], vv)
                e.reduce_sum(mean, xf, axis=AX.X).then_inc(S["dv"], 1)
            t2 = C.bump("dv", 1)
            if E == "v":
                e.wait_ge(S["dv"], t1)
                e.tensor_scalar(mean, mean, 1.0 / D, 0.0, ALU.mult, ALU.add).then_inc(S["dv"], 1)
            t3 = C.bump("dv", 1)
            if E == "v":
                e.wait_ge(S["dv"], t2)
                e.tensor_scalar(nmean, mean, -1.0, 0.0, ALU.mult, ALU.add)
                e.tensor_mul(nmr, mean, mean).then_inc(S["dv"], 1)
            t4 = C.bump("dv", 1)
            if E == "v":
                e.wait_ge(S["ac"], tac)
                e.wait_ge(S["dv"], t3)
                e.tensor_scalar(var, var, 1.0 / D, LN_EPS, ALU.mult, ALU.add).then_inc(S["dv"], 1)
            t5 = C.bump("dv", 1)
            if E == "v":
                e.wait_ge(S["dv"], t4)
                e.tensor_sub(var, var, nmr).then_inc(S["dv"], 1)
            t6 = C.bump("dv", 1)
            if E == "v":
                e.wait_ge(S["dv"], t5)
                e.reciprocal(rstd, var).then_inc(S["dv"], 1)
            tac2 = C.bump("ac", 1)
            if E == "s":
                e.wait_ge(S["dv"], t6)
                e.activation(rstd, rstd, AF.Sqrt).then_inc(S["ac"], 1)
            tac3 = C.bump("ac", 1)
            if E == "s":
                e.wait_ge(S["ac"], tac2)
                e.activation(nmr, rstd, AF.Copy, scale=nmean[:, 0:1]).then_inc(S["ac"], 1)
            tdv = C.bump("dv", 1)
            if E == "v":
                e.wait_ge(S["ac"], tac3)
                e.tensor_scalar(ln_out, xf, rstd[:, 0:1], nmr[:, 0:1],
                                ALU.mult, ALU.add).then_inc(S["dv"], 1)
            return tdv

        def mk_stats(af, n):
            out = []
            for _ in range(n):
                sc = af.tile(8)
                out.append({"mean": sc[:, 0:1], "var": sc[:, 1:2],
                            "rstd": sc[:, 2:3], "nmr": sc[:, 3:4],
                            "nmean": sc[:, 4:5]})
            return out

        def run_phase(fn, name):
            snap = C.snap()
            ends = {}
            with nc.Block(name) as blk:
                def mk(E):
                    def f(e):
                        C.restore(snap)
                        fn(E, e)
                        ends[E] = C.snap()
                    return f
                blk.tensor(mk("t"))
                blk.vector(mk("v"))
                blk.scalar(mk("s"))
                blk.gpsimd(mk("g"))
                blk.sync(mk("y"))
            vals = list(ends.values())
            for v_ in vals[1:]:
                assert v_ == vals[0], f"counter divergence in phase {name}"
            C.restore(vals[0])

        def p0(E, e):
            ab = Alloc(ab_t, AB_N)
            n_small = 11
            if E == "g":
                e.dma_start(out=ident_sb[:, :], in_=identi[:, :]).then_inc(S["pre"], 16)
                e.dma_start(out=nsrow_sb[0:1, :], in_=nsrow[0:1, :]).then_inc(S["pre"], 16)
                e.dma_start(out=bvr_sb[0:1, :], in_=bvr[0:1, :]).then_inc(S["pre"], 16)
                e.dma_start(out=m_sb[:, :].rearrange("d (h e) -> d h e", e=128),
                            in_=m_a.ap().rearrange("h d e -> d h e")).then_inc(S["pre"], 16)
                e.dma_start(out=z_sb[:, :], in_=z_a[:, :]).then_inc(S["pre"], 16)
                e.dma_start(out=bq_sb[0:1, :], in_=bqb_r[0:1, :]).then_inc(S["pre"], 16)
                e.dma_start(out=bk_sb[0:1, :], in_=bk_r[0:1, :]).then_inc(S["pre"], 16)
                e.dma_start(out=bqa_sb[:, :], in_=bqa[:, :]).then_inc(S["pre"], 16)
                e.dma_start(out=mask_sb[:, :].rearrange("p (s q) -> p s q", q=512),
                            in_=masks.ap().rearrange("s p q -> p s q")).then_inc(S["pre"], 16)
                e.dma_start(out=cid_sb[:, :], in_=cid[:, :]).then_inc(S["pre"], 16)
                e.dma_start(out=wqa_bf[:, :], in_=wqa[:, :]).then_inc(S["pre"], 16)
            C.bump("pre", 16 * n_small)
            C.bump("po", 1)
            if E == "g":
                e.memset(ones_sb[0:1, :], 1.0)
                e.memset(ones512[0:1, :], 1.0).then_inc(S["po"], 1)

            bas, adg = [], []
            for k in range(4):
                rb, ra = [], []
                for ob in range(4):
                    tb = ab.tile(D)
                    rb.append(tb)
                    C.bump("preg", 16)
                    if E == "g":
                        e.dma_start(out=tb, in_=bases[k, ob * 128:(ob + 1) * 128, :]).then_inc(S["preg"], 16)
                    ta = ab.tile(128)
                    ra.append(ta)
                    C.bump("preg", 16)
                    if E == "g":
                        e.dma_start(out=ta, in_=adiag[k, ob, :, :]).then_inc(S["preg"], 16)
                bas.append(rb)
                adg.append(ra)
            t_bases = C.val("preg")

            for jj in range(4):
                for h in range(HA):
                    C.bump("pre", 16)
                    if E == "g":
                        src = wo[:, h * 512 + jj * 128: h * 512 + (jj + 1) * 128]
                        if jj == 0:
                            e.dma_start(out=wof[:, h * 128:(h + 1) * 128], in_=src).then_inc(S["pre"], 16)
                        else:
                            e.dma_start(out=wof[:, h * 128:(h + 1) * 128], in_=src,
                                        accum_op=ALU.add).then_inc(S["pre"], 16)
                if E == "g":
                    e.wait_ge(S["pre"], C.val("pre"))
            C.bump("pre", 16)
            if E == "g":
                e.dma_start(out=wo_bf[:, :], in_=wof[:, :]).then_inc(S["pre"], 16)

            wq_stage = [ab.tile(512) for _ in range(2)]
            for ob in range(4):
                for ic in range(8):
                    gi = ob * 8 + ic
                    bank = gi % 4
                    slot = gi % 2
                    ssem = "wqs%d" % slot
                    tpe = C.bump("pe", 1)
                    if E == "t":
                        if gi == 0:
                            e.wait_ge(S["preg"], t_bases)
                        e.wait_ge(S["ev"], C.get("wqev_%d" % bank))
                        for k in range(4):
                            ins = e.matmul(ps[bank][:, 0:512], adg[k][ob],
                                           bas[k][ob][:, ic * 512:(ic + 1) * 512],
                                           start=(k == 0), stop=(k == 3))
                        ins.then_inc(S["pe"], 1)
                    tev = C.bump("ev", 1)
                    if E == "v":
                        e.wait_ge(S["pe"], tpe)
                        e.wait_ge(S[ssem], C.val(ssem))
                        e.tensor_copy(wq_stage[slot], ps[bank][:, 0:512]).then_inc(S["ev"], 1)
                    C.set("wqev_%d" % bank, tev)
                    C.bump(ssem, 16)
                    if E == "y":
                        e.wait_ge(S["ev"], tev)
                        e.dma_start(out=wq_own[ob * 128:(ob + 1) * 128, ic * 512:(ic + 1) * 512],
                                    in_=wq_stage[slot]).then_inc(S[ssem], 16)
            t_wq0, t_wq1 = C.val("wqs0"), C.val("wqs1")

            C.bump("cc", 1)
            if E == "g":
                e.wait_ge(S["wqs0"], t_wq0)
                e.wait_ge(S["wqs1"], t_wq1)
                if P0_CC:
                    e.collective_compute(
                        "AllGather", ALU.bypass,
                        replica_groups=[[0, 1, 2, 3, 4, 5, 6, 7]],
                        ins=[wq_own.ap().opt()],
                        outs=[wq_all.ap().opt()],
                    ).then_inc(S["cc"], 1)
                else:
                    e.engine_nop().then_inc(S["cc"], 1)
                e.dma_start(out=wqb_bf[:, :], in_=wqb[:, :]).then_inc(S["wq1"], 16)
                e.dma_start(out=wk_bf[:, :], in_=wk[:, :]).then_inc(S["wq1"], 16)
                e.dma_start(out=wv_bf[:, :], in_=wv[:, :]).then_inc(S["wq1"], 16)
            C.bump("wq1", 48)

        if emit(0):
            run_phase(p0, "p0")

        def p1a(E, e):
            ab = Alloc(ab_t, AB_N)
            af = Alloc(af_t, AF_N)
            CH = 256
            NCH = TOK // CH
            wqaT = [[ab.tile(512) for _ in range(NI)] for _ in range(2)]
            for hg in range(2):
                for i in range(NI):
                    C.bump("wt", 16)
                    if E == "y":
                        if hg == 0 and i == 0:
                            e.wait_ge(S["pre"], C.val("pre"))
                        e.dma_start(out=wqaT[hg][i],
                                    in_=wqa_bf[hg * 512:(hg + 1) * 512, i * 128:(i + 1) * 128],
                                    transpose=True).then_inc(S["wt"], 16)
            t_wqaT = C.val("wt")

            x_ln = [af.tile(D) for _ in range(2)]
            stat = mk_stats(af, 2)
            stat2 = mk_stats(af, 2)
            sq_s = ab.tile(D)
            ln_s = [ab.tile(D) for _ in range(2)]
            xlt = [ab.tile(CH) for _ in range(NI)]
            woT = [[ab.tile(512) for _ in range(HA)] for _ in range(2)]
            sig = [ab.tile(CH) for _ in range(2)]
            retr = [ab.tile(CH) for _ in range(HA)]
            rec_bf = ab.tile(CH)
            bc_sb = ab.tile(CH)
            elu_t = af.tile(CH)
            elu_t2 = af.tile(CH)
            x0r = [af.tile(512) for _ in range(2)]
            x1t = af.tile(D)

            def ln_store(gt, sl, dst, tdv):
                ssem = "lo%d" % sl
                C.bump(ssem, 16)
                if E == "s":
                    e.wait_ge(S["dv"], tdv)
                    e.dma_start(out=dst[gt * 128:(gt + 1) * 128, :], in_=ln_s[sl]).then_inc(S[ssem], 16)

            for ch in range(NCH):
                for tt in range(2):
                    gt = ch * 2 + tt
                    sl = gt % 2
                    xsem = "xl%d" % sl
                    t_x = C.bump(xsem, 16)
                    if E == "s":
                        e.wait_ge(S["dv"], C.get("xlnuse_%d" % sl))
                        e.dma_start(out=x_ln[sl], in_=x0[gt * 128:(gt + 1) * 128, :]).then_inc(S[xsem], 16)
                    tdv = ln_pipeline(E, e, x_ln[sl], sq_s, stat[sl], ln_s[sl],
                                      waits_v=[(xsem, t_x)], waits_s=[(xsem, t_x)],
                                      war_v=[("lo%d" % sl, C.get("lnslot_%d" % sl))])
                    C.set("xlnuse_%d" % sl, tdv)
                    ln_store(gt, sl, ln1_tm, tdv)
                    C.set("lnslot_%d" % sl, C.val("lo%d" % sl))

                t_lo0, t_lo1 = C.val("lo0"), C.val("lo1")
                for i in range(NI):
                    C.bump("xlt", 16)
                    if E == "y":
                        if i == 0:
                            e.wait_ge(S["lo0"], t_lo0)
                            e.wait_ge(S["lo1"], t_lo1)
                            e.wait_ge(S["pe"], C.get("xlt_use"))
                            if ch == 0:
                                e.wait_ge(S["wt"], t_wqaT)
                        e.dma_start(out=xlt[i],
                                    in_=ln1_tm[ch * CH:(ch + 1) * CH, i * 128:(i + 1) * 128],
                                    transpose=True).then_inc(S["xlt"], 16)
                t_xlt = C.val("xlt")

                for h in range(HA):
                    hg, hh = h // 4, h % 4
                    bank = h % 2
                    tpe = C.bump("pe", 1)
                    if E == "t":
                        if h == 0:
                            e.wait_ge(S["xlt"], t_xlt)
                        e.wait_ge(S["dv"], C.get("sigdone_%d" % bank))
                        for i in range(NI):
                            ins = e.matmul(ps[bank][:, 0:CH], wqaT[hg][i][:, hh * 128:(hh + 1) * 128],
                                           xlt[i], start=(i == 0), stop=(i == NI - 1))
                        ins.then_inc(S["pe"], 1)
                    C.bump("dv", 1)
                    if E == "v":
                        e.wait_ge(S["pe"], tpe)
                        e.tensor_scalar(elu_t[:, 0:CH], ps[bank][:, 0:CH], bqa_sb[:, h:h + 1], 0.0,
                                        ALU.add, ALU.min).then_inc(S["dv"], 1)
                    tac = C.bump("ac", 1)
                    if E == "s":
                        e.wait_ge(S["dv"], C.val("dv"))
                        e.activation(elu_t2[:, 0:CH], elu_t[:, 0:CH], AF.Exp).then_inc(S["ac"], 1)
                    tsg0 = C.bump("dv", 1)
                    if E == "v":
                        e.wait_ge(S["ac"], tac)
                        e.tensor_scalar(elu_t[:, 0:CH], ps[bank][:, 0:CH], bqa_sb[:, h:h + 1], 0.0,
                                        ALU.add, ALU.max).then_inc(S["dv"], 1)
                    tsig = C.bump("dv", 1)
                    if E == "v":
                        e.wait_ge(S["dv"], tsg0)
                        e.tensor_add(sig[bank], elu_t[:, 0:CH], elu_t2[:, 0:CH]).then_inc(S["dv"], 1)
                    C.set("sigdone_%d" % bank, tsig)

                    tpe2 = C.bump("pe", 1)
                    if E == "t":
                        e.wait_ge(S["dv"], tsig)
                        ins = e.matmul(ps[6][0:1, 0:CH], z_sb[:, h:h + 1], sig[bank],
                                       start=True, stop=True)
                        ins.then_inc(S["pe"], 1)
                    td0 = C.bump("dv", 1)
                    if E == "v":
                        e.wait_ge(S["pe"], tpe2)
                        e.tensor_scalar(elu_t[0:1, 0:CH], ps[6][0:1, 0:CH], 1.0, DEN_EPS,
                                        ALU.mult, ALU.add).then_inc(S["dv"], 1)
                    td1 = C.bump("dv", 1)
                    if E == "v":
                        e.wait_ge(S["dv"], td0)
                        e.reciprocal(elu_t2[0:1, 0:CH], elu_t[0:1, 0:CH]).then_inc(S["dv"], 1)
                    td2 = C.bump("dv", 1)
                    if E == "v":
                        e.wait_ge(S["dv"], td1)
                        e.tensor_copy(rec_bf[0:1, 0:CH], elu_t2[0:1, 0:CH]).then_inc(S["dv"], 1)
                    tpe3 = C.bump("pe", 1)
                    if E == "t":
                        e.wait_ge(S["dv"], td2)
                        e.wait_ge(S["ac"], C.get("bc_use"))
                        ins = e.matmul(ps[7][:, 0:CH], ones_sb[0:1, :], rec_bf[0:1, 0:CH],
                                       start=True, stop=True)
                        ins.then_inc(S["pe"], 1)
                    tac2 = C.bump("ac", 1)
                    if E == "s":
                        e.wait_ge(S["pe"], tpe3)
                        e.wait_ge(S["dv"], C.get("bcs_use"))
                        e.activation(bc_sb, ps[7][:, 0:CH], AF.Copy).then_inc(S["ac"], 1)
                    C.set("bc_use", tac2)
                    tpe4 = C.bump("pe", 1)
                    if E == "t":
                        e.wait_ge(S["dv"], C.get("retr_rd_%d" % bank))
                        ins = e.matmul(ps[bank + 2][:, 0:CH], m_sb[:, h * 128:(h + 1) * 128],
                                       sig[bank], start=True, stop=True)
                        ins.then_inc(S["pe"], 1)
                    tretr = C.bump("dv", 1)
                    if E == "v":
                        e.wait_ge(S["pe"], tpe4)
                        e.wait_ge(S["ac"], tac2)
                        e.wait_ge(S["pe"], C.get("woproj_done"))
                        e.tensor_mul(retr[h], ps[bank + 2][:, 0:CH], bc_sb).then_inc(S["dv"], 1)
                    C.set("retr_rd_%d" % bank, tretr)
                    C.set("bcs_use", tretr)
                    C.set("retr_done", tretr)
                C.set("xlt_use", C.val("pe"))

                for oc in range(HA):
                    buf = oc % 2
                    bsem = "w2a" if buf == 0 else "w2b"
                    for h in range(HA):
                        C.bump(bsem, 16)
                        if E == "y":
                            if h == 0:
                                e.wait_ge(S["pe"], C.get("woTuse_%d" % buf))
                            e.dma_start(out=woT[buf][h],
                                        in_=wo_bf[oc * 512:(oc + 1) * 512, h * 128:(h + 1) * 128],
                                        transpose=True).then_inc(S[bsem], 16)
                    t_wo = C.val(bsem)
                    for ts2 in range(2):
                        gt = ch * 2 + ts2
                        bank = 4 + (ts2 % 2)
                        tpe = C.bump("pe", 1)
                        if E == "t":
                            if ts2 == 0:
                                e.wait_ge(S[bsem], t_wo)
                                if oc == 0:
                                    e.wait_ge(S["dv"], C.get("retr_done"))
                            e.wait_ge(S["dv"], C.get("woev_%d" % bank))
                            for h in range(HA):
                                ins = e.matmul(ps[bank][:, 0:512], retr[h][:, ts2 * 128:(ts2 + 1) * 128],
                                               woT[buf][h], start=(h == 0), stop=(h == HA - 1))
                            ins.then_inc(S["pe"], 1)
                        if ts2 == 1:
                            C.set("woTuse_%d" % buf, tpe)
                        sl = ts2 % 2
                        rsem = "xr%d" % sl
                        t_prev = C.val(rsem)
                        t_x0 = C.bump(rsem, 16)
                        if E == "s":
                            e.wait_ge(S["dv"], C.get("x0r_use_%d" % sl))
                            e.wait_ge(S[rsem], t_prev)
                            e.dma_start(out=x0r[sl], in_=x0[gt * 128:(gt + 1) * 128,
                                                           oc * 512:(oc + 1) * 512]).then_inc(S[rsem], 16)
                        tadd = C.bump("dv", 1)
                        if E == "v":
                            e.wait_ge(S["pe"], tpe)
                            e.wait_ge(S[rsem], t_x0)
                            e.tensor_add(x0r[sl], ps[bank][:, 0:512], x0r[sl]).then_inc(S["dv"], 1)
                        C.set("x0r_use_%d" % sl, tadd)
                        C.set("woev_%d" % bank, tadd)
                        C.bump(rsem, 16)
                        if E == "s":
                            e.wait_ge(S["dv"], tadd)
                            e.dma_start(out=x1[gt * 128:(gt + 1) * 128, oc * 512:(oc + 1) * 512],
                                        in_=x0r[sl]).then_inc(S[rsem], 16)
                C.set("woproj_done", C.val("pe"))

                for tt in range(2):
                    gt = ch * 2 + tt
                    t_xr0, t_xr1 = C.val("xr0"), C.val("xr1")
                    t_x1 = C.bump("x1t", 16)
                    if E == "s":
                        e.wait_ge(S["xr0"], t_xr0)
                        e.wait_ge(S["xr1"], t_xr1)
                        e.wait_ge(S["dv"], C.get("x1t_use"))
                        e.dma_start(out=x1t, in_=x1[gt * 128:(gt + 1) * 128, :]).then_inc(S["x1t"], 16)
                    sl = gt % 2
                    tdv = ln_pipeline(E, e, x1t, sq_s, stat2[sl], ln_s[sl],
                                      waits_v=[("x1t", t_x1)], waits_s=[("x1t", t_x1)],
                                      war_v=[("lo%d" % sl, C.get("lnslot_%d" % sl))])
                    C.set("x1t_use", tdv)
                    ln_store(gt, sl, ln2_tm, tdv)
                    C.set("lnslot_%d" % sl, C.val("lo%d" % sl))

            if debug:
                C.bump("dbgs", 16)
                if E == "g":
                    e.wait_ge(S["xr0"], C.val("xr0"))
                    e.wait_ge(S["xr1"], C.val("xr1"))
                    e.dma_start(out=dbg["d_x1"][:, :], in_=x1[:, :]).then_inc(S["dbgs"], 16)

        if emit(1):
            run_phase(p1a, "p1a")

        def ffn_pass(E, e, lnsrc):
            ab = Alloc(ab_t, AB_N)
            xlt2 = [ab.tile(1024) for _ in range(NI)]
            wtiles = [[ab.tile(512) for _ in range(NI)] for _ in range(2)]
            evst = [ab.tile(512) for _ in range(4)]
            t_srcs = [(sm, C.val(sm)) for sm in ("lo0", "lo1", "ro0", "ro1")]
            for th in range(2):
                for i in range(NI):
                    C.bump("xlt", 16)
                    if E == "y":
                        if i == 0:
                            if th == 0:
                                for sm, vv in t_srcs:
                                    if vv > 0:
                                        e.wait_ge(S[sm], vv)
                            e.wait_ge(S["pe"], C.get("fx_use"))
                        e.dma_start(out=xlt2[i],
                                    in_=lnsrc[th * 1024:(th + 1) * 1024, i * 128:(i + 1) * 128],
                                    transpose=True).then_inc(S["xlt"], 16)
                t_xl = C.val("xlt")
                for obg in range(8):
                    buf = obg % 2
                    bsem = "fwa" if buf == 0 else "fwb"
                    for i in range(NI):
                        C.bump(bsem, 16)
                        if E == "y":
                            if i == 0:
                                e.wait_ge(S["pe"], C.get("fw_use_%d" % buf))
                                if th == 0 and obg == 0:
                                    e.wait_ge(S["cc"], 1)
                            e.dma_start(out=wtiles[buf][i],
                                        in_=wq_all[obg * 512:(obg + 1) * 512, i * 128:(i + 1) * 128],
                                        transpose=True).then_inc(S[bsem], 16)
                    t_w = C.val(bsem)
                    for ob4 in range(4):
                        ob = obg * 4 + ob4
                        bank = (ob4 % 2) * 2
                        tpe = C.bump("pe", 1)
                        if E == "t":
                            if ob4 == 0:
                                e.wait_ge(S[bsem], t_w)
                                e.wait_ge(S["xlt"], t_xl)
                            e.wait_ge(S["ev"], C.get("fev_%d" % bank))
                            for i in range(NI):
                                for tc in range(2):
                                    ins = e.matmul(ps[bank + tc][:, 0:512],
                                                   wtiles[buf][i][:, ob4 * 128:(ob4 + 1) * 128],
                                                   xlt2[i][:, tc * 512:(tc + 1) * 512],
                                                   start=(i == 0), stop=(i == NI - 1))
                            ins.then_inc(S["pe"], 1)
                        if ob4 == 3:
                            C.set("fw_use_%d" % buf, tpe)
                            if obg == 7:
                                C.set("fx_use", tpe)
                        for tc in range(2):
                            sl = (ob4 % 2) * 2 + tc
                            ssem = "ev%d" % sl
                            tev = C.bump("ev", 1)
                            if E == "v":
                                e.wait_ge(S["pe"], tpe)
                                e.wait_ge(S[ssem], C.val(ssem))
                                e.tensor_copy(evst[sl], ps[bank + tc][:, 0:512]).then_inc(S["ev"], 1)
                            C.bump(ssem, 16)
                            if E == "s":
                                e.wait_ge(S["ev"], tev)
                                e.dma_start(out=ffn_fm[ob * 128:(ob + 1) * 128,
                                                       th * 1024 + tc * 512: th * 1024 + (tc + 1) * 512],
                                            in_=evst[sl]).then_inc(S[ssem], 16)
                        C.set("fev_%d" % bank, C.val("ev"))

        if emit(2):
            run_phase(lambda E, e: ffn_pass(E, e, ln2_tm), "p1b")

        def resid_ln(E, e, xin, xout, lnout, is_final, xin_sems):
            ab = Alloc(ab_t, AB_N)
            af = Alloc(af_t, AF_N)
            xa = [af.tile(D) for _ in range(2)]
            ft = [ab.tile(D) for _ in range(2)]
            ln_s = [ab.tile(D) for _ in range(2)]
            sq_s = ab.tile(D)
            stat = mk_stats(af, 2)
            t_ev = [(sm, C.val(sm)) for sm in ("ev0", "ev1", "ev2", "ev3")]
            t_xin = [(sm, C.val(sm)) for sm in xin_sems]
            for tt in range(16):
                sl = tt % 2
                asem = "ra%d" % sl
                fsem = "rf%d" % sl
                osem = "ro%d" % sl
                t_x = C.bump(asem, 16)
                if E == "s":
                    if tt < 2:
                        for sm, vv in t_xin:
                            if vv > 0:
                                e.wait_ge(S[sm], vv)
                    e.wait_ge(S["dv"], C.get("rl_xa_%d" % sl))
                    if C.get("rl_xo_%d" % sl) > 0:
                        e.wait_ge(S[osem], C.get("rl_xo_%d" % sl))
                    e.dma_start(out=xa[sl], in_=xin[tt * 128:(tt + 1) * 128, :]).then_inc(S[asem], 16)
                t_f = C.bump(fsem, 16)
                if E == "y":
                    if tt < 2:
                        for sm, vv in t_ev:
                            if vv > 0:
                                e.wait_ge(S[sm], vv)
                    e.wait_ge(S["dv"], C.get("rl_xa_%d" % sl))
                    e.dma_start(out=ft[sl], in_=ffn_fm[0:D, tt * 128:(tt + 1) * 128],
                                transpose=True).then_inc(S[fsem], 16)
                tadd = C.bump("dv", 1)
                if E == "v":
                    e.wait_ge(S[asem], t_x)
                    e.wait_ge(S[fsem], t_f)
                    e.wait_ge(S[osem], C.get("rl_xo_%d" % sl))
                    e.tensor_add(xa[sl], xa[sl], ft[sl]).then_inc(S["dv"], 1)
                C.bump(osem, 16)
                if E == "s":
                    e.wait_ge(S["dv"], tadd)
                    e.dma_start(out=xout[tt * 128:(tt + 1) * 128, :], in_=xa[sl]).then_inc(S[osem], 16)
                tlast = tadd
                if lnout is not None:
                    tdv = ln_pipeline(E, e, xa[sl], sq_s, stat[sl], ln_s[sl],
                                      waits_v=[("dv", tadd)], waits_s=[("dv", tadd)],
                                      war_v=[(osem, C.get("rl_xo_%d" % sl))])
                    tlast = tdv
                    C.bump(osem, 16)
                    if E == "s":
                        e.wait_ge(S["dv"], tdv)
                        e.dma_start(out=lnout[tt * 128:(tt + 1) * 128, :], in_=ln_s[sl]).then_inc(S[osem], 16)
                C.set("rl_xa_%d" % sl, tlast)
                C.set("rl_xo_%d" % sl, C.val(osem))
            if is_final:
                t_ro0, t_ro1 = C.val("ro0"), C.val("ro1")
                C.bump("fin", 1)
                if E == "g":
                    e.wait_ge(S["ro0"], t_ro0)
                    e.wait_ge(S["ro1"], t_ro1)
                    if C.val("dbgs"):
                        e.wait_ge(S["dbgs"], C.val("dbgs"))
                    e.engine_nop().then_inc(S["fin"], 1)

        def p1c(E, e):
            resid_ln(E, e, x1, x2, ln3_own, False, ("xr0", "xr1"))
            t_ro0, t_ro1 = C.val("ro0"), C.val("ro1")
            C.bump("cc", 1)
            if E == "g":
                e.wait_ge(S["ro0"], t_ro0)
                e.wait_ge(S["ro1"], t_ro1)
                if SIM_NOCC:
                    e.engine_nop().then_inc(S["cc"], 1)
                else:
                    e.collective_compute(
                        "AllGather", ALU.bypass,
                        replica_groups=[[0, 1], [2, 3], [4, 5], [6, 7]],
                        ins=[ln3_own.ap().opt()],
                        outs=[ln3_full.ap().opt()],
                    ).then_inc(S["cc"], 1)
                if debug:
                    e.dma_start(out=dbg["d_x2"][:, :], in_=x2[:, :]).then_inc(S["dbgs"], 16)
                    e.wait_ge(S["cc"], C.val("cc"))
                    e.dma_start(out=dbg["d_ln3"][:, :], in_=ln3_full[:, :]).then_inc(S["dbgs"], 16)
                    e.dma_start(out=dbg["d_wq"][:, :], in_=wq_all[:, :]).then_inc(S["dbgs"], 16)
            if debug:
                C.bump("dbgs", 48)

        if emit(3):
            run_phase(p1c, "p1c")

        def p1d(E, e):
            ab = Alloc(ab_t, AB_N)
            xlt3 = [ab.tile(1024) for _ in range(NI)]
            wtl_base = ab.off
            wtl = [[ab.tile(256) for _ in range(NI)] for _ in range(2)]
            qst = [ab.tile(512) for _ in range(8)]
            wvt = [ab_t[:, wtl_base + i * 512: wtl_base + (i + 1) * 512] for i in range(NI)]
            qsems = ["q0", "q1", "q2s", "q3", "q4", "q5", "q6", "q7"]
            t_cc = C.val("cc")
            C.bump("po", 1)
            if E == "g":
                e.wait_ge(S["cc"], t_cc)
                e.engine_nop().then_inc(S["po"], 1)
            for qt4 in range(4):
                for i in range(NI):
                    C.bump("xlt", 16)
                    if E == "y":
                        if i == 0:
                            e.wait_ge(S["po"], C.val("po"))
                            e.wait_ge(S["pe"], C.get("p1d_x_use"))
                        e.dma_start(out=xlt3[i],
                                    in_=ln3_full[qt4 * 1024:(qt4 + 1) * 1024, i * 128:(i + 1) * 128],
                                    transpose=True).then_inc(S["xlt"], 16)
                t_xl = C.val("xlt")
                for wname, wsrc, n_sb, bias_sb, dstT in (
                    ("q", wqb_bf, 4, bq_sb, qT_d),
                    ("k", wk_bf, 2, bk_sb, kT_d),
                ):
                    for sb in range(n_sb):
                        for dbp in range(2):
                            buf = (sb * 2 + dbp) % 2
                            bsem = "wta" if buf == 0 else "wtb"
                            for i in range(NI):
                                C.bump(bsem, 16)
                                if E == "y":
                                    if i == 0:
                                        e.wait_ge(S["wq1"], 48)
                                        e.wait_ge(S["pe"], C.get("p1d_w_use_%d" % buf))
                                        e.wait_ge(S["pe"], C.get("p1d_wv_use"))
                                    e.dma_start(out=wtl[buf][i],
                                                in_=wsrc[sb * 512 + dbp * 256: sb * 512 + (dbp + 1) * 256,
                                                         i * 128:(i + 1) * 128],
                                                transpose=True).then_inc(S[bsem], 16)
                            t_wl = C.val(bsem)
                            grp = (sb * 2 + dbp) % 2
                            pb = grp * 4
                            tpe = C.bump("pe", 1)
                            if E == "t":
                                e.wait_ge(S[bsem], t_wl)
                                if sb == 0 and dbp == 0 and wname == "q":
                                    e.wait_ge(S["xlt"], t_xl)
                                    e.wait_ge(S["dv"], C.get("p1d_vev_all"))
                                e.wait_ge(S["ac"], C.get("p1d_ev_%d" % grp))
                                for i in range(NI):
                                    for db2 in range(2):
                                        for tc in range(2):
                                            e.matmul(ps[pb + db2 * 2 + tc][:, 0:512],
                                                     wtl[buf][i][:, db2 * 128:(db2 + 1) * 128],
                                                     xlt3[i][:, tc * 512:(tc + 1) * 512],
                                                     start=(i == 0), stop=False)
                                for db2 in range(2):
                                    db_ = sb * 4 + dbp * 2 + db2
                                    for tc in range(2):
                                        ins = e.matmul(ps[pb + db2 * 2 + tc][:, 0:512],
                                                       bias_sb[0:1, db_ * 128:(db_ + 1) * 128],
                                                       ones512[0:1, :],
                                                       start=False, stop=True)
                                ins.then_inc(S["pe"], 1)
                            C.set("p1d_w_use_%d" % buf, tpe)
                            for db2 in range(2):
                                db = sb * 4 + dbp * 2 + db2
                                for tc in range(2):
                                    slq = grp * 4 + db2 * 2 + tc
                                    ssem = qsems[slq]
                                    tev = C.bump("ac", 1)
                                    if E == "s":
                                        e.wait_ge(S["pe"], tpe)
                                        e.wait_ge(S[ssem], C.val(ssem))
                                        e.activation(qst[slq], ps[pb + db2 * 2 + tc][:, 0:512],
                                                     AF.Copy).then_inc(S["ac"], 1)
                                    C.set("p1d_ev_%d" % grp, tev)
                                    C.bump(ssem, 16)
                                    if E == "s":
                                        e.wait_ge(S["ac"], tev)
                                        e.dma_start(out=dstT[db * 128:(db + 1) * 128,
                                                             qt4 * 1024 + tc * 512: qt4 * 1024 + (tc + 1) * 512],
                                                    in_=qst[slq]).then_inc(S[ssem], 16)
                for dc in range(2):
                    for i in range(NI):
                        C.bump("wv2", 16)
                        if E == "y":
                            if i == 0:
                                e.wait_ge(S["pe"], C.get("p1d_w_use_0"))
                                e.wait_ge(S["pe"], C.get("p1d_w_use_1"))
                                e.wait_ge(S["pe"], C.get("p1d_wv_use"))
                            e.dma_start(out=wvt[i],
                                        in_=wv_bf[dc * 512:(dc + 1) * 512, i * 128:(i + 1) * 128],
                                        transpose=True).then_inc(S["wv2"], 16)
                    t_wv = C.val("wv2")
                    for tt in range(8):
                        bank = tt % 4
                        tpe = C.bump("pe", 1)
                        if E == "t":
                            if tt == 0:
                                e.wait_ge(S["wv2"], t_wv)
                                if dc == 0:
                                    e.wait_ge(S["ac"], C.get("p1d_ev_0"))
                                    e.wait_ge(S["ac"], C.get("p1d_ev_1"))
                            e.wait_ge(S["dv"], C.get("p1d_vev_%d" % bank))
                            for i in range(NI):
                                e.matmul(ps[bank][:, 0:512],
                                         xlt3[i][:, tt * 128:(tt + 1) * 128],
                                         wvt[i], start=(i == 0), stop=False)
                            ins = e.matmul(ps[bank][:, 0:512], ones_sb[0:1, 0:128],
                                           bvr_sb[0:1, dc * 512:(dc + 1) * 512],
                                           start=False, stop=True)
                            ins.then_inc(S["pe"], 1)
                        if tt == 7:
                            C.set("p1d_wv_use", tpe)
                        slv = 4 + bank
                        ssem = qsems[slv]
                        tev = C.bump("dv", 1)
                        if E == "v":
                            e.wait_ge(S["pe"], tpe)
                            e.wait_ge(S[ssem], C.val(ssem))
                            e.tensor_copy(qst[slv], ps[bank][:, 0:512]).then_inc(S["dv"], 1)
                        C.set("p1d_vev_%d" % bank, tev)
                        C.bump(ssem, 16)
                        if E == "s":
                            e.wait_ge(S["dv"], tev)
                            e.dma_start(out=v_d[qt4 * 1024 + tt * 128: qt4 * 1024 + (tt + 1) * 128,
                                                dc * 512:(dc + 1) * 512],
                                        in_=qst[slv]).then_inc(S[ssem], 16)
                C.set("p1d_x_use", C.val("pe"))
                C.set("p1d_vev_all", C.val("dv"))
            if debug:
                C.bump("dbgs", 48)
                if E == "g":
                    for sm in qsems:
                        if C.val(sm) > 0:
                            e.wait_ge(S[sm], C.val(sm))
                    e.dma_start(out=dbg["d_qT"][:, :], in_=qT_d[:, :]).then_inc(S["dbgs"], 16)
                    e.dma_start(out=dbg["d_kT"][:, :], in_=kT_d[:, :]).then_inc(S["dbgs"], 16)
                    e.dma_start(out=dbg["d_v"][:, :], in_=v_d[:, :]).then_inc(S["dbgs"], 16)

        if emit(4):
            run_phase(p1d, "p1d")

        def p2(E, e):
            ab = Alloc(ab_t, AB_N)
            af = Alloc(af_t, AF_N)
            kts = [ab.tile(T) for _ in range(2)]
            qts = [ab.tile(T) for _ in range(2)]
            v129 = [ab.tile(NI * 129) for _ in range(2)]
            pt_s = [ab.tile(512) for _ in range(2)]
            att_s = [ab.tile(128) for _ in range(4)]
            mxb = ab.tile(4)
            mxcol = af.tile(8)
            mxt = af.tile(256)
            bia = af.tile(2)
            recs = af.tile(4)
            qsems = ["q0", "q1", "q2s", "q3", "q4", "q5", "q6", "q7"]
            t_qkv = [(sm, C.val(sm)) for sm in qsems]
            t_mxinit = C.bump("dv", 1)
            if E == "v":
                e.memset(mxt, -1e30).then_inc(S["dv"], 1)
            C.bump("po", 1)
            if E == "g":
                for b in range(2):
                    vv = v129[b].rearrange("p (a b) -> p a b", b=129)
                    e.memset(vv[:, :, 128:129], 1.0)
                e.engine_nop().then_inc(S["po"], 1)
            t_ones = C.val("po")

            for kk in range(P2_KK):
                kb = kk % 2
                ksem = "k2a" if kb == 0 else "k2b"
                C.bump(ksem, 32)
                if E == "y":
                    e.wait_ge(S["pe"], C.get("p2_kv_use_%d" % kb))
                    if kk < 2:
                        e.wait_ge(S["po"], t_ones)
                        for sm, vv in t_qkv:
                            if vv > 0:
                                e.wait_ge(S[sm], vv)
                    e.dma_start(out=kts[kb], in_=kT_d[kk * 128:(kk + 1) * 128, :]).then_inc(S[ksem], 16)
                    vv = v129[kb].rearrange("p (a b) -> p a b", b=129)
                    e.dma_start(out=vv[:, :, 0:128],
                                in_=v_d.ap().rearrange("(a p) c -> p a c", p=128)[:, :, kk * 128:(kk + 1) * 128],
                                ).then_inc(S[ksem], 16)
                t_kv = C.val(ksem)
                for j in range(2):
                    qb = j % 2
                    qsem = "q2a" if qb == 0 else "q2b"
                    lh = j * 8 + kk
                    C.bump(qsem, 16)
                    if E == "y":
                        e.wait_ge(S["pe"], C.get("p2_q_use_%d" % qb))
                        e.dma_start(out=qts[qb], in_=qT_d[lh * 128:(lh + 1) * 128, :]).then_inc(S[qsem], 16)
                    t_q = C.val(qsem)
                    for qt in range(32):
                        nlc = qt // 4 + 1
                        for lc in range(nlc):
                            bank = 6 + lc % 2
                            tpe = C.bump("pe", 1)
                            if E == "t":
                                if qt == 0 and lc == 0:
                                    e.wait_ge(S[ksem], t_kv)
                                    e.wait_ge(S[qsem], t_q)
                                    e.wait_ge(S["dv"], C.val("dv"))
                                e.wait_ge(S["mx"], C.get("p2_psA_%d" % bank))
                                ins = e.matmul(ps[bank][:, 0:512], qts[qb][:, qt * 128:(qt + 1) * 128],
                                               kts[kb][:, lc * 512:(lc + 1) * 512],
                                               start=True, stop=True)
                                ins.then_inc(S["pe"], 1)
                            tmx = C.bump("mx", 1)
                            if E == "v":
                                e.wait_ge(S["pe"], tpe)
                                if qt == 0 and lc == 0:
                                    e.wait_ge(S["dv"], t_mxinit)
                                e.reduce_max(mxt[:, qt * 8 + lc: qt * 8 + lc + 1],
                                             ps[bank][:, 0:512:4], axis=AX.X).then_inc(S["mx"], 1)
                            C.set("p2_psA_%d" % bank, tmx)
                    def chain_step(c, step):
                        cb2 = c % 2
                        if step == 0:
                            tc0 = C.bump("dv", 1)
                            if E == "v":
                                e.wait_ge(S["mx"], C.val("mx"))
                                e.wait_ge(S["pe"], C.get("p2_mxb_use"))
                                e.wait_ge(S["ac"], C.get("p2_bia_use_%d" % cb2))
                                e.reduce_max(mxcol[:, cb2:cb2 + 1], mxt[:, 32 * c:32 * c + 32],
                                             axis=AX.X).then_inc(S["dv"], 1)
                            C.set("p2_ch0_%d" % cb2, tc0)
                        elif step == 1:
                            tc1 = C.bump("dv", 1)
                            if E == "v":
                                e.wait_ge(S["dv"], C.get("p2_ch0_%d" % cb2))
                                e.tensor_copy(mxb[:, cb2:cb2 + 1], mxcol[:, cb2:cb2 + 1]).then_inc(S["dv"], 1)
                            C.set("p2_ch1_%d" % cb2, tc1)
                        elif step == 2:
                            tpe_ = C.bump("pe", 1)
                            if E == "t":
                                e.wait_ge(S["dv"], C.get("p2_ch1_%d" % cb2))
                                e.wait_ge(S["dv"], C.get("p2_ch4rd"))
                                ins_ = e.matmul(ps[6][0:1, 0:128], mxb[:, cb2:cb2 + 1], ident_sb[:, :],
                                                start=True, stop=True)
                                ins_.then_inc(S["pe"], 1)
                            C.set("p2_ch2_%d" % cb2, tpe_)
                        elif step == 3:
                            tc2 = C.bump("dv", 1)
                            if E == "v":
                                e.wait_ge(S["pe"], C.get("p2_ch2_%d" % cb2))
                                e.reduce_max(mxcol[0:1, 2 + cb2:3 + cb2], ps[6][0:1, 0:128],
                                             axis=AX.X).then_inc(S["dv"], 1)
                            C.set("p2_ch3_%d" % cb2, C.val("dv"))
                            tc3 = C.bump("dv", 1)
                            if E == "v":
                                e.wait_ge(S["dv"], C.get("p2_ch3_%d" % cb2))
                                e.tensor_copy(mxb[0:1, 2 + cb2:3 + cb2],
                                              mxcol[0:1, 2 + cb2:3 + cb2]).then_inc(S["dv"], 1)
                            C.set("p2_ch4rd", tc3)
                            C.set("p2_ch3b_%d" % cb2, tc3)
                        elif step == 4:
                            tpe2_ = C.bump("pe", 1)
                            if E == "t":
                                e.wait_ge(S["dv"], C.get("p2_ch3b_%d" % cb2))
                                ins_ = e.matmul(ps[6][:, 256 + cb2:257 + cb2], nsrow_sb[0:1, :],
                                                mxb[0:1, 2 + cb2:3 + cb2], start=True, stop=True)
                                ins_.then_inc(S["pe"], 1)
                            C.set("p2_mxb_use", tpe2_)
                            C.set("p2_ch5_%d" % cb2, tpe2_)
                        elif step == 5:
                            tbia_ = C.bump("dv", 1)
                            if E == "v":
                                e.wait_ge(S["pe"], C.get("p2_ch5_%d" % cb2))
                                e.tensor_copy(bia[:, cb2:cb2 + 1],
                                              ps[6][:, 256 + cb2:257 + cb2]).then_inc(S["dv"], 1)
                            C.set("p2_tbia_%d" % cb2, tbia_)

                    for st_ in range(6):
                        chain_step(P2_C0, st_)
                    for c in range(P2_C0, P2_NC):
                        cb = c % 2
                        tbia = C.get("p2_tbia_%d" % cb)
                        nlt = 4 * c + 4

                        def st_mm(lt):
                            bank = lt % 2
                            t_ = C.bump("pe", 1)
                            if E == "t":
                                e.wait_ge(S["ac"], C.get("p2_psT_%d" % bank))
                                ins_ = e.matmul(ps[2 + bank][:, 0:512],
                                                kts[kb][:, lt * 128:(lt + 1) * 128],
                                                qts[qb][:, c * 512:(c + 1) * 512],
                                                start=True, stop=True)
                                ins_.then_inc(S["pe"], 1)
                            return t_

                        tpe3_cur = st_mm(0)
                        for lt in range(nlt):
                            bank = lt % 2
                            slp = lt % 2
                            tex = C.bump("ac", 1)
                            if E == "s":
                                e.wait_ge(S["pe"], tpe3_cur)
                                if lt == 0:
                                    e.wait_ge(S["dv"], tbia)
                                e.wait_ge(S["pe"], C.get("p2_pt_use_%d" % slp))
                                e.wait_ge(S["po"], C.get("p2_msk_%d" % slp))
                                e.activation(pt_s[slp], ps[2 + bank][:, 0:512], AF.Exp,
                                             bias=bia[:, cb:cb + 1], scale=SCALE).then_inc(S["ac"], 1)
                            C.set("p2_psT_%d" % bank, tex)
                            if c + 1 < P2_NC and 1 <= lt <= 6:
                                chain_step(c + 1, lt - 1)
                            tmask = None
                            if lt >= 4 * c:
                                s_idx = lt - 4 * c
                                tmask = C.bump("po", 1)
                                if E == "g":
                                    e.wait_ge(S["ac"], tex)
                                    e.tensor_mul(pt_s[slp], pt_s[slp],
                                                 mask_sb[:, s_idx * 512:(s_idx + 1) * 512]).then_inc(S["po"], 1)
                                C.set("p2_msk_%d" % slp, tmask)
                            tpe3_next = st_mm(lt + 1) if lt + 1 < nlt else None
                            tpe4 = C.bump("pe", 1)
                            if E == "t":
                                if tmask is not None:
                                    e.wait_ge(S["po"], tmask)
                                else:
                                    e.wait_ge(S["ac"], tex)
                                if lt == 0:
                                    for s2 in range(4):
                                        e.wait_ge(S["ac"], C.get("p2_avb_%d" % s2))
                                        e.wait_ge(S["dv"], C.get("p2_avd_%d" % s2))
                                for s2 in range(4):
                                    avb = (4, 5, 0, 1)[s2]
                                    ins = e.matmul(ps[avb][:, 0:129],
                                                   pt_s[slp][:, s2 * 128:(s2 + 1) * 128],
                                                   v129[kb][:, lt * 129:(lt + 1) * 129],
                                                   start=(lt == 0), stop=(lt == nlt - 1))
                                ins.then_inc(S["pe"], 1)
                            C.set("p2_pt_use_%d" % slp, tpe4)
                            tpe3_cur = tpe3_next
                        if c + 1 < P2_NC and nlt < 7:
                            for st_ in range(max(0, nlt - 1), 6):
                                chain_step(c + 1, st_)
                        tpe_av = C.val("pe")
                        for s2 in range(4):
                            avb = (4, 5, 0, 1)[s2]
                            pcol = ps[avb][:, 128:129]
                            pmain = ps[avb][:, 0:128]
                            asem = "at%d" % s2
                            trc = C.bump("dv", 1)
                            if E == "v":
                                e.wait_ge(S["pe"], tpe_av)
                                e.wait_ge(S["ac"], C.get("p2_rc_use_%d" % s2))
                                e.reciprocal(recs[:, s2:s2 + 1], pcol).then_inc(S["dv"], 1)
                            C.set("p2_avd_%d" % s2, trc)
                            tav = C.bump("ac", 1)
                            if E == "s":
                                e.wait_ge(S["dv"], trc)
                                e.wait_ge(S[asem], C.get("p2_att_%d" % s2))
                                e.activation(att_s[s2], pmain, AF.Copy,
                                             scale=recs[:, s2:s2 + 1]).then_inc(S["ac"], 1)
                            C.set("p2_avb_%d" % s2, tav)
                            C.set("p2_rc_use_%d" % s2, tav)
                            C.bump(asem, 16)
                            if E == "y":
                                e.wait_ge(S["ac"], tav)
                                e.dma_start(out=attn_own[c * 512 + s2 * 128: c * 512 + (s2 + 1) * 128,
                                                         lh * 128:(lh + 1) * 128],
                                            in_=att_s[s2]).then_inc(S[asem], 16)
                            C.set("p2_att_%d" % s2, C.val(asem))
                    C.set("p2_q_use_%d" % qb, C.val("pe"))
                C.set("p2_kv_use_%d" % kb, C.val("pe"))

            if P2_CC:
                t_at = [(("at%d" % s2), C.val("at%d" % s2)) for s2 in range(4)]
                C.bump("cc", 1)
                if E == "g":
                    for sm, vv in t_at:
                        if vv > 0:
                            e.wait_ge(S[sm], vv)
                    if SIM_NOCC:
                        e.engine_nop().then_inc(S["cc"], 1)
                    else:
                        e.collective_compute(
                            "AllGather", ALU.bypass,
                            replica_groups=[[0, 1], [2, 3], [4, 5], [6, 7]],
                            ins=[attn_own.ap().opt()],
                            outs=[attn_full.ap().opt()],
                        ).then_inc(S["cc"], 1)
                    if debug:
                        e.dma_start(out=dbg["d_attn"][:, :], in_=attn_full[:, :]).then_inc(S["dbgs"], 16)
                if debug:
                    C.bump("dbgs", 16)

        if emit(5):
            run_phase(p2, "p2")

        def p3a(E, e):
            ab = Alloc(ab_t, AB_N)
            af = Alloc(af_t, AF_N)
            xa = [af.tile(D) for _ in range(2)]
            at = [ab.tile(2048) for _ in range(8)]
            ln_s = [ab.tile(D) for _ in range(2)]
            sq_s = ab.tile(D)
            stat = mk_stats(af, 2)
            t_x2w = [(sm, C.val(sm)) for sm in ("ro0", "ro1")]
            t_cc3 = C.val("cc")
            C.bump("po", 1)
            if E == "g":
                e.wait_ge(S["cc"], t_cc3)
                e.engine_nop().then_inc(S["po"], 1)
            t_rdy = C.val("po")

            dv0 = C.val("dv")

            def add_dv_of(tt):
                return dv0 + 8 * tt + 1

            def emit_at_loads(g, base):
                for tt in range(16):
                    sl = tt % 4
                    if tt >= 4:
                        g.wait_ge(S["dv"], add_dv_of(tt - 4))
                    for r in range(2):
                        g.dma_start(out=at[r * 4 + sl],
                                    in_=attn_full[r * T + base + tt * 128: r * T + base + (tt + 1) * 128, :],
                                    ).then_inc(S["pa%d" % (r * 4 + sl)], 16)

            if E == "g":
                e.wait_ge(S["po"], t_rdy)
                if no_branch:
                    emit_at_loads(e, 0)
                else:
                    with e.register("rid") as rid:
                        e.reg_load(rid, cid_sb[0:1, 0:1])
                        with e.If_eq(rid, 0):
                            emit_at_loads(e, 0)
                        with e.If_ne(rid, 0):
                            emit_at_loads(e, TOK)
            pa_base = [C.val("pa%d" % i) for i in range(8)]
            for tt in range(16):
                for r in range(2):
                    C.bump("pa%d" % (r * 4 + tt % 4), 16)

            for tt in range(16):
                sl = tt % 2
                slat = tt % 4
                asem = "ra%d" % sl
                osem = "ro%d" % sl
                t_x = C.bump(asem, 16)
                if E == "s":
                    if tt < 2:
                        for sm, vv in t_x2w:
                            if vv > 0:
                                e.wait_ge(S[sm], vv)
                    e.wait_ge(S["dv"], C.get("p3_xa_%d" % sl))
                    if C.get("p3_xo_%d" % sl) > 0:
                        e.wait_ge(S[osem], C.get("p3_xo_%d" % sl))
                    e.dma_start(out=xa[sl], in_=x2[tt * 128:(tt + 1) * 128, :]).then_inc(S[asem], 16)
                tadd = C.bump("dv", 1)
                assert tadd == add_dv_of(tt)
                if E == "v":
                    e.wait_ge(S[asem], t_x)
                    for r in range(2):
                        e.wait_ge(S["pa%d" % (r * 4 + slat)],
                                  pa_base[r * 4 + slat] + 16 * (tt // 4 + 1))
                    e.wait_ge(S[osem], C.get("p3_xo_%d" % sl))
                    for r in range(2):
                        for jj in range(2):
                            o0 = (jj * 16 + r * 8) * 128
                            ins = e.tensor_add(xa[sl][:, o0:o0 + 1024],
                                               xa[sl][:, o0:o0 + 1024],
                                               at[r * 4 + slat][:, jj * 1024:(jj + 1) * 1024])
                    ins.then_inc(S["dv"], 1)
                C.bump(osem, 16)
                if E == "s":
                    e.wait_ge(S["dv"], tadd)
                    e.dma_start(out=x3[tt * 128:(tt + 1) * 128, :], in_=xa[sl]).then_inc(S[osem], 16)
                tdv = ln_pipeline(E, e, xa[sl], sq_s, stat[sl], ln_s[sl],
                                  waits_v=[("dv", tadd)], waits_s=[("dv", tadd)],
                                  war_v=[(osem, C.get("p3_xo_%d" % sl))])
                C.set("p3_xa_%d" % sl, tdv)
                C.bump(osem, 16)
                if E == "s":
                    e.wait_ge(S["dv"], tdv)
                    e.dma_start(out=ln4_tm[tt * 128:(tt + 1) * 128, :], in_=ln_s[sl]).then_inc(S[osem], 16)
                C.set("p3_xo_%d" % sl, C.val(osem))
            if debug:
                C.bump("dbgs", 16)
                if E == "g":
                    e.wait_ge(S["ro0"], C.val("ro0"))
                    e.wait_ge(S["ro1"], C.val("ro1"))
                    e.dma_start(out=dbg["d_x3"][:, :], in_=x3[:, :]).then_inc(S["dbgs"], 16)

        if emit(6):
            run_phase(p3a, "p3a")

        if emit(7):
            run_phase(lambda E, e: ffn_pass(E, e, ln4_tm), "p3b")

        if emit(8):
            run_phase(lambda E, e: resid_ln(E, e, x3, out_t, None, True, ("ro0", "ro1")), "p3c")

    return nc



def _shard_inputs(inp):
    X = np.asarray(inp["L0_attn_a_hidden_states"], np.float32)
    Wqa = np.asarray(inp["L0_attn_a_Wq"], np.float32).reshape(32, 128, D)
    bqa_full = np.asarray(inp["L0_attn_a_bq"], np.float32).reshape(32, 128)
    M = np.asarray(inp["L0_attn_a_M"], np.float32)
    Z = np.asarray(inp["L0_attn_a_z"], np.float32)
    Wo = np.asarray(inp["L0_attn_a_Wo"], np.float32)
    Bs = np.asarray(inp["L0_ffn_bases"], np.int8)
    Al = np.asarray(inp["L0_ffn_alphas"], np.float32)[:, :, 0]
    Wqb = np.asarray(inp["L0_attn_b_Wq"], np.float32).reshape(32, 128, D)
    bqb_full = np.asarray(inp["L0_attn_b_bq"], np.float32).reshape(32, 128)
    Wk = np.asarray(inp["L0_attn_b_Wk"], np.float32).reshape(16, 128, D)
    bk_full = np.asarray(inp["L0_attn_b_bk"], np.float32).reshape(16, 128)
    Wv = np.asarray(inp["L0_attn_b_Wv"], np.float32).reshape(16, 128, D)
    bv_full = np.asarray(inp["L0_attn_b_bv"], np.float32).reshape(16, 128)

    sel = list(range(0, 32, 4))
    wqa_s = np.ascontiguousarray(Wqa[sel].reshape(HA * 128, D))
    bqa_s = np.ascontiguousarray(bqa_full[sel].T)
    z_s = np.ascontiguousarray(Z.T)

    ident = np.eye(128, dtype=np.float32)
    nsr = np.full((1, 128), -SCALE, np.float32)
    msk = np.zeros((4, 128, 512), np.float32)
    for s in range(4):
        for l in range(128):
            msk[s, l, s * 128 + l:] = 1.0

    in_maps = []
    for c in range(8):
        p, half = c // 2, c % 2
        x_own = np.ascontiguousarray(X[p, half * TOK:(half + 1) * TOK, :])
        kvs = [half * 8 + k for k in range(8)]
        qhs = [j * 16 + half * 8 + k for j in range(2) for k in range(8)]
        ad = np.zeros((4, 4, 128, 128), np.float32)
        for k in range(4):
            for ob in range(4):
                np.fill_diagonal(ad[k, ob],
                                 Al[k, 512 * c + 128 * ob: 512 * c + 128 * (ob + 1)])
        in_maps.append({
            "x0": x_own,
            "wqa": wqa_s,
            "bqa": bqa_s,
            "m_a": M,
            "z_a": z_s,
            "wo": Wo,
            "bases": np.ascontiguousarray(Bs[:, 512 * c:512 * (c + 1), :]),
            "adiag": ad,
            "wqb": np.ascontiguousarray(Wqb[qhs].reshape(16 * 128, D)),
            "bqb_r": np.ascontiguousarray(bqb_full[qhs].reshape(1, 16 * 128)),
            "wk": np.ascontiguousarray(Wk[kvs].reshape(KVH * 128, D)),
            "bk_r": np.ascontiguousarray(bk_full[kvs].reshape(1, KVH * 128)),
            "wv": np.ascontiguousarray(Wv[kvs].reshape(KVH * 128, D)),
            "bvr": np.ascontiguousarray(bv_full[kvs].reshape(1, KVH * 128)),
            "identi": ident,
            "nsrow": nsr,
            "masks": msk,
            "cid": np.array([[half]], np.int32),
        })
    return in_maps


def kernel(**inputs):
    nc = build_nc(debug=False)
    in_maps = _shard_inputs(inputs)
    res = run_bass_kernel_spmd(nc, in_maps, core_ids=list(range(8)))
    X = np.asarray(inputs["L0_attn_a_hidden_states"], np.float32)
    outp = np.empty_like(X)
    for c in range(8):
        p, half = c // 2, c % 2
        outp[p, half * TOK:(half + 1) * TOK, :] = np.asarray(res.results[c]["out"], np.float32)
    return outp


# version 05 (final)
# speedup vs baseline: 1.3298x; 1.0239x over previous
import sys
for _p in ("/opt/trn_rl_repo",):
    if _p not in sys.path:
        sys.path.insert(0, _p)
import math
import os as _os
from contextlib import ExitStack

import numpy as np

import concourse.bass as bass
import concourse.mybir as mybir
from concourse.bass_utils import run_bass_kernel_spmd

F32 = mybir.dt.float32
BF16 = mybir.dt.bfloat16
I8 = mybir.dt.int8
I32 = mybir.dt.int32
AF = mybir.ActivationFunctionType
ALU = mybir.AluOpType
AX = mybir.AxisListType

TOK = 2048
D = 4096
T = 4096
NI = 32
SCALE = 1.0 / math.sqrt(32.0)
LN_EPS = 1e-5
DEN_EPS = 1e-6
HA = 8
KVH = 8

P2_KK = int(_os.environ.get("P2_KK", "8"))
P2_NC = int(_os.environ.get("P2_NC", "8"))
P2_C0 = int(_os.environ.get("P2_C0", "0"))
P2_CC = int(_os.environ.get("P2_CC", "1"))
P0_CC = int(_os.environ.get("P0_CC", "1"))
SIM_NOCC = int(_os.environ.get("SIM_NOCC", "0"))


class Counters:
    def __init__(self):
        self.c = {}

    def bump(self, name, n):
        self.c[name] = self.c.get(name, 0) + n
        return self.c[name]

    def val(self, name):
        return self.c.get(name, 0)

    def get(self, name):
        return self.c.get(name, 0)

    def set(self, name, v):
        self.c[name] = v

    def snap(self):
        return dict(self.c)

    def restore(self, s):
        self.c = dict(s)


class Alloc:
    def __init__(self, tensor, n):
        self.t = tensor
        self.n = n
        self.off = 0

    def tile(self, n):
        assert self.off + n <= self.n, f"arena overflow {self.off}+{n}>{self.n}"
        o = self.off
        self.off += n
        return self.t[:, o:o + n]


def build_nc(debug=False, no_branch=False, upto=99, only=None):
    nc = bass.Bass("TRN2", target_bir_lowering=False)

    def emit(i):
        return (upto >= i) if only is None else (i in only)

    x0 = nc.dram_tensor("x0", [TOK, D], F32, kind="ExternalInput")
    wqa = nc.dram_tensor("wqa", [HA * 128, D], F32, kind="ExternalInput")
    bqa = nc.dram_tensor("bqa", [128, HA], F32, kind="ExternalInput")
    m_a = nc.dram_tensor("m_a", [HA, 128, 128], F32, kind="ExternalInput")
    z_a = nc.dram_tensor("z_a", [128, HA], F32, kind="ExternalInput")
    wo = nc.dram_tensor("wo", [D, D], F32, kind="ExternalInput")
    bases = nc.dram_tensor("bases", [4, 512, D], I8, kind="ExternalInput")
    adiag = nc.dram_tensor("adiag", [4, 4, 128, 128], F32, kind="ExternalInput")
    wqb = nc.dram_tensor("wqb", [16 * 128, D], F32, kind="ExternalInput")
    bqb_r = nc.dram_tensor("bqb_r", [1, 16 * 128], F32, kind="ExternalInput")
    wk = nc.dram_tensor("wk", [KVH * 128, D], F32, kind="ExternalInput")
    bk_r = nc.dram_tensor("bk_r", [1, KVH * 128], F32, kind="ExternalInput")
    wv = nc.dram_tensor("wv", [KVH * 128, D], F32, kind="ExternalInput")
    bvr = nc.dram_tensor("bvr", [1, KVH * 128], F32, kind="ExternalInput")
    identi = nc.dram_tensor("identi", [128, 128], F32, kind="ExternalInput")
    nsrow = nc.dram_tensor("nsrow", [1, 128], F32, kind="ExternalInput")
    masks = nc.dram_tensor("masks", [4, 128, 512], F32, kind="ExternalInput")
    cid = nc.dram_tensor("cid", [1, 1], I32, kind="ExternalInput")

    out_t = nc.dram_tensor("out", [TOK, D], F32, kind="ExternalOutput")

    wqa_bf = nc.dram_tensor("wqa_bf", [HA * 128, D], BF16)
    wof = nc.dram_tensor("wof", [D, HA * 128], F32)
    wo_bf = nc.dram_tensor("wo_bf", [D, HA * 128], BF16)
    wqb_bf = nc.dram_tensor("wqb_bf", [16 * 128, D], BF16)
    wk_bf = nc.dram_tensor("wk_bf", [KVH * 128, D], BF16)
    wv_bf = nc.dram_tensor("wv_bf", [KVH * 128, D], BF16)
    wq_own = nc.dram_tensor("wq_own", [512, D], BF16)
    wq_all = nc.dram_tensor("wq_all", [D, D], BF16, addr_space="Shared")
    x1 = nc.dram_tensor("x1", [TOK, D], F32)
    x2 = nc.dram_tensor("x2", [TOK, D], F32)
    x3 = nc.dram_tensor("x3", [TOK, D], F32)
    ln1_tm = nc.dram_tensor("ln1_tm", [TOK, D], BF16)
    ln2_tm = nc.dram_tensor("ln2_tm", [TOK, D], BF16)
    ln4_tm = nc.dram_tensor("ln4_tm", [TOK, D], BF16)
    ln3_own = nc.dram_tensor("ln3_own", [TOK, D], BF16)
    ln3_full = nc.dram_tensor("ln3_full", [T, D], BF16)
    qT_d = nc.dram_tensor("qT_d", [16 * 128, T], BF16)
    kT_d = nc.dram_tensor("kT_d", [KVH * 128, T], BF16)
    v_d = nc.dram_tensor("v_d", [T, KVH * 128], BF16)
    attn_own = nc.dram_tensor("attn_own", [T, 2048], BF16)
    attn_full = nc.dram_tensor("attn_full", [2 * T, 2048], BF16)
    ffn_fm = nc.dram_tensor("ffn_fm", [D, TOK], BF16)

    dbg = {}
    if debug:
        for nm, shp, dt in [
            ("d_x1", [TOK, D], F32), ("d_x2", [TOK, D], F32),
            ("d_ln3", [T, D], BF16), ("d_qT", [16 * 128, T], BF16),
            ("d_kT", [KVH * 128, T], BF16), ("d_v", [T, KVH * 128], BF16),
            ("d_attn", [2 * T, 2048], BF16), ("d_x3", [TOK, D], F32),
            ("d_wq", [D, D], BF16),
        ]:
            dbg[nm] = nc.dram_tensor(nm, shp, dt, kind="ExternalOutput")

    C = Counters()

    with ExitStack() as ctx:
        en = ctx.enter_context
        AB_N = 67 * 1024
        AF_N = 14 * 1024
        ab_t = en(nc.sbuf_tensor("ab", [128, AB_N], BF16))
        af_t = en(nc.sbuf_tensor("af", [128, AF_N], F32))

        bq_sb = en(nc.sbuf_tensor("bq_sb", [1, 16 * 128], BF16))
        bk_sb = en(nc.sbuf_tensor("bk_sb", [1, KVH * 128], BF16))
        ones512 = en(nc.sbuf_tensor("ones512", [1, 512], BF16))
        bqa_sb = en(nc.sbuf_tensor("bqa_sb", [128, HA], F32))
        m_sb = en(nc.sbuf_tensor("m_sb", [128, HA * 128], BF16))
        z_sb = en(nc.sbuf_tensor("z_sb", [128, HA], BF16))
        ident_sb = en(nc.sbuf_tensor("ident_sb", [128, 128], BF16))
        ones_sb = en(nc.sbuf_tensor("ones_sb", [1, 128], BF16))
        nsrow_sb = en(nc.sbuf_tensor("nsrow_sb", [1, 128], BF16))
        bvr_sb = en(nc.sbuf_tensor("bvr_sb", [1, KVH * 128], BF16))
        mask_sb = en(nc.sbuf_tensor("mask_sb", [128, 4 * 512], BF16))
        cid_sb = en(nc.sbuf_tensor("cid_sb", [1, 1], I32))

        ps = [en(nc.psum_tensor(f"ps{i}", [128, 512], F32)) for i in range(8)]

        base_sems = ["pre", "preg", "cwqa", "cwo", "wt", "wt2", "xlt", "cc",
                     "pe", "dv", "ac", "po", "ev", "q2", "k2",
                     "mx", "wq1", "wv2", "fin", "dbgs"]
        slot_sems = ["wta", "wtb", "fwa", "fwb", "w2a", "w2b",
                     "k2a", "k2b", "q2a", "q2b",
                     "wqs0", "wqs1",
                     "xl0", "xl1",
                     "lo0", "lo1",
                     "xr0", "xr1",
                     "x1t",
                     "ev0", "ev1", "ev2", "ev3",
                     "ra0", "ra1",
                     "rf0", "rf1",
                     "ro0", "ro1",
                     "q0", "q1", "q2s", "q3", "q4", "q5", "q6", "q7",
                     "at0", "at1", "at2", "at3",
                     "pa0", "pa1", "pa2", "pa3", "pa4", "pa5", "pa6", "pa7",
                     ]
        S = {n: en(nc.semaphore(f"s_{n}")) for n in base_sems + slot_sems}

        def W(E, e, eng, *pairs):
            if E == eng:
                for sm, vv in pairs:
                    if vv > 0:
                        e.wait_ge(S[sm], vv)

        def ln_pipeline(E, e, xf, sq, st_, ln_out, waits_v=(), waits_s=(), war_v=()):
            mean, var, rstd, nmr = st_["mean"], st_["var"], st_["rstd"], st_["nmr"]
            nmean = st_["nmean"]
            tac = C.bump("ac", 1)
            if E == "s":
                for sm, vv in waits_s:
                    if vv > 0:
                        e.wait_ge(S[sm], vv)
                e.activation(sq, xf, AF.Square, accum_out=var).then_inc(S["ac"], 1)
            t1 = C.bump("dv", 1)
            if E == "v":
                for sm, vv in list(waits_v) + list(war_v):
                    if vv > 0:
                        e.wait_ge(S[sm], vv)
                e.reduce_sum(mean, xf, axis=AX.X).then_inc(S["dv"], 1)
            t2 = C.bump("dv", 1)
            if E == "v":
                e.wait_ge(S["dv"], t1)
                e.tensor_scalar(mean, mean, 1.0 / D, 0.0, ALU.mult, ALU.add).then_inc(S["dv"], 1)
            t3 = C.bump("dv", 1)
            if E == "v":
                e.wait_ge(S["dv"], t2)
                e.tensor_scalar(nmean, mean, -1.0, 0.0, ALU.mult, ALU.add)
                e.tensor_mul(nmr, mean, mean).then_inc(S["dv"], 1)
            t4 = C.bump("dv", 1)
            if E == "v":
                e.wait_ge(S["ac"], tac)
                e.wait_ge(S["dv"], t3)
                e.tensor_scalar(var, var, 1.0 / D, LN_EPS, ALU.mult, ALU.add).then_inc(S["dv"], 1)
            t5 = C.bump("dv", 1)
            if E == "v":
                e.wait_ge(S["dv"], t4)
                e.tensor_sub(var, var, nmr).then_inc(S["dv"], 1)
            t6 = C.bump("dv", 1)
            if E == "v":
                e.wait_ge(S["dv"], t5)
                e.reciprocal(rstd, var).then_inc(S["dv"], 1)
            tac2 = C.bump("ac", 1)
            if E == "s":
                e.wait_ge(S["dv"], t6)
                e.activation(rstd, rstd, AF.Sqrt).then_inc(S["ac"], 1)
            tac3 = C.bump("ac", 1)
            if E == "s":
                e.wait_ge(S["ac"], tac2)
                e.activation(nmr, rstd, AF.Copy, scale=nmean[:, 0:1]).then_inc(S["ac"], 1)
            tdv = C.bump("dv", 1)
            if E == "v":
                e.wait_ge(S["ac"], tac3)
                e.tensor_scalar(ln_out, xf, rstd[:, 0:1], nmr[:, 0:1],
                                ALU.mult, ALU.add).then_inc(S["dv"], 1)
            return tdv


        def ln_stats(E, e, xf, sq, mcol, vcol, waits_v=(), waits_s=()):
            tac = C.bump("ac", 1)
            if E == "s":
                for sm, vv in waits_s:
                    if vv > 0:
                        e.wait_ge(S[sm], vv)
                e.activation(sq, xf, AF.Square, accum_out=vcol).then_inc(S["ac"], 1)
            tdv = C.bump("dv", 1)
            if E == "v":
                for sm, vv in waits_v:
                    if vv > 0:
                        e.wait_ge(S[sm], vv)
                e.reduce_sum(mcol, xf, axis=AX.X).then_inc(S["dv"], 1)
            return tdv, tac

        def ln_chain(E, e, mean, var, rstd, nmr, nmean, wait_dv, wait_ac):
            t1 = C.bump("dv", 1)
            if E == "v":
                e.wait_ge(S["dv"], wait_dv)
                e.wait_ge(S["ac"], wait_ac)
                e.tensor_scalar(mean, mean, 1.0 / D, 0.0, ALU.mult, ALU.add).then_inc(S["dv"], 1)
            t2 = C.bump("dv", 1)
            if E == "v":
                e.wait_ge(S["dv"], t1)
                e.tensor_scalar(nmean, mean, -1.0, 0.0, ALU.mult, ALU.add)
                e.tensor_mul(nmr, mean, mean).then_inc(S["dv"], 1)
            t3 = C.bump("dv", 1)
            if E == "v":
                e.wait_ge(S["dv"], t2)
                e.tensor_scalar(var, var, 1.0 / D, LN_EPS, ALU.mult, ALU.add).then_inc(S["dv"], 1)
            t4 = C.bump("dv", 1)
            if E == "v":
                e.wait_ge(S["dv"], t3)
                e.tensor_sub(var, var, nmr).then_inc(S["dv"], 1)
            t5 = C.bump("dv", 1)
            if E == "v":
                e.wait_ge(S["dv"], t4)
                e.reciprocal(rstd, var).then_inc(S["dv"], 1)
            t6 = C.bump("ac", 1)
            if E == "s":
                e.wait_ge(S["dv"], t5)
                e.activation(rstd, rstd, AF.Sqrt).then_inc(S["ac"], 1)
            t7 = C.bump("dv", 1)
            if E == "v":
                e.wait_ge(S["ac"], t6)
                e.tensor_mul(nmr, rstd, nmean).then_inc(S["dv"], 1)
            return t7

        def ln_norm(E, e, xf, rstd_col, nmr_col, out, chain_dv, waits_v=(), war_v=()):
            tdv = C.bump("dv", 1)
            if E == "v":
                e.wait_ge(S["dv"], chain_dv)
                for sm, vv in list(waits_v) + list(war_v):
                    if vv > 0:
                        e.wait_ge(S[sm], vv)
                e.tensor_scalar(out, xf, rstd_col, nmr_col, ALU.mult, ALU.add).then_inc(S["dv"], 1)
            return tdv

        def mk_stats(af, n):
            out = []
            for _ in range(n):
                sc = af.tile(8)
                out.append({"mean": sc[:, 0:1], "var": sc[:, 1:2],
                            "rstd": sc[:, 2:3], "nmr": sc[:, 3:4],
                            "nmean": sc[:, 4:5]})
            return out

        def run_phase(fn, name):
            snap = C.snap()
            ends = {}
            with nc.Block(name) as blk:
                def mk(E):
                    def f(e):
                        C.restore(snap)
                        fn(E, e)
                        ends[E] = C.snap()
                    return f
                blk.tensor(mk("t"))
                blk.vector(mk("v"))
                blk.scalar(mk("s"))
                blk.gpsimd(mk("g"))
                blk.sync(mk("y"))
            vals = list(ends.values())
            for v_ in vals[1:]:
                assert v_ == vals[0], f"counter divergence in phase {name}"
            C.restore(vals[0])

        def p0(E, e):
            ab = Alloc(ab_t, AB_N)
            n_small = 11
            if E == "g":
                e.dma_start(out=ident_sb[:, :], in_=identi[:, :]).then_inc(S["pre"], 16)
                e.dma_start(out=nsrow_sb[0:1, :], in_=nsrow[0:1, :]).then_inc(S["pre"], 16)
                e.dma_start(out=bvr_sb[0:1, :], in_=bvr[0:1, :]).then_inc(S["pre"], 16)
                e.dma_start(out=m_sb[:, :].rearrange("d (h e) -> d h e", e=128),
                            in_=m_a.ap().rearrange("h d e -> d h e")).then_inc(S["pre"], 16)
                e.dma_start(out=z_sb[:, :], in_=z_a[:, :]).then_inc(S["pre"], 16)
                e.dma_start(out=bq_sb[0:1, :], in_=bqb_r[0:1, :]).then_inc(S["pre"], 16)
                e.dma_start(out=bk_sb[0:1, :], in_=bk_r[0:1, :]).then_inc(S["pre"], 16)
                e.dma_start(out=bqa_sb[:, :], in_=bqa[:, :]).then_inc(S["pre"], 16)
                e.dma_start(out=mask_sb[:, :].rearrange("p (s q) -> p s q", q=512),
                            in_=masks.ap().rearrange("s p q -> p s q")).then_inc(S["pre"], 16)
                e.dma_start(out=cid_sb[:, :], in_=cid[:, :]).then_inc(S["pre"], 16)
                e.dma_start(out=wqa_bf[:, :], in_=wqa[:, :]).then_inc(S["cwqa"], 16)
            C.bump("pre", 16 * (n_small - 1))
            C.bump("cwqa", 16)
            C.bump("po", 1)
            if E == "g":
                e.memset(ones_sb[0:1, :], 1.0)
                e.memset(ones512[0:1, :], 1.0).then_inc(S["po"], 1)

            bas, adg = [], []
            for k in range(4):
                rb, ra = [], []
                for ob in range(4):
                    tb = ab.tile(D)
                    rb.append(tb)
                    C.bump("preg", 16)
                    if E == "g":
                        e.dma_start(out=tb, in_=bases[k, ob * 128:(ob + 1) * 128, :]).then_inc(S["preg"], 16)
                    ta = ab.tile(128)
                    ra.append(ta)
                    C.bump("preg", 16)
                    if E == "g":
                        e.dma_start(out=ta, in_=adiag[k, ob, :, :]).then_inc(S["preg"], 16)
                bas.append(rb)
                adg.append(ra)
            t_bases = C.val("preg")

            for jj in range(4):
                for h in range(HA):
                    C.bump("pre", 16)
                    if E == "g":
                        src = wo[:, h * 512 + jj * 128: h * 512 + (jj + 1) * 128]
                        if jj == 0:
                            e.dma_start(out=wof[:, h * 128:(h + 1) * 128], in_=src).then_inc(S["pre"], 16)
                        else:
                            e.dma_start(out=wof[:, h * 128:(h + 1) * 128], in_=src,
                                        accum_op=ALU.add).then_inc(S["pre"], 16)
                if E == "g":
                    e.wait_ge(S["pre"], C.val("pre"))
            C.bump("cwo", 16)
            if E == "g":
                e.dma_start(out=wo_bf[:, :], in_=wof[:, :]).then_inc(S["cwo"], 16)

            wq_stage = [ab.tile(512) for _ in range(2)]
            for ob in range(4):
                for ic in range(8):
                    gi = ob * 8 + ic
                    bank = gi % 4
                    slot = gi % 2
                    ssem = "wqs%d" % slot
                    tpe = C.bump("pe", 1)
                    if E == "t":
                        if gi == 0:
                            e.wait_ge(S["preg"], t_bases)
                        e.wait_ge(S["ev"], C.get("wqev_%d" % bank))
                        for k in range(4):
                            ins = e.matmul(ps[bank][:, 0:512], adg[k][ob],
                                           bas[k][ob][:, ic * 512:(ic + 1) * 512],
                                           start=(k == 0), stop=(k == 3))
                        ins.then_inc(S["pe"], 1)
                    tev = C.bump("ev", 1)
                    if E == "v":
                        e.wait_ge(S["pe"], tpe)
                        e.wait_ge(S[ssem], C.val(ssem))
                        e.tensor_copy(wq_stage[slot], ps[bank][:, 0:512]).then_inc(S["ev"], 1)
                    C.set("wqev_%d" % bank, tev)
                    C.bump(ssem, 16)
                    if E == "y":
                        e.wait_ge(S["ev"], tev)
                        e.dma_start(out=wq_own[ob * 128:(ob + 1) * 128, ic * 512:(ic + 1) * 512],
                                    in_=wq_stage[slot]).then_inc(S[ssem], 16)
            t_wq0, t_wq1 = C.val("wqs0"), C.val("wqs1")

            C.bump("cc", 1)
            if E == "g":
                e.wait_ge(S["wqs0"], t_wq0)
                e.wait_ge(S["wqs1"], t_wq1)
                if P0_CC:
                    e.collective_compute(
                        "AllGather", ALU.bypass,
                        replica_groups=[[0, 1, 2, 3, 4, 5, 6, 7]],
                        ins=[wq_own.ap().opt()],
                        outs=[wq_all.ap().opt()],
                    ).then_inc(S["cc"], 1)
                else:
                    e.engine_nop().then_inc(S["cc"], 1)
                e.dma_start(out=wqb_bf[:, :], in_=wqb[:, :]).then_inc(S["wq1"], 16)
                e.dma_start(out=wk_bf[:, :], in_=wk[:, :]).then_inc(S["wq1"], 16)
                e.dma_start(out=wv_bf[:, :], in_=wv[:, :]).then_inc(S["wq1"], 16)
            C.bump("wq1", 48)

        if emit(0):
            run_phase(p0, "p0")

        def p1a(E, e):
            ab = Alloc(ab_t, AB_N)
            af = Alloc(af_t, AF_N)
            CH = 256
            NCH = TOK // CH
            wqaT = [[ab.tile(512) for _ in range(NI)] for _ in range(2)]
            for hg in range(2):
                for i in range(NI):
                    C.bump("wt", 16)
                    if E == "y":
                        if hg == 0 and i == 0:
                            e.wait_ge(S["cwqa"], C.val("cwqa"))
                        e.dma_start(out=wqaT[hg][i],
                                    in_=wqa_bf[hg * 512:(hg + 1) * 512, i * 128:(i + 1) * 128],
                                    transpose=True).then_inc(S["wt"], 16)
            t_wqaT = C.val("wt")

            x_ln = [af.tile(D) for _ in range(2)]
            stat2c = [{k: af.tile(2) for k in ("mean", "var", "rstd", "nmr", "nmean")}
                      for _ in range(2)]
            sq_s = ab.tile(D)
            ln_s = [ab.tile(D) for _ in range(2)]
            xlt = [ab.tile(CH) for _ in range(NI)]
            woT = [[ab.tile(512) for _ in range(HA)] for _ in range(2)]
            sig = [ab.tile(CH) for _ in range(2)]
            retr = [ab.tile(CH) for _ in range(HA)]
            rec_bf = ab.tile(CH)
            bc_sb = ab.tile(CH)
            elu_t = af.tile(CH)
            elu_t2 = af.tile(CH)
            x0r = [af.tile(512) for _ in range(2)]
            x1t = af.tile(D)

            def ln_store(gt, sl, dst, tdv):
                ssem = "lo%d" % sl
                C.bump(ssem, 16)
                if E == "s":
                    e.wait_ge(S["dv"], tdv)
                    e.dma_start(out=dst[gt * 128:(gt + 1) * 128, :], in_=ln_s[sl]).then_inc(S[ssem], 16)

            st1 = {k: af.tile(16) for k in ("mean", "var", "rstd", "nmr", "nmean")}
            for gt in range(16):
                sl = gt % 2
                xsem = "xl%d" % sl
                t_x = C.bump(xsem, 16)
                if E == "s":
                    e.wait_ge(S["dv"], C.get("xlnuse_%d" % sl))
                    e.wait_ge(S["ac"], C.get("xlnuse_a_%d" % sl))
                    e.dma_start(out=x_ln[sl], in_=x0[gt * 128:(gt + 1) * 128, :]).then_inc(S[xsem], 16)
                tdv, tac = ln_stats(E, e, x_ln[sl], sq_s,
                                    st1["mean"][:, gt:gt + 1], st1["var"][:, gt:gt + 1],
                                    waits_v=[(xsem, t_x)], waits_s=[(xsem, t_x)])
                C.set("xlnuse_%d" % sl, tdv)
                C.set("xlnuse_a_%d" % sl, tac)
            chain_dv = ln_chain(E, e, st1["mean"], st1["var"], st1["rstd"],
                                st1["nmr"], st1["nmean"], C.val("dv"), C.val("ac"))
            for gt in range(16):
                sl = gt % 2
                xsem = "xl%d" % sl
                t_x = C.bump(xsem, 16)
                if E == "s":
                    e.wait_ge(S["dv"], C.get("xlnuse_%d" % sl))
                    e.dma_start(out=x_ln[sl], in_=x0[gt * 128:(gt + 1) * 128, :]).then_inc(S[xsem], 16)
                tdv = ln_norm(E, e, x_ln[sl], st1["rstd"][:, gt:gt + 1], st1["nmr"][:, gt:gt + 1],
                              ln_s[sl], chain_dv, waits_v=[(xsem, t_x)],
                              war_v=[("lo%d" % sl, C.get("lnslot_%d" % sl))])
                C.set("xlnuse_%d" % sl, tdv)
                ln_store(gt, sl, ln1_tm, tdv)
                C.set("lnslot_%d" % sl, C.val("lo%d" % sl))
                C.set("ln1w_%d" % (gt // 2), (C.val("lo0"), C.val("lo1")))

            for ch in range(NCH):
                t_lo0, t_lo1 = C.c.get("ln1w_%d" % ch, (0, 0))
                for i in range(NI):
                    C.bump("xlt", 16)
                    if E == "y":
                        if i == 0:
                            e.wait_ge(S["lo0"], t_lo0)
                            e.wait_ge(S["lo1"], t_lo1)
                            e.wait_ge(S["pe"], C.get("xlt_use"))
                            if ch == 0:
                                e.wait_ge(S["wt"], t_wqaT)
                        e.dma_start(out=xlt[i],
                                    in_=ln1_tm[ch * CH:(ch + 1) * CH, i * 128:(i + 1) * 128],
                                    transpose=True).then_inc(S["xlt"], 16)
                t_xlt = C.val("xlt")

                for h in range(HA):
                    hg, hh = h // 4, h % 4
                    bank = h % 2
                    tpe = C.bump("pe", 1)
                    if E == "t":
                        if h == 0:
                            e.wait_ge(S["xlt"], t_xlt)
                        e.wait_ge(S["dv"], C.get("sigdone_%d" % bank))
                        for i in range(NI):
                            ins = e.matmul(ps[bank][:, 0:CH], wqaT[hg][i][:, hh * 128:(hh + 1) * 128],
                                           xlt[i], start=(i == 0), stop=(i == NI - 1))
                        ins.then_inc(S["pe"], 1)
                    C.bump("dv", 1)
                    if E == "v":
                        e.wait_ge(S["pe"], tpe)
                        e.tensor_scalar(elu_t[:, 0:CH], ps[bank][:, 0:CH], bqa_sb[:, h:h + 1], 0.0,
                                        ALU.add, ALU.min).then_inc(S["dv"], 1)
                    tac = C.bump("ac", 1)
                    if E == "s":
                        e.wait_ge(S["dv"], C.val("dv"))
                        e.activation(elu_t2[:, 0:CH], elu_t[:, 0:CH], AF.Exp).then_inc(S["ac"], 1)
                    tsg0 = C.bump("dv", 1)
                    if E == "v":
                        e.wait_ge(S["ac"], tac)
                        e.tensor_scalar(elu_t[:, 0:CH], ps[bank][:, 0:CH], bqa_sb[:, h:h + 1], 0.0,
                                        ALU.add, ALU.max).then_inc(S["dv"], 1)
                    tsig = C.bump("dv", 1)
                    if E == "v":
                        e.wait_ge(S["dv"], tsg0)
                        e.tensor_add(sig[bank], elu_t[:, 0:CH], elu_t2[:, 0:CH]).then_inc(S["dv"], 1)
                    C.set("sigdone_%d" % bank, tsig)

                    tpe2 = C.bump("pe", 1)
                    if E == "t":
                        e.wait_ge(S["dv"], tsig)
                        ins = e.matmul(ps[6][0:1, 0:CH], z_sb[:, h:h + 1], sig[bank],
                                       start=True, stop=True)
                        ins.then_inc(S["pe"], 1)
                    td0 = C.bump("dv", 1)
                    if E == "v":
                        e.wait_ge(S["pe"], tpe2)
                        e.tensor_scalar(elu_t[0:1, 0:CH], ps[6][0:1, 0:CH], 1.0, DEN_EPS,
                                        ALU.mult, ALU.add).then_inc(S["dv"], 1)
                    td1 = C.bump("dv", 1)
                    if E == "v":
                        e.wait_ge(S["dv"], td0)
                        e.reciprocal(elu_t2[0:1, 0:CH], elu_t[0:1, 0:CH]).then_inc(S["dv"], 1)
                    td2 = C.bump("dv", 1)
                    if E == "v":
                        e.wait_ge(S["dv"], td1)
                        e.tensor_copy(rec_bf[0:1, 0:CH], elu_t2[0:1, 0:CH]).then_inc(S["dv"], 1)
                    tpe3 = C.bump("pe", 1)
                    if E == "t":
                        e.wait_ge(S["dv"], td2)
                        e.wait_ge(S["ac"], C.get("bc_use"))
                        ins = e.matmul(ps[7][:, 0:CH], ones_sb[0:1, :], rec_bf[0:1, 0:CH],
                                       start=True, stop=True)
                        ins.then_inc(S["pe"], 1)
                    tac2 = C.bump("ac", 1)
                    if E == "s":
                        e.wait_ge(S["pe"], tpe3)
                        e.wait_ge(S["dv"], C.get("bcs_use"))
                        e.activation(bc_sb, ps[7][:, 0:CH], AF.Copy).then_inc(S["ac"], 1)
                    C.set("bc_use", tac2)
                    tpe4 = C.bump("pe", 1)
                    if E == "t":
                        e.wait_ge(S["dv"], C.get("retr_rd_%d" % bank))
                        ins = e.matmul(ps[bank + 2][:, 0:CH], m_sb[:, h * 128:(h + 1) * 128],
                                       sig[bank], start=True, stop=True)
                        ins.then_inc(S["pe"], 1)
                    tretr = C.bump("dv", 1)
                    if E == "v":
                        e.wait_ge(S["pe"], tpe4)
                        e.wait_ge(S["ac"], tac2)
                        e.wait_ge(S["pe"], C.get("woproj_done"))
                        e.tensor_mul(retr[h], ps[bank + 2][:, 0:CH], bc_sb).then_inc(S["dv"], 1)
                    C.set("retr_rd_%d" % bank, tretr)
                    C.set("bcs_use", tretr)
                    C.set("retr_done", tretr)
                C.set("xlt_use", C.val("pe"))

                for oc in range(HA):
                    buf = oc % 2
                    bsem = "w2a" if buf == 0 else "w2b"
                    for h in range(HA):
                        C.bump(bsem, 16)
                        if E == "y":
                            if h == 0:
                                e.wait_ge(S["pe"], C.get("woTuse_%d" % buf))
                                if ch == 0 and oc < 2:
                                    e.wait_ge(S["cwo"], C.val("cwo"))
                            e.dma_start(out=woT[buf][h],
                                        in_=wo_bf[oc * 512:(oc + 1) * 512, h * 128:(h + 1) * 128],
                                        transpose=True).then_inc(S[bsem], 16)
                    t_wo = C.val(bsem)
                    for ts2 in range(2):
                        gt = ch * 2 + ts2
                        bank = 4 + (ts2 % 2)
                        tpe = C.bump("pe", 1)
                        if E == "t":
                            if ts2 == 0:
                                e.wait_ge(S[bsem], t_wo)
                                if oc == 0:
                                    e.wait_ge(S["dv"], C.get("retr_done"))
                            e.wait_ge(S["dv"], C.get("woev_%d" % bank))
                            for h in range(HA):
                                ins = e.matmul(ps[bank][:, 0:512], retr[h][:, ts2 * 128:(ts2 + 1) * 128],
                                               woT[buf][h], start=(h == 0), stop=(h == HA - 1))
                            ins.then_inc(S["pe"], 1)
                        if ts2 == 1:
                            C.set("woTuse_%d" % buf, tpe)
                        sl = ts2 % 2
                        rsem = "xr%d" % sl
                        t_prev = C.val(rsem)
                        t_x0 = C.bump(rsem, 16)
                        if E == "s":
                            e.wait_ge(S["dv"], C.get("x0r_use_%d" % sl))
                            e.wait_ge(S[rsem], t_prev)
                            e.dma_start(out=x0r[sl], in_=x0[gt * 128:(gt + 1) * 128,
                                                           oc * 512:(oc + 1) * 512]).then_inc(S[rsem], 16)
                        tadd = C.bump("dv", 1)
                        if E == "v":
                            e.wait_ge(S["pe"], tpe)
                            e.wait_ge(S[rsem], t_x0)
                            e.tensor_add(x0r[sl], ps[bank][:, 0:512], x0r[sl]).then_inc(S["dv"], 1)
                        C.set("x0r_use_%d" % sl, tadd)
                        C.set("woev_%d" % bank, tadd)
                        C.bump(rsem, 16)
                        if E == "s":
                            e.wait_ge(S["dv"], tadd)
                            e.dma_start(out=x1[gt * 128:(gt + 1) * 128, oc * 512:(oc + 1) * 512],
                                        in_=x0r[sl]).then_inc(S[rsem], 16)
                C.set("woproj_done", C.val("pe"))

                st2 = stat2c[ch % 2]
                for tt in range(2):
                    gt = ch * 2 + tt
                    t_xr0, t_xr1 = C.val("xr0"), C.val("xr1")
                    sl = gt % 2
                    xsem = "xl%d" % sl
                    t_x1 = C.bump(xsem, 16)
                    if E == "s":
                        e.wait_ge(S["xr0"], t_xr0)
                        e.wait_ge(S["xr1"], t_xr1)
                        e.wait_ge(S["dv"], C.get("xlnuse_%d" % sl))
                        e.wait_ge(S["ac"], C.get("xlnuse_a_%d" % sl))
                        e.dma_start(out=x_ln[sl], in_=x1[gt * 128:(gt + 1) * 128, :]).then_inc(S[xsem], 16)
                    tdv, tac = ln_stats(E, e, x_ln[sl], sq_s,
                                        st2["mean"][:, tt:tt + 1], st2["var"][:, tt:tt + 1],
                                        waits_v=[(xsem, t_x1)], waits_s=[(xsem, t_x1)])
                    C.set("xlnuse_%d" % sl, tdv)
                    C.set("xlnuse_a_%d" % sl, tac)
                chain2 = ln_chain(E, e, st2["mean"], st2["var"], st2["rstd"],
                                  st2["nmr"], st2["nmean"], C.val("dv"), C.val("ac"))
                for tt in range(2):
                    gt = ch * 2 + tt
                    sl = gt % 2
                    xsem = "xl%d" % sl
                    t_x1 = C.bump(xsem, 16)
                    if E == "s":
                        e.wait_ge(S["dv"], C.get("xlnuse_%d" % sl))
                        e.dma_start(out=x_ln[sl], in_=x1[gt * 128:(gt + 1) * 128, :]).then_inc(S[xsem], 16)
                    tdv = ln_norm(E, e, x_ln[sl], st2["rstd"][:, tt:tt + 1], st2["nmr"][:, tt:tt + 1],
                                  ln_s[sl], chain2, waits_v=[(xsem, t_x1)],
                                  war_v=[("lo%d" % sl, C.get("lnslot_%d" % sl))])
                    C.set("xlnuse_%d" % sl, tdv)
                    ln_store(gt, sl, ln2_tm, tdv)
                    C.set("lnslot_%d" % sl, C.val("lo%d" % sl))

            if debug:
                C.bump("dbgs", 16)
                if E == "g":
                    e.wait_ge(S["xr0"], C.val("xr0"))
                    e.wait_ge(S["xr1"], C.val("xr1"))
                    e.dma_start(out=dbg["d_x1"][:, :], in_=x1[:, :]).then_inc(S["dbgs"], 16)

        if emit(1):
            run_phase(p1a, "p1a")

        def ffn_pass(E, e, lnsrc):
            ab = Alloc(ab_t, AB_N)
            xlt2 = [ab.tile(1024) for _ in range(NI)]
            wtiles = [[ab.tile(512) for _ in range(NI)] for _ in range(2)]
            evst = [ab.tile(512) for _ in range(4)]
            t_srcs = [(sm, C.val(sm)) for sm in ("lo0", "lo1", "ro0", "ro1")]
            for th in range(2):
                for i in range(NI):
                    C.bump("xlt", 16)
                    if E == "y":
                        if i == 0:
                            if th == 0:
                                for sm, vv in t_srcs:
                                    if vv > 0:
                                        e.wait_ge(S[sm], vv)
                            e.wait_ge(S["pe"], C.get("fx_use"))
                        e.dma_start(out=xlt2[i],
                                    in_=lnsrc[th * 1024:(th + 1) * 1024, i * 128:(i + 1) * 128],
                                    transpose=True).then_inc(S["xlt"], 16)
                t_xl = C.val("xlt")
                for obg in range(8):
                    buf = obg % 2
                    bsem = "fwa" if buf == 0 else "fwb"
                    for i in range(NI):
                        C.bump(bsem, 16)
                        if E == "y":
                            if i == 0:
                                e.wait_ge(S["pe"], C.get("fw_use_%d" % buf))
                                if th == 0 and obg == 0:
                                    e.wait_ge(S["cc"], 1)
                            e.dma_start(out=wtiles[buf][i],
                                        in_=wq_all[obg * 512:(obg + 1) * 512, i * 128:(i + 1) * 128],
                                        transpose=True).then_inc(S[bsem], 16)
                    t_w = C.val(bsem)
                    for ob4 in range(4):
                        ob = obg * 4 + ob4
                        bank = (ob4 % 2) * 2
                        tpe = C.bump("pe", 1)
                        if E == "t":
                            if ob4 == 0:
                                e.wait_ge(S[bsem], t_w)
                                e.wait_ge(S["xlt"], t_xl)
                            e.wait_ge(S["ev"], C.get("fev_%d" % bank))
                            for i in range(NI):
                                for tc in range(2):
                                    ins = e.matmul(ps[bank + tc][:, 0:512],
                                                   wtiles[buf][i][:, ob4 * 128:(ob4 + 1) * 128],
                                                   xlt2[i][:, tc * 512:(tc + 1) * 512],
                                                   start=(i == 0), stop=(i == NI - 1))
                            ins.then_inc(S["pe"], 1)
                        if ob4 == 3:
                            C.set("fw_use_%d" % buf, tpe)
                            if obg == 7:
                                C.set("fx_use", tpe)
                        for tc in range(2):
                            sl = (ob4 % 2) * 2 + tc
                            ssem = "ev%d" % sl
                            tev = C.bump("ev", 1)
                            if E == "v":
                                e.wait_ge(S["pe"], tpe)
                                e.wait_ge(S[ssem], C.val(ssem))
                                e.tensor_copy(evst[sl], ps[bank + tc][:, 0:512]).then_inc(S["ev"], 1)
                            C.bump(ssem, 16)
                            if E == "s":
                                e.wait_ge(S["ev"], tev)
                                e.dma_start(out=ffn_fm[ob * 128:(ob + 1) * 128,
                                                       th * 1024 + tc * 512: th * 1024 + (tc + 1) * 512],
                                            in_=evst[sl]).then_inc(S[ssem], 16)
                        C.set("fev_%d" % bank, C.val("ev"))

        if emit(2):
            run_phase(lambda E, e: ffn_pass(E, e, ln2_tm), "p1b")

        def resid_ln(E, e, xin, xout, lnout, is_final, xin_sems, ag_cb=None):
            ab = Alloc(ab_t, AB_N)
            af = Alloc(af_t, AF_N)
            xa = [af.tile(D) for _ in range(2)]
            ft = [ab.tile(D) for _ in range(2)]
            ln_s = [ab.tile(D) for _ in range(2)]
            sq_s = ab.tile(D)
            stb = {k: af.tile(16) for k in ("mean", "var", "rstd", "nmr", "nmean")}
            t_ev = [(sm, C.val(sm)) for sm in ("ev0", "ev1", "ev2", "ev3")]
            t_xin = [(sm, C.val(sm)) for sm in xin_sems]
            st_sem = "st" if is_final else "xs2"
            for tt in range(16):
                sl = tt % 2
                asem = "ra%d" % sl
                fsem = "rf%d" % sl
                osem = "ro%d" % sl
                t_x = C.bump(asem, 16)
                if E == "s":
                    if tt < 2:
                        for sm, vv in t_xin:
                            if vv > 0:
                                e.wait_ge(S[sm], vv)
                    e.wait_ge(S["dv"], C.get("rl_xa_%d" % sl))
                    e.wait_ge(S["ac"], C.get("rl_xaa_%d" % sl))
                    if C.get("rl_xo_%d" % sl) > 0:
                        e.wait_ge(S[osem], C.get("rl_xo_%d" % sl))
                    e.dma_start(out=xa[sl], in_=xin[tt * 128:(tt + 1) * 128, :]).then_inc(S[asem], 16)
                t_f = C.bump(fsem, 16)
                if E == "y":
                    if tt < 2:
                        for sm, vv in t_ev:
                            if vv > 0:
                                e.wait_ge(S[sm], vv)
                    e.wait_ge(S["dv"], C.get("rl_ft_%d" % sl))
                    e.dma_start(out=ft[sl], in_=ffn_fm[0:D, tt * 128:(tt + 1) * 128],
                                transpose=True).then_inc(S[fsem], 16)
                tadd = C.bump("dv", 1)
                if E == "v":
                    e.wait_ge(S[asem], t_x)
                    e.wait_ge(S[fsem], t_f)
                    e.tensor_add(xa[sl], xa[sl], ft[sl]).then_inc(S["dv"], 1)
                C.set("rl_ft_%d" % sl, tadd)
                C.bump(osem, 16)
                if E == "s":
                    e.wait_ge(S["dv"], tadd)
                    e.dma_start(out=xout[tt * 128:(tt + 1) * 128, :], in_=xa[sl]).then_inc(S[osem], 16)
                C.set("rl_xo_%d" % sl, C.val(osem))
                if lnout is not None:
                    tdv, tac = ln_stats(E, e, xa[sl], sq_s,
                                        stb["mean"][:, tt:tt + 1], stb["var"][:, tt:tt + 1],
                                        waits_v=[("dv", tadd)], waits_s=[("dv", tadd)])
                    C.set("rl_xa_%d" % sl, tdv)
                    C.set("rl_xaa_%d" % sl, tac)
                else:
                    C.set("rl_xa_%d" % sl, tadd)
            if lnout is not None:
                chain_dv = ln_chain(E, e, stb["mean"], stb["var"], stb["rstd"],
                                    stb["nmr"], stb["nmean"], C.val("dv"), C.val("ac"))
                t_ro_all = [(sm, C.val(sm)) for sm in ("ro0", "ro1")]
                for tt in range(16):
                    sl = tt % 2
                    asem = "ra%d" % sl
                    osem = "ro%d" % sl
                    t_x = C.bump(asem, 16)
                    if E == "s":
                        if tt < 2:
                            for sm, vv in t_ro_all:
                                if vv > 0:
                                    e.wait_ge(S[sm], vv)
                        e.wait_ge(S["dv"], C.get("rl_xa_%d" % sl))
                        if C.get("rl_lo_%d" % sl) > 0:
                            e.wait_ge(S[osem], C.get("rl_lo_%d" % sl))
                        e.dma_start(out=xa[sl], in_=xout[tt * 128:(tt + 1) * 128, :]).then_inc(S[asem], 16)
                    tdv = ln_norm(E, e, xa[sl], stb["rstd"][:, tt:tt + 1], stb["nmr"][:, tt:tt + 1],
                                  ln_s[sl], chain_dv, waits_v=[(asem, t_x)],
                                  war_v=[(osem, C.get("rl_lo_%d" % sl))])
                    C.set("rl_xa_%d" % sl, tdv)
                    C.bump(osem, 16)
                    if E == "s":
                        e.wait_ge(S["dv"], tdv)
                        e.dma_start(out=lnout[tt * 128:(tt + 1) * 128, :], in_=ln_s[sl]).then_inc(S[osem], 16)
                    C.set("rl_lo_%d" % sl, C.val(osem))
                    if ag_cb is not None and tt % 4 == 3:
                        ag_cb(tt // 4)
            if is_final:
                t_ro0, t_ro1 = C.val("ro0"), C.val("ro1")
                C.bump("fin", 1)
                if E == "g":
                    e.wait_ge(S["ro0"], t_ro0)
                    e.wait_ge(S["ro1"], t_ro1)
                    if C.val("dbgs"):
                        e.wait_ge(S["dbgs"], C.val("dbgs"))
                    e.engine_nop().then_inc(S["fin"], 1)

        def p1c(E, e):
            def ln3_ag(E2, e2, q):
                t_ro0, t_ro1 = C.val("ro0"), C.val("ro1")
                C.bump("cc", 1)
                if E2 == "g":
                    e2.wait_ge(S["ro0"], t_ro0)
                    e2.wait_ge(S["ro1"], t_ro1)
                    if SIM_NOCC:
                        e2.engine_nop().then_inc(S["cc"], 1)
                    else:
                        e2.collective_compute(
                            "AllGather", ALU.bypass,
                            replica_groups=[[0, 1], [2, 3], [4, 5], [6, 7]],
                            ins=[ln3_own[q * 512:(q + 1) * 512, :]],
                            outs=[ln3_full[q * 1024:(q + 1) * 1024, :]],
                        ).then_inc(S["cc"], 1)

            resid_ln(E, e, x1, x2, ln3_own, False, ("xr0", "xr1"),
                     ag_cb=lambda q: ln3_ag(E, e, q))
            if debug:
                if E == "g":
                    e.dma_start(out=dbg["d_x2"][:, :], in_=x2[:, :]).then_inc(S["dbgs"], 16)
                    e.wait_ge(S["cc"], C.val("cc"))
                    e.dma_start(out=dbg["d_ln3"][:, :], in_=ln3_full[:, :]).then_inc(S["dbgs"], 16)
                    e.dma_start(out=dbg["d_wq"][:, :], in_=wq_all[:, :]).then_inc(S["dbgs"], 16)
                C.bump("dbgs", 48)

        if emit(3):
            run_phase(p1c, "p1c")

        def p1d(E, e):
            ab = Alloc(ab_t, AB_N)
            xlt3 = [ab.tile(1024) for _ in range(NI)]
            wtl_base = ab.off
            wtl = [[ab.tile(256) for _ in range(NI)] for _ in range(2)]
            qst = [ab.tile(512) for _ in range(8)]
            wvt = [ab_t[:, wtl_base + i * 512: wtl_base + (i + 1) * 512] for i in range(NI)]
            qsems = ["q0", "q1", "q2s", "q3", "q4", "q5", "q6", "q7"]
            cc_base = C.val("cc") - 4
            for qt4 in range(4):
                for i in range(NI):
                    C.bump("xlt", 16)
                    if E == "y":
                        if i == 0:
                            e.wait_ge(S["cc"], cc_base + qt4 + 1)
                            e.wait_ge(S["pe"], C.get("p1d_x_use"))
                        e.dma_start(out=xlt3[i],
                                    in_=ln3_full[qt4 * 1024:(qt4 + 1) * 1024, i * 128:(i + 1) * 128],
                                    transpose=True).then_inc(S["xlt"], 16)
                t_xl = C.val("xlt")
                for wname, wsrc, n_sb, bias_sb, dstT in (
                    ("q", wqb_bf, 4, bq_sb, qT_d),
                    ("k", wk_bf, 2, bk_sb, kT_d),
                ):
                    for sb in range(n_sb):
                        for dbp in range(2):
                            buf = (sb * 2 + dbp) % 2
                            bsem = "wta" if buf == 0 else "wtb"
                            for i in range(NI):
                                C.bump(bsem, 16)
                                if E == "y":
                                    if i == 0:
                                        e.wait_ge(S["wq1"], 48)
                                        e.wait_ge(S["pe"], C.get("p1d_w_use_%d" % buf))
                                        e.wait_ge(S["pe"], C.get("p1d_wv_use"))
                                    e.dma_start(out=wtl[buf][i],
                                                in_=wsrc[sb * 512 + dbp * 256: sb * 512 + (dbp + 1) * 256,
                                                         i * 128:(i + 1) * 128],
                                                transpose=True).then_inc(S[bsem], 16)
                            t_wl = C.val(bsem)
                            grp = (sb * 2 + dbp) % 2
                            pb = grp * 4
                            tpe = C.bump("pe", 1)
                            if E == "t":
                                e.wait_ge(S[bsem], t_wl)
                                if sb == 0 and dbp == 0 and wname == "q":
                                    e.wait_ge(S["xlt"], t_xl)
                                    e.wait_ge(S["dv"], C.get("p1d_vev_all"))
                                e.wait_ge(S["ac"], C.get("p1d_ev_%d" % grp))
                                for i in range(NI):
                                    for db2 in range(2):
                                        for tc in range(2):
                                            e.matmul(ps[pb + db2 * 2 + tc][:, 0:512],
                                                     wtl[buf][i][:, db2 * 128:(db2 + 1) * 128],
                                                     xlt3[i][:, tc * 512:(tc + 1) * 512],
                                                     start=(i == 0), stop=False)
                                for db2 in range(2):
                                    db_ = sb * 4 + dbp * 2 + db2
                                    for tc in range(2):
                                        ins = e.matmul(ps[pb + db2 * 2 + tc][:, 0:512],
                                                       bias_sb[0:1, db_ * 128:(db_ + 1) * 128],
                                                       ones512[0:1, :],
                                                       start=False, stop=True)
                                ins.then_inc(S["pe"], 1)
                            C.set("p1d_w_use_%d" % buf, tpe)
                            for db2 in range(2):
                                db = sb * 4 + dbp * 2 + db2
                                for tc in range(2):
                                    slq = grp * 4 + db2 * 2 + tc
                                    ssem = qsems[slq]
                                    tev = C.bump("ac", 1)
                                    if E == "s":
                                        e.wait_ge(S["pe"], tpe)
                                        e.wait_ge(S[ssem], C.val(ssem))
                                        e.activation(qst[slq], ps[pb + db2 * 2 + tc][:, 0:512],
                                                     AF.Copy).then_inc(S["ac"], 1)
                                    C.set("p1d_ev_%d" % grp, tev)
                                    cbase = tc * 2048 + qt4 * 512
                                    C.bump(ssem, 16)
                                    if E == "s":
                                        e.wait_ge(S["ac"], tev)
                                        e.dma_start(out=dstT[db * 128:(db + 1) * 128,
                                                             cbase: cbase + 512],
                                                    in_=qst[slq]).then_inc(S[ssem], 16)
                for dc in range(2):
                    for i in range(NI):
                        C.bump("wv2", 16)
                        if E == "y":
                            if i == 0:
                                e.wait_ge(S["pe"], C.get("p1d_w_use_0"))
                                e.wait_ge(S["pe"], C.get("p1d_w_use_1"))
                                e.wait_ge(S["pe"], C.get("p1d_wv_use"))
                            e.dma_start(out=wvt[i],
                                        in_=wv_bf[dc * 512:(dc + 1) * 512, i * 128:(i + 1) * 128],
                                        transpose=True).then_inc(S["wv2"], 16)
                    t_wv = C.val("wv2")
                    for tt in range(8):
                        bank = tt % 4
                        tpe = C.bump("pe", 1)
                        if E == "t":
                            if tt == 0:
                                e.wait_ge(S["wv2"], t_wv)
                                if dc == 0:
                                    e.wait_ge(S["ac"], C.get("p1d_ev_0"))
                                    e.wait_ge(S["ac"], C.get("p1d_ev_1"))
                            e.wait_ge(S["dv"], C.get("p1d_vev_%d" % bank))
                            for i in range(NI):
                                e.matmul(ps[bank][:, 0:512],
                                         xlt3[i][:, tt * 128:(tt + 1) * 128],
                                         wvt[i], start=(i == 0), stop=False)
                            ins = e.matmul(ps[bank][:, 0:512], ones_sb[0:1, 0:128],
                                           bvr_sb[0:1, dc * 512:(dc + 1) * 512],
                                           start=False, stop=True)
                            ins.then_inc(S["pe"], 1)
                        if tt == 7:
                            C.set("p1d_wv_use", tpe)
                        slv = 4 + bank
                        ssem = qsems[slv]
                        tev = C.bump("dv", 1)
                        if E == "v":
                            e.wait_ge(S["pe"], tpe)
                            e.wait_ge(S[ssem], C.val(ssem))
                            e.tensor_copy(qst[slv], ps[bank][:, 0:512]).then_inc(S["dv"], 1)
                        C.set("p1d_vev_%d" % bank, tev)
                        rbase = (tt // 4) * 2048 + qt4 * 512 + (tt % 4) * 128
                        C.bump(ssem, 16)
                        if E == "s":
                            e.wait_ge(S["dv"], tev)
                            e.dma_start(out=v_d[rbase: rbase + 128,
                                                dc * 512:(dc + 1) * 512],
                                        in_=qst[slv]).then_inc(S[ssem], 16)
                C.set("p1d_x_use", C.val("pe"))
                C.set("p1d_vev_all", C.val("dv"))
            if debug:
                C.bump("dbgs", 48)
                if E == "g":
                    for sm in qsems:
                        if C.val(sm) > 0:
                            e.wait_ge(S[sm], C.val(sm))
                    e.dma_start(out=dbg["d_qT"][:, :], in_=qT_d[:, :]).then_inc(S["dbgs"], 16)
                    e.dma_start(out=dbg["d_kT"][:, :], in_=kT_d[:, :]).then_inc(S["dbgs"], 16)
                    e.dma_start(out=dbg["d_v"][:, :], in_=v_d[:, :]).then_inc(S["dbgs"], 16)

        if emit(4):
            run_phase(p1d, "p1d")

        def p2(E, e):
            ab = Alloc(ab_t, AB_N)
            af = Alloc(af_t, AF_N)
            kts = [ab.tile(T) for _ in range(2)]
            qts = [ab.tile(T) for _ in range(2)]
            v129 = [ab.tile(NI * 129) for _ in range(2)]
            pt_s = [ab.tile(512) for _ in range(2)]
            att_s = [ab.tile(128) for _ in range(4)]
            mxb = ab.tile(4)
            mxcol = af.tile(8)
            mxt = af.tile(256)
            bia = af.tile(2)
            recs = af.tile(4)
            qsems = ["q0", "q1", "q2s", "q3", "q4", "q5", "q6", "q7"]
            t_qkv = [(sm, C.val(sm)) for sm in qsems]
            t_mxinit = C.bump("dv", 1)
            if E == "v":
                e.memset(mxt, -1e30).then_inc(S["dv"], 1)
            C.bump("po", 1)
            if E == "g":
                for b in range(2):
                    vv = v129[b].rearrange("p (a b) -> p a b", b=129)
                    e.memset(vv[:, :, 128:129], 1.0)
                e.engine_nop().then_inc(S["po"], 1)
            t_ones = C.val("po")

            for kk in range(P2_KK):
                kb = kk % 2
                ksem = "k2a" if kb == 0 else "k2b"
                C.bump(ksem, 32)
                if E == "y":
                    e.wait_ge(S["pe"], C.get("p2_kv_use_%d" % kb))
                    if kk < 2:
                        e.wait_ge(S["po"], t_ones)
                        for sm, vv in t_qkv:
                            if vv > 0:
                                e.wait_ge(S[sm], vv)
                    e.dma_start(out=kts[kb], in_=kT_d[kk * 128:(kk + 1) * 128, :]).then_inc(S[ksem], 16)
                    vv = v129[kb].rearrange("p (a b) -> p a b", b=129)
                    e.dma_start(out=vv[:, :, 0:128],
                                in_=v_d.ap().rearrange("(a p) c -> p a c", p=128)[:, :, kk * 128:(kk + 1) * 128],
                                ).then_inc(S[ksem], 16)
                t_kv = C.val(ksem)
                for j in range(2):
                    qb = j % 2
                    qsem = "q2a" if qb == 0 else "q2b"
                    lh = j * 8 + kk
                    C.bump(qsem, 16)
                    if E == "y":
                        e.wait_ge(S["pe"], C.get("p2_q_use_%d" % qb))
                        e.dma_start(out=qts[qb], in_=qT_d[lh * 128:(lh + 1) * 128, :]).then_inc(S[qsem], 16)
                    t_q = C.val(qsem)
                    for qt in range(32):
                        nlc = qt // 4 + 1
                        for lc in range(nlc):
                            bank = 6 + lc % 2
                            tpe = C.bump("pe", 1)
                            if E == "t":
                                if qt == 0 and lc == 0:
                                    e.wait_ge(S[ksem], t_kv)
                                    e.wait_ge(S[qsem], t_q)
                                    e.wait_ge(S["dv"], C.val("dv"))
                                e.wait_ge(S["mx"], C.get("p2_psA_%d" % bank))
                                ins = e.matmul(ps[bank][:, 0:512], qts[qb][:, qt * 128:(qt + 1) * 128],
                                               kts[kb][:, lc * 512:(lc + 1) * 512],
                                               start=True, stop=True)
                                ins.then_inc(S["pe"], 1)
                            tmx = C.bump("mx", 1)
                            if E == "v":
                                e.wait_ge(S["pe"], tpe)
                                if qt == 0 and lc == 0:
                                    e.wait_ge(S["dv"], t_mxinit)
                                e.reduce_max(mxt[:, qt * 8 + lc: qt * 8 + lc + 1],
                                             ps[bank][:, 0:512:4], axis=AX.X).then_inc(S["mx"], 1)
                            C.set("p2_psA_%d" % bank, tmx)
                    def chain_step(c, step):
                        cb2 = c % 2
                        if step == 0:
                            tc0 = C.bump("dv", 1)
                            if E == "v":
                                e.wait_ge(S["mx"], C.val("mx"))
                                e.wait_ge(S["pe"], C.get("p2_mxb_use"))
                                e.wait_ge(S["ac"], C.get("p2_bia_use_%d" % cb2))
                                e.reduce_max(mxcol[:, cb2:cb2 + 1], mxt[:, 32 * c:32 * c + 32],
                                             axis=AX.X).then_inc(S["dv"], 1)
                            C.set("p2_ch0_%d" % cb2, tc0)
                        elif step == 1:
                            tc1 = C.bump("dv", 1)
                            if E == "v":
                                e.wait_ge(S["dv"], C.get("p2_ch0_%d" % cb2))
                                e.tensor_copy(mxb[:, cb2:cb2 + 1], mxcol[:, cb2:cb2 + 1]).then_inc(S["dv"], 1)
                            C.set("p2_ch1_%d" % cb2, tc1)
                        elif step == 2:
                            tpe_ = C.bump("pe", 1)
                            if E == "t":
                                e.wait_ge(S["dv"], C.get("p2_ch1_%d" % cb2))
                                e.wait_ge(S["dv"], C.get("p2_ch4rd"))
                                ins_ = e.matmul(ps[6][0:1, 0:128], mxb[:, cb2:cb2 + 1], ident_sb[:, :],
                                                start=True, stop=True)
                                ins_.then_inc(S["pe"], 1)
                            C.set("p2_ch2_%d" % cb2, tpe_)
                        elif step == 3:
                            tc2 = C.bump("dv", 1)
                            if E == "v":
                                e.wait_ge(S["pe"], C.get("p2_ch2_%d" % cb2))
                                e.reduce_max(mxcol[0:1, 2 + cb2:3 + cb2], ps[6][0:1, 0:128],
                                             axis=AX.X).then_inc(S["dv"], 1)
                            C.set("p2_ch3_%d" % cb2, C.val("dv"))
                            tc3 = C.bump("dv", 1)
                            if E == "v":
                                e.wait_ge(S["dv"], C.get("p2_ch3_%d" % cb2))
                                e.tensor_copy(mxb[0:1, 2 + cb2:3 + cb2],
                                              mxcol[0:1, 2 + cb2:3 + cb2]).then_inc(S["dv"], 1)
                            C.set("p2_ch4rd", tc3)
                            C.set("p2_ch3b_%d" % cb2, tc3)
                        elif step == 4:
                            tpe2_ = C.bump("pe", 1)
                            if E == "t":
                                e.wait_ge(S["dv"], C.get("p2_ch3b_%d" % cb2))
                                ins_ = e.matmul(ps[6][:, 256 + cb2:257 + cb2], nsrow_sb[0:1, :],
                                                mxb[0:1, 2 + cb2:3 + cb2], start=True, stop=True)
                                ins_.then_inc(S["pe"], 1)
                            C.set("p2_mxb_use", tpe2_)
                            C.set("p2_ch5_%d" % cb2, tpe2_)
                        elif step == 5:
                            tbia_ = C.bump("dv", 1)
                            if E == "v":
                                e.wait_ge(S["pe"], C.get("p2_ch5_%d" % cb2))
                                e.tensor_copy(bia[:, cb2:cb2 + 1],
                                              ps[6][:, 256 + cb2:257 + cb2]).then_inc(S["dv"], 1)
                            C.set("p2_tbia_%d" % cb2, tbia_)

                    for st_ in range(6):
                        chain_step(P2_C0, st_)
                    for c in range(P2_C0, P2_NC):
                        cb = c % 2
                        tbia = C.get("p2_tbia_%d" % cb)
                        nlt = 4 * c + 4

                        def st_mm(lt):
                            bank = lt % 2
                            t_ = C.bump("pe", 1)
                            if E == "t":
                                e.wait_ge(S["ac"], C.get("p2_psT_%d" % bank))
                                ins_ = e.matmul(ps[2 + bank][:, 0:512],
                                                kts[kb][:, lt * 128:(lt + 1) * 128],
                                                qts[qb][:, c * 512:(c + 1) * 512],
                                                start=True, stop=True)
                                ins_.then_inc(S["pe"], 1)
                            return t_

                        tpe3_cur = st_mm(0)
                        for lt in range(nlt):
                            bank = lt % 2
                            slp = lt % 2
                            tex = C.bump("ac", 1)
                            if E == "s":
                                e.wait_ge(S["pe"], tpe3_cur)
                                if lt == 0:
                                    e.wait_ge(S["dv"], tbia)
                                e.wait_ge(S["pe"], C.get("p2_pt_use_%d" % slp))
                                e.wait_ge(S["po"], C.get("p2_msk_%d" % slp))
                                e.activation(pt_s[slp], ps[2 + bank][:, 0:512], AF.Exp,
                                             bias=bia[:, cb:cb + 1], scale=SCALE).then_inc(S["ac"], 1)
                            C.set("p2_psT_%d" % bank, tex)
                            if c + 1 < P2_NC and 1 <= lt <= 6:
                                chain_step(c + 1, lt - 1)
                            tmask = None
                            if lt >= 4 * c:
                                s_idx = lt - 4 * c
                                tmask = C.bump("po", 1)
                                if E == "g":
                                    e.wait_ge(S["ac"], tex)
                                    e.tensor_mul(pt_s[slp], pt_s[slp],
                                                 mask_sb[:, s_idx * 512:(s_idx + 1) * 512]).then_inc(S["po"], 1)
                                C.set("p2_msk_%d" % slp, tmask)
                            tpe3_next = st_mm(lt + 1) if lt + 1 < nlt else None
                            tpe4 = C.bump("pe", 1)
                            if E == "t":
                                if tmask is not None:
                                    e.wait_ge(S["po"], tmask)
                                else:
                                    e.wait_ge(S["ac"], tex)
                                if lt == 0:
                                    for s2 in range(4):
                                        e.wait_ge(S["ac"], C.get("p2_avb_%d" % s2))
                                        e.wait_ge(S["dv"], C.get("p2_avd_%d" % s2))
                                for s2 in range(4):
                                    avb = (4, 5, 0, 1)[s2]
                                    ins = e.matmul(ps[avb][:, 0:129],
                                                   pt_s[slp][:, s2 * 128:(s2 + 1) * 128],
                                                   v129[kb][:, lt * 129:(lt + 1) * 129],
                                                   start=(lt == 0), stop=(lt == nlt - 1))
                                ins.then_inc(S["pe"], 1)
                            C.set("p2_pt_use_%d" % slp, tpe4)
                            tpe3_cur = tpe3_next
                        if c + 1 < P2_NC and nlt < 7:
                            for st_ in range(max(0, nlt - 1), 6):
                                chain_step(c + 1, st_)
                        tpe_av = C.val("pe")
                        for s2 in range(4):
                            avb = (4, 5, 0, 1)[s2]
                            pcol = ps[avb][:, 128:129]
                            pmain = ps[avb][:, 0:128]
                            asem = "at%d" % s2
                            trc = C.bump("dv", 1)
                            if E == "v":
                                e.wait_ge(S["pe"], tpe_av)
                                e.wait_ge(S["ac"], C.get("p2_rc_use_%d" % s2))
                                e.reciprocal(recs[:, s2:s2 + 1], pcol).then_inc(S["dv"], 1)
                            C.set("p2_avd_%d" % s2, trc)
                            tav = C.bump("ac", 1)
                            if E == "s":
                                e.wait_ge(S["dv"], trc)
                                e.wait_ge(S[asem], C.get("p2_att_%d" % s2))
                                e.activation(att_s[s2], pmain, AF.Copy,
                                             scale=recs[:, s2:s2 + 1]).then_inc(S["ac"], 1)
                            C.set("p2_avb_%d" % s2, tav)
                            C.set("p2_rc_use_%d" % s2, tav)
                            C.bump(asem, 16)
                            if E == "y":
                                e.wait_ge(S["ac"], tav)
                                e.dma_start(out=attn_own[c * 512 + s2 * 128: c * 512 + (s2 + 1) * 128,
                                                         lh * 128:(lh + 1) * 128],
                                            in_=att_s[s2]).then_inc(S[asem], 16)
                            C.set("p2_att_%d" % s2, C.val(asem))
                    C.set("p2_q_use_%d" % qb, C.val("pe"))
                C.set("p2_kv_use_%d" % kb, C.val("pe"))

            if P2_CC:
                t_at = [(("at%d" % s2), C.val("at%d" % s2)) for s2 in range(4)]
                C.bump("cc", 1)
                if E == "g":
                    for sm, vv in t_at:
                        if vv > 0:
                            e.wait_ge(S[sm], vv)
                    if SIM_NOCC:
                        e.engine_nop().then_inc(S["cc"], 1)
                    else:
                        e.collective_compute(
                            "AllGather", ALU.bypass,
                            replica_groups=[[0, 1], [2, 3], [4, 5], [6, 7]],
                            ins=[attn_own.ap().opt()],
                            outs=[attn_full.ap().opt()],
                        ).then_inc(S["cc"], 1)
                    if debug:
                        e.dma_start(out=dbg["d_attn"][:, :], in_=attn_full[:, :]).then_inc(S["dbgs"], 16)
                if debug:
                    C.bump("dbgs", 16)

        if emit(5):
            run_phase(p2, "p2")

        def p3a(E, e):
            ab = Alloc(ab_t, AB_N)
            af = Alloc(af_t, AF_N)
            xa = [af.tile(D) for _ in range(2)]
            at = [ab.tile(2048) for _ in range(8)]
            ln_s = [ab.tile(D) for _ in range(2)]
            sq_s = ab.tile(D)
            stb = {k: af.tile(16) for k in ("mean", "var", "rstd", "nmr", "nmean")}
            t_x2w = [(sm, C.val(sm)) for sm in ("ro0", "ro1")]
            t_cc3 = C.val("cc")
            C.bump("po", 1)
            if E == "g":
                e.wait_ge(S["cc"], t_cc3)
                e.engine_nop().then_inc(S["po"], 1)
            t_rdy = C.val("po")

            dv0 = C.val("dv")

            def add_dv_of(tt):
                return dv0 + 2 * tt + 1

            def emit_at_loads(g, base):
                for tt in range(16):
                    sl = tt % 4
                    if tt >= 4:
                        g.wait_ge(S["dv"], add_dv_of(tt - 4))
                    for r in range(2):
                        g.dma_start(out=at[r * 4 + sl],
                                    in_=attn_full[r * T + base + tt * 128: r * T + base + (tt + 1) * 128, :],
                                    ).then_inc(S["pa%d" % (r * 4 + sl)], 16)

            if E == "g":
                e.wait_ge(S["po"], t_rdy)
                if no_branch:
                    emit_at_loads(e, 0)
                else:
                    with e.register("rid") as rid:
                        e.reg_load(rid, cid_sb[0:1, 0:1])
                        with e.If_eq(rid, 0):
                            emit_at_loads(e, 0)
                        with e.If_ne(rid, 0):
                            emit_at_loads(e, TOK)
            pa_base = [C.val("pa%d" % i) for i in range(8)]
            for tt in range(16):
                for r in range(2):
                    C.bump("pa%d" % (r * 4 + tt % 4), 16)

            for tt in range(16):
                sl = tt % 2
                slat = tt % 4
                asem = "ra%d" % sl
                osem = "ro%d" % sl
                t_x = C.bump(asem, 16)
                if E == "s":
                    if tt < 2:
                        for sm, vv in t_x2w:
                            if vv > 0:
                                e.wait_ge(S[sm], vv)
                    e.wait_ge(S["dv"], C.get("p3_xa_%d" % sl))
                    e.wait_ge(S["ac"], C.get("p3_xaa_%d" % sl))
                    if C.get("p3_xo_%d" % sl) > 0:
                        e.wait_ge(S[osem], C.get("p3_xo_%d" % sl))
                    e.dma_start(out=xa[sl], in_=x2[tt * 128:(tt + 1) * 128, :]).then_inc(S[asem], 16)
                tadd = C.bump("dv", 1)
                assert tadd == add_dv_of(tt)
                if E == "v":
                    e.wait_ge(S[asem], t_x)
                    for r in range(2):
                        e.wait_ge(S["pa%d" % (r * 4 + slat)],
                                  pa_base[r * 4 + slat] + 16 * (tt // 4 + 1))
                    for r in range(2):
                        for jj in range(2):
                            o0 = (jj * 16 + r * 8) * 128
                            ins = e.tensor_add(xa[sl][:, o0:o0 + 1024],
                                               xa[sl][:, o0:o0 + 1024],
                                               at[r * 4 + slat][:, jj * 1024:(jj + 1) * 1024])
                    ins.then_inc(S["dv"], 1)
                C.bump(osem, 16)
                if E == "s":
                    e.wait_ge(S["dv"], tadd)
                    e.dma_start(out=x3[tt * 128:(tt + 1) * 128, :], in_=xa[sl]).then_inc(S[osem], 16)
                C.set("p3_xo_%d" % sl, C.val(osem))
                tdv, tac = ln_stats(E, e, xa[sl], sq_s,
                                    stb["mean"][:, tt:tt + 1], stb["var"][:, tt:tt + 1],
                                    waits_v=[("dv", tadd)], waits_s=[("dv", tadd)])
                C.set("p3_xa_%d" % sl, tdv)
                C.set("p3_xaa_%d" % sl, tac)
            chain_dv = ln_chain(E, e, stb["mean"], stb["var"], stb["rstd"],
                                stb["nmr"], stb["nmean"], C.val("dv"), C.val("ac"))
            t_ro_all = [(sm, C.val(sm)) for sm in ("ro0", "ro1")]
            for tt in range(16):
                sl = tt % 2
                asem = "ra%d" % sl
                osem = "ro%d" % sl
                t_x = C.bump(asem, 16)
                if E == "s":
                    if tt < 2:
                        for sm, vv in t_ro_all:
                            if vv > 0:
                                e.wait_ge(S[sm], vv)
                    e.wait_ge(S["dv"], C.get("p3_xa_%d" % sl))
                    if C.get("p3_lo_%d" % sl) > 0:
                        e.wait_ge(S[osem], C.get("p3_lo_%d" % sl))
                    e.dma_start(out=xa[sl], in_=x3[tt * 128:(tt + 1) * 128, :]).then_inc(S[asem], 16)
                tdv = ln_norm(E, e, xa[sl], stb["rstd"][:, tt:tt + 1], stb["nmr"][:, tt:tt + 1],
                              ln_s[sl], chain_dv, waits_v=[(asem, t_x)],
                              war_v=[(osem, C.get("p3_lo_%d" % sl))])
                C.set("p3_xa_%d" % sl, tdv)
                C.bump(osem, 16)
                if E == "s":
                    e.wait_ge(S["dv"], tdv)
                    e.dma_start(out=ln4_tm[tt * 128:(tt + 1) * 128, :], in_=ln_s[sl]).then_inc(S[osem], 16)
                C.set("p3_lo_%d" % sl, C.val(osem))
            if debug:
                C.bump("dbgs", 16)
                if E == "g":
                    e.wait_ge(S["ro0"], C.val("ro0"))
                    e.wait_ge(S["ro1"], C.val("ro1"))
                    e.dma_start(out=dbg["d_x3"][:, :], in_=x3[:, :]).then_inc(S["dbgs"], 16)

        if emit(6):
            run_phase(p3a, "p3a")

        if emit(7):
            run_phase(lambda E, e: ffn_pass(E, e, ln4_tm), "p3b")

        if emit(8):
            run_phase(lambda E, e: resid_ln(E, e, x3, out_t, None, True, ("ro0", "ro1")), "p3c")

    return nc



def _shard_inputs(inp):
    X = np.asarray(inp["L0_attn_a_hidden_states"], np.float32)
    Wqa = np.asarray(inp["L0_attn_a_Wq"], np.float32).reshape(32, 128, D)
    bqa_full = np.asarray(inp["L0_attn_a_bq"], np.float32).reshape(32, 128)
    M = np.asarray(inp["L0_attn_a_M"], np.float32)
    Z = np.asarray(inp["L0_attn_a_z"], np.float32)
    Wo = np.asarray(inp["L0_attn_a_Wo"], np.float32)
    Bs = np.asarray(inp["L0_ffn_bases"], np.int8)
    Al = np.asarray(inp["L0_ffn_alphas"], np.float32)[:, :, 0]
    Wqb = np.asarray(inp["L0_attn_b_Wq"], np.float32).reshape(32, 128, D)
    bqb_full = np.asarray(inp["L0_attn_b_bq"], np.float32).reshape(32, 128)
    Wk = np.asarray(inp["L0_attn_b_Wk"], np.float32).reshape(16, 128, D)
    bk_full = np.asarray(inp["L0_attn_b_bk"], np.float32).reshape(16, 128)
    Wv = np.asarray(inp["L0_attn_b_Wv"], np.float32).reshape(16, 128, D)
    bv_full = np.asarray(inp["L0_attn_b_bv"], np.float32).reshape(16, 128)

    sel = list(range(0, 32, 4))
    wqa_s = np.ascontiguousarray(Wqa[sel].reshape(HA * 128, D))
    bqa_s = np.ascontiguousarray(bqa_full[sel].T)
    z_s = np.ascontiguousarray(Z.T)

    ident = np.eye(128, dtype=np.float32)
    nsr = np.full((1, 128), -SCALE, np.float32)
    msk = np.zeros((4, 128, 512), np.float32)
    for s in range(4):
        for l in range(128):
            msk[s, l, s * 128 + l:] = 1.0

    in_maps = []
    for c in range(8):
        p, half = c // 2, c % 2
        x_own = np.ascontiguousarray(X[p, half * TOK:(half + 1) * TOK, :])
        kvs = [half * 8 + k for k in range(8)]
        qhs = [j * 16 + half * 8 + k for j in range(2) for k in range(8)]
        ad = np.zeros((4, 4, 128, 128), np.float32)
        for k in range(4):
            for ob in range(4):
                np.fill_diagonal(ad[k, ob],
                                 Al[k, 512 * c + 128 * ob: 512 * c + 128 * (ob + 1)])
        in_maps.append({
            "x0": x_own,
            "wqa": wqa_s,
            "bqa": bqa_s,
            "m_a": M,
            "z_a": z_s,
            "wo": Wo,
            "bases": np.ascontiguousarray(Bs[:, 512 * c:512 * (c + 1), :]),
            "adiag": ad,
            "wqb": np.ascontiguousarray(Wqb[qhs].reshape(16 * 128, D)),
            "bqb_r": np.ascontiguousarray(bqb_full[qhs].reshape(1, 16 * 128)),
            "wk": np.ascontiguousarray(Wk[kvs].reshape(KVH * 128, D)),
            "bk_r": np.ascontiguousarray(bk_full[kvs].reshape(1, KVH * 128)),
            "wv": np.ascontiguousarray(Wv[kvs].reshape(KVH * 128, D)),
            "bvr": np.ascontiguousarray(bv_full[kvs].reshape(1, KVH * 128)),
            "identi": ident,
            "nsrow": nsr,
            "masks": msk,
            "cid": np.array([[half]], np.int32),
        })
    return in_maps


def kernel(**inputs):
    nc = build_nc(debug=False)
    in_maps = _shard_inputs(inputs)
    res = run_bass_kernel_spmd(nc, in_maps, core_ids=list(range(8)))
    X = np.asarray(inputs["L0_attn_a_hidden_states"], np.float32)
    outp = np.empty_like(X)
    for c in range(8):
        p, half = c // 2, c % 2
        outp[p, half * TOK:(half + 1) * TOK, :] = np.asarray(res.results[c]["out"], np.float32)
    return outp
